# version 08
# baseline (speedup 1.0000x reference)
import numpy as np
import ml_dtypes
from contextlib import ExitStack

import concourse.bass as bass
import concourse.mybir as mybir
from concourse.bass_utils import run_bass_kernel_spmd

F32 = mybir.dt.float32
BF16 = mybir.dt.bfloat16
I8 = mybir.dt.int8
I32 = mybir.dt.int32
AF = mybir.ActivationFunctionType
ALU = mybir.AluOpType
AX = mybir.AxisListType

NCORES = 8
B = 4
SEQ = 4096
D = 2048
G = 16
NBLK = 8
OWN = {0: [0, 3, 4, 7], 1: [1, 2, 5, 6]}
CH = 512
EPS_LN = 1e-5
EPS_RMS = 1.1920929e-07
ISQ = 0.08838834764831845
_LASTP = None
NEG = -1e9

BF = ml_dtypes.bfloat16



def _vec_fm(v, g):
    return np.ascontiguousarray(v.reshape(g, 128).T.astype(np.float32))


def _host_prepare(inp):
    W = {}
    W["qd_w"] = np.ascontiguousarray(inp["L0_attn_a_q_down_w"].T)
    W["qu_w"] = np.ascontiguousarray(inp["L0_attn_a_q_up_w"].T)
    W["kvd_w"] = np.ascontiguousarray(inp["L0_attn_a_kv_down_w"].T)
    kvu = inp["L0_attn_a_kv_up_w"]
    kvu3 = kvu.reshape(16, 192, 512)
    W["kwk_w"] = np.ascontiguousarray(
        kvu3[:, 0:64, :].reshape(1024, 512).T)
    W["kwv_w"] = np.ascontiguousarray(
        kvu3[:, 64:192, :].reshape(2048, 512).T)
    W["krope_w"] = np.ascontiguousarray(inp["L0_attn_a_k_rope_w"].T)
    W["ow_w"] = np.ascontiguousarray(inp["L0_attn_a_out_w"].T)
    W["wk_w"] = np.ascontiguousarray(inp["L0_attn_b_Wk"].T)
    W["wv_w"] = np.ascontiguousarray(inp["L0_attn_b_Wv"].T)
    W["wr_w"] = np.ascontiguousarray(inp["L0_attn_b_Wr"].T)
    W["wo_w"] = np.ascontiguousarray(inp["L0_attn_b_Wo"].T)
    W["fg_w"] = np.ascontiguousarray(inp["L0_ffn_w_gate"])
    W["fu_w"] = np.ascontiguousarray(inp["L0_ffn_w_up"])
    W["fd_w"] = np.ascontiguousarray(inp["L0_ffn_w_down"])

    W["qd_b"] = _vec_fm(inp["L0_attn_a_q_down_b"], 8)
    W["qnorm_w"] = _vec_fm(inp["L0_attn_a_q_norm_w"], 8)
    W["qu_b"] = _vec_fm(inp["L0_attn_a_q_up_b"], 16)
    W["kvd_b"] = _vec_fm(inp["L0_attn_a_kv_down_b"], 4)
    W["kvnorm_w"] = _vec_fm(inp["L0_attn_a_kv_norm_w"], 4)
    kvub = inp["L0_attn_a_kv_up_b"].reshape(16, 192)
    W["knope_b"] = _vec_fm(np.ascontiguousarray(kvub[:, 0:64]).reshape(1024), 8)
    W["krope_b"] = _vec_fm(inp["L0_attn_a_k_rope_b"], 8)
    W["vb_row"] = np.ascontiguousarray(
        kvub[:, 64:192].reshape(1, 2048).astype(np.float32))
    W["ow_b"] = _vec_fm(inp["L0_attn_a_out_b"], 16)
    W["lnw"] = _vec_fm(inp["L0_attn_b_ln_w"], 16)
    W["lnb"] = _vec_fm(inp["L0_attn_b_ln_b"], 16)
    W["mixk"] = _vec_fm(inp["L0_attn_b_mix_k"], 16)
    W["mixv"] = _vec_fm(inp["L0_attn_b_mix_v"], 16)
    W["mixr"] = _vec_fm(inp["L0_attn_b_mix_r"], 16)
    W["td"] = _vec_fm(inp["L0_attn_b_time_decay"], 16)
    W["cw"] = _vec_fm(inp["L0_attn_b_current_token_weight"], 16)
    W["sgate"] = _vec_fm(inp["L0_ffn_s_gate"], 64)
    W["sup"] = _vec_fm(inp["L0_ffn_s_up"], 64)
    W["sdown"] = _vec_fm(inp["L0_ffn_s_down"], 16)

    x = inp["L0_attn_a_x"]
    per_core = []
    for c in range(NCORES):
        b, h = c // 2, c % 2
        blocks = OWN[h]
        own_rows = np.concatenate(
            [np.arange(g0 * 512, (g0 + 1) * 512) for g0 in blocks])
        d = dict(W)
        d["x_all"] = np.ascontiguousarray(x[b])
        d["x_own"] = np.ascontiguousarray(x[b][own_rows])

        M = np.zeros((4, 8, 128, 512), np.float32)
        for i, g0 in enumerate(blocks):
            bound = 1024 * (i + 1)
            qglob = g0 * 512 + np.arange(512)[None, :]
            for s in range(8):
                k0 = bound - 1024 + s * 128
                kglob = (k0 + np.arange(128))[:, None]
                M[i, s] = np.where(kglob <= qglob, 0.0, NEG)
        d["mmask"] = M.astype(BF)

        cm = [0.0, 0.0, 1.0, 0.0] if h == 0 else [0.0, 1.0, 0.0, 1.0]
        d["carrymask"] = np.broadcast_to(
            np.array(cm, np.float32), (128, 4)).copy()
        a_own = [0.0, 0.0, 1.0, 0.0] if h == 0 else [0.0, 1.0, 0.0, 1.0]
        d["sxown"] = np.broadcast_to(
            np.array(a_own, np.float32), (128, 4)).copy()
        p = c ^ 1
        msel = np.zeros((4, 128, 8 * 64), np.float32)
        e1need = {0: {1: (p, 1), 3: (p, 3)}, 1: {0: (p, 0), 2: (p, 2)}}[h]
        for cI, (rk, j) in e1need.items():
            msel[cI, :, rk * 64 + j * 16: rk * 64 + (j + 1) * 16] = 1.0
        d["sxsel"] = msel
        mselS = np.zeros((4, 128, 8 * 32), np.float32)
        for cI in range(4):
            rk = 2 * b + (cI & 1)
            mselS[cI, :, rk * 32:(rk + 1) * 32] = 1.0
        d["mselS"] = mselS
        ga = [0.0, 1.0, 0.0, 1.0] if h == 0 else [1.0, 0.0, 1.0, 0.0]
        ge = [0.0, 0.0, 1.0, 0.0] if h == 0 else [0.0, 1.0, 0.0, 1.0]
        d["gmask_a"] = np.broadcast_to(np.array(ga, np.float32), (128, 4)).copy()
        d["gmask_e"] = np.broadcast_to(np.array(ge, np.float32), (128, 4)).copy()
        per_core.append(d)
    return per_core



ENGS = ("sync", "gpsimd", "vector", "scalar", "tensor")


class P:

    ESEM = {"sync": "DS0", "gpsimd": "GE", "vector": "VE", "scalar": "SE",
            "tensor": "TE"}
    NLANES = 4
    DQUEUES = {"sync": "DS", "gpsimd": "DG", "scalar": "DC"}
    DMASEMS = tuple("%s%d" % (q, l) for q in ("DS", "DG", "DC")
                    for l in range(4))

    def __init__(self, nc):
        self.nc = nc
        self.ops = []
        self.sem = {}
        self.cnt = {s: 0 for s in
                    P.DMASEMS + ("TE", "VE", "SE", "GE", "CC")}
        self.dma_rr = {"DS": 0, "DG": 0, "DC": 0}
        self.lastw = {}
        self.writer = {}
        self.readers = {}

    def emit(self, eng, fn):
        self.ops.append((eng, fn))

    def wait(self, eng, sem, val=None):
        v = self.cnt[sem] if val is None else val
        if v <= 0 or self.lastw.get((eng, sem), 0) >= v:
            return
        self.lastw[(eng, sem)] = v
        self.emit(eng, lambda e, s=sem, v=v: e.wait_ge(self.sem[s], v))

    def _dep_wait(self, eng, sem, val):
        self.wait(eng, sem, val)

    def _pre(self, eng, r, w):
        for key in r:
            wr = self.writer.get(key)
            if wr is not None:
                self._dep_wait(eng, wr[0], wr[1])
        for key in w:
            wr = self.writer.get(key)
            if wr is not None:
                self._dep_wait(eng, wr[0], wr[1])
            for sem, val in self.readers.get(key, {}).items():
                self._dep_wait(eng, sem, val)

    def _post(self, sem, r, w):
        v = self.cnt[sem]
        for key in w:
            self.writer[key] = (sem, v)
            self.readers[key] = {}
        for key in r:
            self.readers.setdefault(key, {})[sem] = v

    def do(self, eng, build, inc=None, n=None, r=(), w=()):
        if inc is None:
            inc = P.ESEM[eng]
        if n is None:
            n = 16 if inc.startswith(("DS", "DG", "DC")) else 1
        if isinstance(r, str):
            r = (r,)
        if isinstance(w, str):
            w = (w,)
        self._pre(eng, r, w)
        self.cnt[inc] += n
        def f(e, build=build, inc=inc, n=n):
            ins = build(e)
            ins.then_inc(self.sem[inc], n)
        self.emit(eng, f)
        self._post(inc, r, w)
        return self.cnt[inc]

    def dma(self, eng, build, r=(), w=()):
        q = P.DQUEUES[eng]
        lane = self.dma_rr[q]
        self.dma_rr[q] = (lane + 1) % P.NLANES
        sem = "%s%d" % (q, lane)
        self.wait(eng, sem, self.cnt[sem])
        return self.do(eng, build, inc=sem, n=16, r=r, w=w)

    def group(self, eng, builds, inc=None, n=None, r=(), w=()):
        if inc is None:
            inc = P.ESEM[eng]
        if n is None:
            n = 16 if inc in ("DS", "DG", "DC") else 1
        if isinstance(r, str):
            r = (r,)
        if isinstance(w, str):
            w = (w,)
        self._pre(eng, r, w)
        self.cnt[inc] += n
        def f(e, builds=builds, inc=inc, n=n):
            ins = None
            for b in builds:
                ins = b(e)
            ins.then_inc(self.sem[inc], n)
        self.emit(eng, f)
        self._post(inc, r, w)
        return self.cnt[inc]

    def barrier(self, engines=ENGS, sems=None):
        for eng in engines:
            for s in (sems or self.cnt.keys()):
                self.wait(eng, s)


def build_program(debug=(), kv_chunks=8, q_chunks=4, p1_stages=99, dump_at='after_rope', do_p2=True, do_p3=True, do_p4=True, do_p5=True):
    nc = bass.Bass("TRN2", target_bir_lowering=False)
    p = P(nc)
    global _LASTP
    _LASTP = p

    dram = {}
    def din(name, shape, dt):
        dram[name] = nc.dram_tensor(name, shape, dt, kind="ExternalInput").ap()
    din("x_all", [SEQ, D], F32)
    din("x_own", [D, D], F32)
    for nm, sh in [("qd_w", [2048, 1024]), ("qu_w", [1024, 2048]),
                   ("kvd_w", [2048, 512]), ("kwk_w", [512, 1024]),
                   ("kwv_w", [512, 2048]), ("krope_w", [2048, 1024]),
                   ("ow_w", [2048, 2048]), ("wk_w", [2048, 2048]),
                   ("wv_w", [2048, 2048]), ("wr_w", [2048, 2048]),
                   ("wo_w", [2048, 2048])]:
        din(nm, sh, F32)
    for nm in ("fg_w", "fu_w"):
        din(nm, [2048, 8192], I8)
    din("fd_w", [8192, 2048], I8)
    for nm, g in [("qd_b", 8), ("qnorm_w", 8), ("qu_b", 16), ("kvd_b", 4),
                  ("kvnorm_w", 4), ("knope_b", 8), ("krope_b", 8),
                  ("ow_b", 16), ("lnw", 16), ("lnb", 16), ("mixk", 16),
                  ("mixv", 16), ("mixr", 16), ("td", 16), ("cw", 16),
                  ("sgate", 64), ("sup", 64), ("sdown", 16)]:
        din(nm, [128, g], F32)
    din("vb_row", [1, 2048], F32)
    din("mmask", [4, 8, 128, 512], BF16)
    for nm in ("carrymask", "sxown", "gmask_a", "gmask_e"):
        din(nm, [128, 4], F32)
    din("sxsel", [4, 128, 512], F32)
    din("mselS", [4, 128, 256], F32)

    kdN = nc.dram_tensor("kdN", [1024, SEQ], BF16, kind="Internal").ap()
    kdR = nc.dram_tensor("kdR", [1024, SEQ], BF16, kind="Internal").ap()
    vd = nc.dram_tensor("vd", [SEQ, 2048], BF16, kind="Internal").ap()
    qd = nc.dram_tensor("qd", [2048, 2048], BF16, kind="Internal").ap()
    resd = nc.dram_tensor("resd", [2048, 2048], F32, kind="Internal").ap()
    xind = nc.dram_tensor("xind", [2048, 2048], BF16, kind="Internal").ap()
    e1i = nc.dram_tensor("e1i", [128, 64], F32, kind="Internal")
    e1o = nc.dram_tensor("e1o", [128 * NCORES, 64], F32, kind="Internal")
    e2i = nc.dram_tensor("e2i", [128, 32], F32, kind="Internal")
    e2o = nc.dram_tensor("e2o", [128 * NCORES, 32], F32, kind="Internal")

    outy = nc.dram_tensor("y", [2048, D], F32, kind="ExternalOutput").ap()
    dbg = {}
    for nm in debug:
        shp = {"ln0T": [128, 8192], "kdN": [1024, SEQ], "kdR": [1024, SEQ], "vd": [SEQ, 2048],
               "qd": [2048, 2048], "resid1": [2048, 2048], "resid2": [2048, 2048],
               "resid3": [2048, 2048], "xin": [2048, 2048]}[nm]
        dt = BF16 if nm in ("kdN", "kdR", "vd", "qd", "xin") else F32
        if nm == "ln0T":
            shp, dt = [128, 8192], BF16
        dbg[nm] = nc.dram_tensor("dbg_" + nm, shp, dt, kind="ExternalOutput").ap()

    st = ExitStack()
    SB = lambda name, shape, dt: st.enter_context(nc.sbuf_tensor(name, shape, dt))
    PS = lambda name, shape, dt=F32: st.enter_context(nc.psum_tensor(name, shape, dt))

    identb = SB("identb", [128, 128], BF16)
    identf = SB("identf", [128, 128], F32)
    iot = SB("iot", [128, 128], I32)
    ones_row = SB("ones_row", [1, 128], F32)
    ones_col = SB("ones_col", [128, 1], F32)
    ones_colb = SB("ones_colb", [128, 1], BF16)
    vecs = {}
    for nm, g in [("qd_b", 8), ("qnorm_w", 8), ("qu_b", 16), ("kvd_b", 4),
                  ("kvnorm_w", 4), ("knope_b", 8), ("krope_b", 8),
                  ("ow_b", 16), ("lnw", 16), ("lnb", 16), ("mixk", 16),
                  ("mixv", 16), ("mixr", 16), ("td", 16), ("cw", 16),
                  ("sgate", 64), ("sup", 64), ("sdown", 16),
                  ("carrymask", 4), ("sxown", 4), ("gmask_a", 4), ("gmask_e", 4)]:
        vecs[nm] = SB("v_" + nm, [128, g], F32)

    row_a = SB("row_a", [1, 512], F32)
    row_b = SB("row_b", [1, 512], F32)
    eps_ln = SB("eps_ln", [128, 1], F32)
    eps_rms = SB("eps_rms", [128, 1], F32)
    dumV = SB("dumV", [1, 512], F32)
    dumS = SB("dumS", [1, 512], F32)
    dumW = SB("dumW", [128, 512], F32)
    dumWb = SB("dumWb", [128, 512], BF16)

    pwu = ExitStack()
    psW = pwu.enter_context(nc.psum_tensor("psW", [128, 512], F32))
    p.do("vector", lambda e: e.memset(dumW[:], 1.25), w="dumW")
    p.do("vector", lambda e: e.memset(dumWb[:], 1.0), w="dumWb")
    p.do("vector", lambda e: e.memset(dumV[:], 0.0), w="dumV")
    p.do("gpsimd", lambda e: e.memset(dumS[:], 1.0), w="dumS")
    for i in range(24):
        p.do("vector", lambda e: e.tensor_tensor(dumW[:], dumW[:], dumW[:], ALU.max),
             r="dumW", w="dumW")
    for fn in (AF.Exp, AF.Ln, AF.Sigmoid, AF.Silu):
        p.do("scalar", lambda e, fn=fn: e.activation(dumS[:], dumS[:], fn),
             r="dumS", w="dumS")
    for i in range(16):
        p.do("scalar", lambda e: e.activation(dumS[:], dumS[:], AF.Square),
             r="dumS", w="dumS")
    for i in range(24):
        p.group("tensor", [lambda e, i=i: e.matmul(
            psW[:], dumWb[:, 0:128], dumWb[:], start=(i == 0), stop=(i == 23))],
            r="dumWb", w="psW")
    p.do("vector", lambda e: e.tensor_copy(dumW[:], psW[:]), r="psW", w="dumW")
    p.barrier()
    pwu.close()

    for nm in vecs:
        p.dma("sync", lambda e, nm=nm: e.dma_start(out=vecs[nm][:], in_=dram[nm]),
              w="v_" + nm)
    p.do("gpsimd", lambda e: e.iota(iot[:], [[1, 128]], base=0, channel_multiplier=-1),
         w="iot")
    p.do("vector", lambda e: e.tensor_single_scalar(identb[:], iot[:], 0, ALU.is_equal),
         r="iot", w="identb")
    p.do("vector", lambda e: e.tensor_single_scalar(identf[:], iot[:], 0, ALU.is_equal),
         r="iot", w="identf")
    p.do("vector", lambda e: e.memset(ones_row[:], 1.0), w="ones_row")
    p.do("vector", lambda e: e.memset(ones_col[:], 1.0), w="ones_col")
    p.do("vector", lambda e: e.memset(ones_colb[:], 1.0), w="ones_colb")
    p.do("vector", lambda e: e.memset(eps_ln[:], float(EPS_LN)), w="eps_ln")
    p.do("vector", lambda e: e.memset(eps_rms[:], float(EPS_RMS)), w="eps_rms")
    p.barrier()

    def margin(eng):
        if eng == "vector":
            p.do("vector", lambda e: e.tensor_copy(dumV[:], dumV[:]), r="dumV", w="dumV")
        elif eng == "scalar":
            p.do("scalar", lambda e: e.activation(dumS[:], dumS[:], AF.Copy),
                 r="dumS", w="dumS")

    def tokmajor_ln(xsb, ntile, lnout, scr, xkey, lnkey):
        col = lambda nm, tt: scr[nm][:, tt:tt + 1]
        K_ = lambda nm: "ln." + nm
        for tt in range(ntile):
            xs = xsb[:, tt * 2048:(tt + 1) * 2048]
            p.do("vector", lambda e, xs=xs, o=col("mu", tt): e.reduce_sum(o, xs, axis=AX.X),
                 r=xkey, w=K_("mu"))
            p.do("vector", lambda e, xs=xs: e.tensor_tensor(
                scr["junk"][:], xs, xs, ALU.mult), r=xkey, w="ln.junk")
            p.do("vector", lambda e, o=col("ss", tt): e.reduce_sum(
                o, scr["junk"][:], axis=AX.X), r="ln.junk", w=K_("ss"))
            p.do("vector", lambda e, o=col("mu", tt): e.tensor_scalar(
                o, o, 1.0 / D, None, ALU.mult), r=K_("mu"), w=K_("mu"))
            p.do("vector", lambda e, o=col("var", tt), i=col("ss", tt): e.tensor_scalar(
                o, i, 1.0 / D, None, ALU.mult), r=K_("ss"), w=K_("var"))
            p.do("vector", lambda e, o=col("m2", tt), i=col("mu", tt): e.tensor_tensor(
                o, i, i, ALU.mult), r=K_("mu"), w=K_("m2"))
            p.do("vector", lambda e, o=col("var", tt), i=col("m2", tt): e.tensor_tensor(
                o, o, i, ALU.subtract), r=(K_("var"), K_("m2")), w=K_("var"))
        margin("vector")
        for tt in range(ntile):
            p.do("scalar", lambda e, o=col("lnv", tt), i=col("var", tt): e.activation(
                o, i, AF.Ln, bias=eps_ln[:]), r=K_("var"), w=K_("lnv"))
        for tt in range(ntile):
            p.do("scalar", lambda e, o=col("rstd", tt), i=col("lnv", tt): e.activation(
                o, i, AF.Exp, scale=-0.5), r=K_("lnv"), w=K_("rstd"))
        margin("scalar")
        for tt in range(ntile):
            p.do("vector", lambda e, o=col("nb", tt), i=col("mu", tt), rr=col("rstd", tt):
                 e.tensor_scalar(o, i, rr, -1.0, ALU.mult, ALU.mult),
                 r=(K_("mu"), K_("rstd")), w=K_("nb"))
        margin("vector")
        for tt in range(ntile):
            p.do("scalar", lambda e, tt=tt, bb=col("nb", tt), rr=col("rstd", tt):
                 e.activation(
                     lnout[:, tt * 2048:(tt + 1) * 2048],
                     xsb[:, tt * 2048:(tt + 1) * 2048],
                     AF.Identity, bias=bb, scale=rr),
                 r=(xkey, K_("nb"), K_("rstd")), w=lnkey)

    def transpose_to_fm(lnout, ntile, dstT, ptr, Tcols, lnkey, dstkey):
        n = ntile * G
        se_after = {}
        def tr(k):
            tt, fg, slot = k // G, k % G, k % 4
            p.do("tensor", lambda e, tt=tt, fg=fg, slot=slot: e.transpose(
                ptr[slot][:],
                lnout[:, tt * 2048 + fg * 128: tt * 2048 + (fg + 1) * 128],
                identb[:]), r=lnkey, w="ptr%d" % slot)
            se_after[k] = p.cnt["TE"]
        def cp(k):
            tt, fg, slot = k // G, k % G, k % 4
            p.wait("scalar", "TE", se_after[min(k + 2, n + 1)])
            p.do("scalar", lambda e, tt=tt, fg=fg, slot=slot: e.activation(
                dstT[:, fg * Tcols + tt * 128: fg * Tcols + (tt + 1) * 128],
                ptr[slot][:], AF.Copy), r="ptr%d" % slot, w=dstkey)
        for k in range(n):
            tr(k)
            if k >= 2:
                cp(k - 2)
        for d in range(2):
            kk = n + d
            tt, fg, slot = 0, 0, kk % 4
            p.do("tensor", lambda e, slot=slot: e.transpose(
                ptr[slot][:], lnout[:, 0:128], identb[:]),
                r=lnkey, w="ptr%d" % slot)
            se_after[kk] = p.cnt["TE"]
            cp(n - 2 + d)

    def mm_rot(ngroups, emit_mms, emit_evac):
        for k in range(ngroups):
            ps, pskey = (psA, "psA") if k % 2 == 0 else (psB, "psB")
            emit_mms(k, ps, pskey)
            emit_evac(k, ps, pskey)

    p1 = ExitStack()
    SB1 = lambda name, shape, dt: p1.enter_context(nc.sbuf_tensor(name, shape, dt))
    PS1 = lambda name, shape, dt=F32: p1.enter_context(nc.psum_tensor(name, shape, dt))

    xsb = SB1("xsb", [128, 4 * 2048], F32)
    lnsb = SB1("lnsb", [128, 4 * 2048], BF16)
    ln0T2_ = [SB1("ln0T%d" % j, [128, G * 512], BF16) for j in range(2)]
    junk = SB1("junk", [128, 2048], F32)
    scr = {"junk": junk}
    for nm in ("mu", "ss", "var", "m2", "lnv", "rstd", "nb"):
        scr[nm] = SB1("scr_" + nm, [128, 4], F32)
    ckv = SB1("ckv", [128, 4 * 512], F32)
    ckvn = SB1("ckvn", [128, 4 * 512], BF16)
    ksb = SB1("ksb", [128, 2 * 512], BF16)
    vsb = SB1("vsb", [128, 2 * 512], BF16)
    p1a = ExitStack()
    SB1a = lambda name, shape, dt: p1a.enter_context(nc.sbuf_tensor(name, shape, dt))
    kvdw = SB1a("kvdw", [128, 16 * 512], BF16)
    vbrow = SB1a("vbrow", [1, 2048], F32)
    kwkw = SB1a("kwkw", [128, 4 * 1024], BF16)
    kwvw = SB1a("kwvw", [128, 4 * 2048], BF16)
    kropew = SB1a("kropew", [128, 16 * 1024], BF16)
    ptr = [PS1("ptr%d" % i, [128, 128], BF16) for i in range(4)]
    psA = PS1("psA", [128, 512])
    psB = PS1("psB", [128, 512])
    psS = PS1("psS", [1, 512])
    psBC = PS1("psBC", [128, 512])

    for nm, dst, na in [("kvd_w", kvdw, 16), ("kwk_w", kwkw, 4),
                        ("kwv_w", kwvw, 4), ("krope_w", kropew, 16)]:
        p.dma("gpsimd", lambda e, nm=nm, dst=dst, na=na: e.dma_start(
            out=dst[:].rearrange("p (a o) -> p a o", a=na),
            in_=dram[nm].rearrange("(a p) o -> p a o", p=128)), w=nm)
    p.dma("sync", lambda e: e.dma_start(out=vbrow[:], in_=dram["vb_row"]), w="vbrow")
    p.barrier()

    def rms_apply(nsub, src, dst, wvec, eps, srckey, dstkey):
        for sg in range(nsub):
            p.do("scalar", lambda e, sg=sg: e.activation(
                junk[:, 0:512], src[:, sg * 512:(sg + 1) * 512], AF.Square),
                r=srckey, w="rms.junk")
            p.group("tensor", [lambda e, sg=sg: e.matmul(
                psS[:], ones_col[:], junk[:, 0:512],
                start=(sg == 0), stop=(sg == nsub - 1))],
                r="rms.junk", w="psS" if sg == nsub - 1 else "psS.partial")
        p.do("vector", lambda e: e.tensor_scalar(
            row_a[:], psS[:], 1.0 / (nsub * 128), None, ALU.mult),
            r=("psS", "psS.partial"), w="row_a")
        margin("vector")
        p.do("scalar", lambda e, eps=eps: e.activation(
            row_b[:], row_a[:], AF.Ln,
            bias=(eps_ln if eps == EPS_LN else eps_rms)[0:1, :]),
            r="row_a", w="row_b")
        p.do("scalar", lambda e: e.activation(row_a[:], row_b[:], AF.Exp, scale=-0.5),
             r="row_b", w="row_a")
        margin("scalar")
        p.group("tensor", [lambda e: e.matmul(
            psBC[:], ones_row[:], row_a[:], start=True, stop=True)],
            r="row_a", w="psBC")
        for sg in range(nsub):
            p.do("vector", lambda e, sg=sg: e.scalar_tensor_tensor(
                dst[:, sg * 512:(sg + 1) * 512], src[:, sg * 512:(sg + 1) * 512],
                wvec[:, sg:sg + 1], psBC[:], ALU.mult, ALU.mult),
                r=(srckey, "psBC"), w=dstkey)

    def dump_ln0T(tag, ch):
        pass

    def p1_kv_chunk(ch):
        ln0T = ln0T2_[ch % 2]
        lnk = "ln0T%d" % (ch % 2)
        p.dma("sync", lambda e, ch=ch: e.dma_start(
            out=xsb[:].rearrange("p (a f) -> p a f", a=4),
            in_=dram["x_all"][ch * 512:(ch + 1) * 512, :]
            .rearrange("(a p) f -> p a f", p=128)), w="xsb")
        if p1_stages < 2: return
        tokmajor_ln(xsb, 4, lnsb, scr, "xsb", "lnsb")
        if p1_stages < 3: return
        transpose_to_fm(lnsb, 4, ln0T, ptr, 512, "lnsb", lnk)
        dump_ln0T("after_tr", ch)
        if p1_stages < 4: return
        def mm_ckv(dg, ps, pskey):
            p.group("tensor", [lambda e, ps=ps, dg=dg, fg=fg: e.matmul(
                ps[:], kvdw[:, fg * 512 + dg * 128: fg * 512 + (dg + 1) * 128],
                ln0T[:, fg * 512:(fg + 1) * 512],
                start=(fg == 0), stop=(fg == G - 1)) for fg in range(G)],
                r=(lnk, "kvd_w"), w=pskey)
        def ev_ckv(dg, ps, pskey):
            p.do("scalar", lambda e, ps=ps, dg=dg: e.activation(
                ckv[:, dg * 512:(dg + 1) * 512], ps[:], AF.Identity,
                bias=vecs["kvd_b"][:, dg:dg + 1]), r=pskey, w="ckv")
        mm_rot(4, mm_ckv, ev_ckv)
        dump_ln0T("after_ckv", ch)
        if p1_stages < 5: return
        rms_apply(4, ckv, ckvn, vecs["kvnorm_w"], EPS_RMS, "ckv", "ckvn")
        dump_ln0T("after_rms", ch)
        if p1_stages < 6: return
        def mm_kn(ng, ps, pskey):
            p.group("tensor", [lambda e, ps=ps, ng=ng, dg=dg: e.matmul(
                ps[:], kwkw[:, dg * 1024 + ng * 128: dg * 1024 + (ng + 1) * 128],
                ckvn[:, dg * 512:(dg + 1) * 512],
                start=(dg == 0), stop=(dg == 3)) for dg in range(4)],
                r=("ckvn", "kwk_w"), w=pskey)
        def ev_kn(ng, ps, pskey):
            slot = "ksb%d" % (ng % 2)
            buf = ksb[:, (ng % 2) * 512:(ng % 2 + 1) * 512]
            p.do("scalar", lambda e, ps=ps, ng=ng, buf=buf: e.activation(
                buf, ps[:], AF.Identity, bias=vecs["knope_b"][:, ng:ng + 1]),
                r=pskey, w=slot)
            p.dma("scalar", lambda e, ng=ng, buf=buf, ch=ch: e.dma_start(
                out=kdN[ng * 128:(ng + 1) * 128, ch * 512:(ch + 1) * 512], in_=buf),
                r=slot, w="kdN")
        mm_rot(8, mm_kn, ev_kn)
        dump_ln0T("after_nope", ch)
        if p1_stages < 7: return
        def mm_kr(rg, ps, pskey):
            p.group("tensor", [lambda e, ps=ps, rg=rg, fg=fg: e.matmul(
                ps[:], kropew[:, fg * 1024 + rg * 128: fg * 1024 + (rg + 1) * 128],
                ln0T[:, fg * 512:(fg + 1) * 512],
                start=(fg == 0), stop=(fg == G - 1)) for fg in range(G)],
                r=(lnk, "krope_w"), w=pskey)
        def ev_kr(rg, ps, pskey):
            slot = "ksb%d" % (rg % 2)
            buf = ksb[:, (rg % 2) * 512:(rg % 2 + 1) * 512]
            p.do("scalar", lambda e, ps=ps, rg=rg, buf=buf: e.activation(
                buf, ps[:], AF.Identity, bias=vecs["krope_b"][:, rg:rg + 1]),
                r=pskey, w=slot)
            p.dma("scalar", lambda e, rg=rg, buf=buf, ch=ch: e.dma_start(
                out=kdR[rg * 128:(rg + 1) * 128, ch * 512:(ch + 1) * 512], in_=buf),
                r=slot, w="kdR")
        mm_rot(8, mm_kr, ev_kr)
        dump_ln0T("after_rope", ch)
        if p1_stages < 8: return
        def mm_v(k, ps, pskey):
            tt, vc = k // 4, k % 4
            blds = [lambda e, ps=ps, tt=tt, vc=vc, dg=dg: e.matmul(
                ps[:], ckvn[:, dg * 512 + tt * 128: dg * 512 + (tt + 1) * 128],
                kwvw[:, dg * 2048 + vc * 512: dg * 2048 + (vc + 1) * 512],
                start=(dg == 0), stop=(dg == 3)) for dg in range(4)]
            blds.append(lambda e, ps=ps, vc=vc: e.matmul(
                ps[:], ones_row[:], vbrow[:, vc * 512:(vc + 1) * 512],
                start=False, stop=True, skip_group_check=True))
            p.group("tensor", blds, r=("ckvn", "kwv_w", "vbrow", "ones_row"), w=pskey)
        def ev_v(k, ps, pskey):
            tt, vc = k // 4, k % 4
            slot = "vsb%d" % (k % 2)
            buf = vsb[:, (k % 2) * 512:(k % 2 + 1) * 512]
            p.do("scalar", lambda e, ps=ps, buf=buf: e.activation(
                buf, ps[:], AF.Copy), r=pskey, w=slot)
            p.dma("scalar", lambda e, buf=buf, ch=ch, tt=tt, vc=vc: e.dma_start(
                out=vd[ch * 512 + tt * 128: ch * 512 + (tt + 1) * 128,
                       vc * 512:(vc + 1) * 512], in_=buf), r=slot, w="vd")
        mm_rot(16, mm_v, ev_v)

    for ch in range(kv_chunks):
        p1_kv_chunk(ch)
    p.barrier()

    p1a.close()
    qdw = SB1("qdw", [128, 16 * 1024], BF16)
    quw = SB1("quw", [128, 8 * 2048], BF16)
    cq = SB1("cq", [128, 8 * 512], F32)
    cqn = SB1("cqn", [128, 8 * 512], BF16)
    for eng in ("gpsimd",):
        for sm in ("TE", "SE", "VE"):
            p.wait(eng, sm)
    for nm, dst, na in [("qd_w", qdw, 16), ("qu_w", quw, 8)]:
        p.dma("gpsimd", lambda e, nm=nm, dst=dst, na=na: e.dma_start(
            out=dst[:].rearrange("p (a o) -> p a o", a=na),
            in_=dram[nm].rearrange("(a p) o -> p a o", p=128)), w=nm)

    def p1_q_chunk(qc):
        ln0T = ln0T2_[qc % 2]
        lnk = "ln0T%d" % (qc % 2)
        p.dma("sync", lambda e, qc=qc: e.dma_start(
            out=xsb[:].rearrange("p (a f) -> p a f", a=4),
            in_=dram["x_own"][qc * 512:(qc + 1) * 512, :]
            .rearrange("(a p) f -> p a f", p=128)), w="xsb")
        tokmajor_ln(xsb, 4, lnsb, scr, "xsb", "lnsb")
        transpose_to_fm(lnsb, 4, ln0T, ptr, 512, "lnsb", lnk)
        def mm_cq(qg, ps, pskey):
            p.group("tensor", [lambda e, ps=ps, qg=qg, fg=fg: e.matmul(
                ps[:], qdw[:, fg * 1024 + qg * 128: fg * 1024 + (qg + 1) * 128],
                ln0T[:, fg * 512:(fg + 1) * 512],
                start=(fg == 0), stop=(fg == G - 1)) for fg in range(G)],
                r=(lnk, "qd_w"), w=pskey)
        def ev_cq(qg, ps, pskey):
            p.do("scalar", lambda e, ps=ps, qg=qg: e.activation(
                cq[:, qg * 512:(qg + 1) * 512], ps[:], AF.Identity,
                bias=vecs["qd_b"][:, qg:qg + 1]), r=pskey, w="cq")
        mm_rot(8, mm_cq, ev_cq)
        rms_apply(8, cq, cqn, vecs["qnorm_w"], EPS_RMS, "cq", "cqn")
        def mm_q(og, ps, pskey):
            p.group("tensor", [lambda e, ps=ps, og=og, qg=qg: e.matmul(
                ps[:], quw[:, qg * 2048 + og * 128: qg * 2048 + (og + 1) * 128],
                cqn[:, qg * 512:(qg + 1) * 512],
                start=(qg == 0), stop=(qg == 7)) for qg in range(8)],
                r=("cqn", "qu_w"), w=pskey)
        def ev_q(og, ps, pskey):
            slot = "ksb%d" % (og % 2)
            buf = ksb[:, (og % 2) * 512:(og % 2 + 1) * 512]
            p.do("scalar", lambda e, ps=ps, og=og, buf=buf: e.activation(
                buf, ps[:], AF.Identity, bias=vecs["qu_b"][:, og:og + 1]),
                r=pskey, w=slot)
            p.dma("scalar", lambda e, og=og, buf=buf, qc=qc: e.dma_start(
                out=qd[og * 128:(og + 1) * 128, qc * 512:(qc + 1) * 512], in_=buf),
                r=slot, w="qd")
        mm_rot(16, mm_q, ev_q)

    for qc in range(q_chunks):
        p1_q_chunk(qc)

    p.barrier()
    p1.close()

    if do_p2:
        p2 = ExitStack()
        SB2 = lambda name, shape, dt: p2.enter_context(nc.sbuf_tensor(name, shape, dt))
        PS2 = lambda name, shape, dt=F32: p2.enter_context(nc.psum_tensor(name, shape, dt))
        qsb = SB2("qsb", [128, G * 512], BF16)
        kb = [SB2("kb%d" % j, [128, 4 * 512], BF16) for j in range(2)]
        vb = [SB2("vb%d" % j, [128, 4 * 512], BF16) for j in range(2)]
        pb = [SB2("pb%d" % j, [128, 512], BF16) for j in range(2)]
        attnbf = SB2("attnbf", [128, G * 512], BF16)
        msb = SB2("msb", [128, 8 * 512], BF16)
        xosb = SB2("xosb", [128, 4 * 2048], F32)
        owsb = SB2("owsb", [128, G * 2048], BF16)
        otmp = SB2("otmp", [128, 512], F32)
        drow = SB2("drow", [1, 512], F32)
        r1sb = SB2("r1sb", [128, 2 * 512], F32)
        psS2 = [PS2("p2S%d" % j, [128, 512]) for j in range(2)]
        psO = [PS2("p2O%d" % j, [128, 512]) for j in range(4)]
        psD = [PS2("p2D%d" % j, [33, 512]) for j in range(2)]

        p.dma("gpsimd", lambda e: e.dma_start(
            out=owsb[:].rearrange("p (a o) -> p a o", a=G),
            in_=dram["ow_w"].rearrange("(a p) o -> p a o", p=128)), w="owsb")

        def p2_block(i):
            nkc = 2 * (i + 1)
            p.dma("sync", lambda e, i=i: e.dma_start(
                out=qsb[:].rearrange("p (g t) -> p g t", g=G),
                in_=qd[:, i * 512:(i + 1) * 512].rearrange("(g p) t -> p g t", p=128)),
                r="qd", w="qsb")
            p.dma("sync", lambda e, i=i: e.dma_start(
                out=msb[:].rearrange("p (s t) -> p s t", s=8),
                in_=dram["mmask"][i].rearrange("s p t -> p s t")), w="msb")
            p.dma("sync", lambda e, i=i: e.dma_start(
                out=xosb[:].rearrange("p (a f) -> p a f", a=4),
                in_=dram["x_own"][i * 512:(i + 1) * 512, :]
                .rearrange("(a p) f -> p a f", p=128)), w="xosb")
            for hp in range(4):
                for kc in range(nkc):
                    j = kc % 2
                    p.dma("sync", lambda e, hp=hp, kc=kc, j=j: e.dma_start(
                        out=kb[j][0:64, :].rearrange("p (h t) -> p h t", h=4),
                        in_=kdN[hp * 256:(hp + 1) * 256, kc * 512:(kc + 1) * 512]
                        .rearrange("(h p) t -> p h t", p=64)),
                        r="kdN", w="kb%d" % j)
                    p.dma("sync", lambda e, hp=hp, kc=kc, j=j: e.dma_start(
                        out=kb[j][64:128, :].rearrange("p (h t) -> p h t", h=4),
                        in_=kdR[hp * 256:(hp + 1) * 256, kc * 512:(kc + 1) * 512]
                        .rearrange("(h p) t -> p h t", p=64)),
                        r="kdR", w="kb%d" % j)
                    p.dma("sync", lambda e, i=i, hp=hp, kc=kc, j=j: e.dma_start(
                        out=vb[j][:].rearrange("p (a v) -> p a v", a=4),
                        in_=vd[kc * 512:(kc + 1) * 512, hp * 512:(hp + 1) * 512]
                        .rearrange("(a p) v -> p a v", p=128)),
                        r="vd", w="vb%d" % j)
                    masked = kc >= nkc - 2

                    def emit_S(hi, ks):
                        h = hp * 4 + hi
                        sj = (hi * 4 + ks) % 2
                        pj = sj
                        skey, pkey = "p2S%d" % sj, "pb%d" % pj
                        blds = [lambda e, j=j, hi=hi, ks=ks, h=h, sj=sj: e.matmul(
                            psS2[sj][:],
                            kb[j][:, hi * 512 + ks * 128: hi * 512 + (ks + 1) * 128],
                            qsb[:, h * 512:(h + 1) * 512],
                            start=True, stop=True)]
                        if masked:
                            sidx = (kc - (nkc - 2)) * 4 + ks
                            blds.append(lambda e, sj=sj, sidx=sidx: e.matmul(
                                psS2[sj][:], identb[:],
                                msb[:, sidx * 512:(sidx + 1) * 512],
                                start=False, stop=True, skip_group_check=True))
                        p.group("tensor", blds,
                                r=("kb%d" % j, "qsb", "msb", "identb"), w=skey)
                        p.do("scalar", lambda e, sj=sj, pj=pj: e.activation(
                            pb[pj][:], psS2[sj][:], AF.Exp, scale=ISQ),
                            r=skey, w=pkey)

                    def emit_AVD(hi, ks):
                        pj = (hi * 4 + ks) % 2
                        pkey = "pb%d" % pj
                        first = (kc == 0 and ks == 0)
                        last = (kc == nkc - 1 and ks == 3)
                        p.group("tensor", [lambda e, j=j, hi=hi, ks=ks, pj=pj,
                                           first=first, last=last: e.matmul(
                            psO[hi][:],
                            vb[j][:, ks * 512 + hi * 128: ks * 512 + (hi + 1) * 128],
                            pb[pj][:], start=first, stop=last)],
                            r=("vb%d" % j, pkey), w="p2O%d" % hi)
                        p.group("tensor", [lambda e, hi=hi, pj=pj,
                                           first=first, last=last: e.matmul(
                            psD[hi // 2][(hi % 2) * 32:(hi % 2) * 32 + 1, :],
                            ones_colb[:], pb[pj][:], start=first, stop=last)],
                            r=("ones_colb", pkey), w="p2D%d" % hi)

                    prev = None
                    for hi in range(4):
                        for ks in range(4):
                            emit_S(hi, ks)
                            if prev is not None:
                                emit_AVD(*prev)
                            prev = (hi, ks)
                    emit_AVD(*prev)
                for hi in range(4):
                    h = hp * 4 + hi
                    p.do("scalar", lambda e, hi=hi: e.activation(
                        drow[:], psD[hi // 2][(hi % 2) * 32:(hi % 2) * 32 + 1, :],
                        AF.Ln), r="p2D%d" % hi, w="drow")
                    p.do("scalar", lambda e: e.activation(
                        drow[:], drow[:], AF.Exp, scale=-1.0), r="drow", w="drow")
                    margin("scalar")
                    p.group("tensor", [lambda e: e.matmul(
                        psS2[0][:], ones_row[:], drow[:], start=True, stop=True)],
                        r=("ones_row", "drow"), w="p2S0")
                    p.do("scalar", lambda e, hi=hi: e.activation(
                        otmp[:], psO[hi][:], AF.Copy), r="p2O%d" % hi, w="otmp")
                    p.do("vector", lambda e, h=h: e.tensor_tensor(
                        attnbf[:, h * 512:(h + 1) * 512], otmp[:], psS2[0][:],
                        ALU.mult), r=("otmp", "p2S0"), w="attnbf")
            for og in range(G):
                blds = [lambda e, og=og, fg=fg: e.matmul(
                    psO[og % 2][:],
                    owsb[:, fg * 2048 + og * 128: fg * 2048 + (og + 1) * 128],
                    attnbf[:, fg * 512:(fg + 1) * 512],
                    start=(fg == 0), stop=(fg == G - 1)) for fg in range(G)]
                for tsub in range(4):
                    blds.append(lambda e, og=og, tsub=tsub: e.matmul(
                        psO[og % 2][:, tsub * 128:(tsub + 1) * 128],
                        xosb[:, tsub * 2048 + og * 128: tsub * 2048 + (og + 1) * 128],
                        identf[:], start=False, stop=True,
                        skip_group_check=True))
                p.group("tensor", blds, r=("owsb", "attnbf", "xosb", "identf"),
                        w="p2O%d" % (og % 2))
                slot = "r1sb%d" % (og % 2)
                buf = r1sb[:, (og % 2) * 512:(og % 2 + 1) * 512]
                p.do("scalar", lambda e, og=og, buf=buf: e.activation(
                    buf, psO[og % 2][:], AF.Identity, bias=vecs["ow_b"][:, og:og + 1]),
                    r="p2O%d" % (og % 2), w=slot)
                p.dma("scalar", lambda e, og=og, buf=buf, i=i: e.dma_start(
                    out=resd[og * 128:(og + 1) * 128, i * 512:(i + 1) * 512], in_=buf),
                    r=slot, w="resd%d.%d" % (og, i))

        for i in range(4):
            p2_block(i)
        p.barrier()
        if "resid1" in dbg:
            p.dma("sync", lambda e: e.dma_start(out=dbg["resid1"], in_=resd),
                  r="resd", w="dbg_resid1")
            p.barrier()
        p2.close()

    p.barrier()

    dhalf = nc.dram_tensor("dhalf", [2048, 1024], F32, kind="Internal").ap()

    def emit_ffn_phase(final):
        pf = ExitStack()
        sfx = "_f1" if final else "_f0"
        SBF = lambda name, shape, dt: pf.enter_context(
            nc.sbuf_tensor(name + sfx, shape, dt))
        PSF = lambda name, shape, dt=F32: pf.enter_context(
            nc.psum_tensor(name + sfx, shape, dt))
        xlnf = SBF("xlnf", [128, G * 1024], BF16)
        fusef = SBF("fusef", [128, 32 * 1024], BF16)
        wsec = SBF("wsec", [128, G * 256], BF16)
        wsec2 = SBF("wsec2", [128, G * 256], BF16)
        wsec3 = SBF("wsec3", [128, G * 256], BF16)
        wsec4 = SBF("wsec4", [128, G * 256], BF16)
        wdn = SBF("wdn", [128, 8192], BF16)
        wdn2 = SBF("wdn2", [128, 8192], BF16)
        rx = [SBF("rx%d" % j, [128, 1024], F32) for j in range(2)]
        junk2 = SBF("junk2", [128, 1024], F32)
        xtmp = SBF("xtmp", [128, 512], F32)
        gbf = SBF("gbf", [128, 512], BF16)
        ubf = SBF("ubf", [128, 512], BF16)
        r2s = SBF("r2s", [128, 512], F32)
        prowt = SBF("prowt", [1, 1536], F32)
        prow = {"mu": prowt[0:1, 0:512], "rs": prowt[0:1, 512:1024],
                "t": prowt[0:1, 1024:1536]}
        psF = [PSF("pf%d" % j, [128, 512]) for j in range(8)]

        def ffn_group(tg):
            t0c = tg * 1024
            for g in range(G):
                j = g % 2
                p.dma("sync", lambda e, g=g, t0c=t0c, j=j: e.dma_start(
                    out=rx[j][:], in_=resd[g * 128:(g + 1) * 128, t0c:t0c + 1024]),
                    r=("resd%d.%d" % (g, t0c // 512), "resd%d.%d" % (g, t0c // 512 + 1)),
                    w="rx%d" % j)
                for sub in range(2):
                    p.group("tensor", [lambda e, j=j, sub=sub, g=g: e.matmul(
                        psF[sub][0:1, :], ones_col[:],
                        rx[j][:, sub * 512:(sub + 1) * 512],
                        start=(g == 0), stop=(g == G - 1))],
                        r=("rx%d" % j, "ones_col"), w="pf%d" % sub)
                p.do("scalar", lambda e, j=j: e.activation(
                    junk2[:], rx[j][:], AF.Square), r="rx%d" % j, w="junk2")
                for sub in range(2):
                    p.group("tensor", [lambda e, sub=sub, g=g: e.matmul(
                        psF[2 + sub][0:1, :], ones_col[:],
                        junk2[:, sub * 512:(sub + 1) * 512],
                        start=(g == 0), stop=(g == G - 1))],
                        r=("junk2", "ones_col"), w="pf%d" % (2 + sub))
            for sub in range(2):
                mu, rs, t_ = prow["mu"], prow["rs"], prow["t"]
                p.do("vector", lambda e, mu=mu, sub=sub: e.tensor_scalar(
                    mu, psF[sub][0:1, :], 1.0 / D, None, ALU.mult),
                    r="pf%d" % sub, w="f.mu")
                p.do("vector", lambda e, t_=t_, sub=sub: e.tensor_scalar(
                    t_, psF[2 + sub][0:1, :], 1.0 / D, None, ALU.mult),
                    r="pf%d" % (2 + sub), w="f.t")
                p.do("vector", lambda e, mu=mu: e.tensor_tensor(
                    junk2[0:1, 0:512], mu, mu, ALU.mult),
                    r="f.mu", w="junk2")
                p.do("vector", lambda e, t_=t_: e.tensor_tensor(
                    t_, t_, junk2[0:1, 0:512], ALU.subtract),
                    r=("f.t", "junk2"), w="f.t")
                margin("vector")
                p.do("scalar", lambda e, t_=t_: e.activation(
                    t_, t_, AF.Ln, bias=eps_ln[0:1, :]),
                    r="f.t", w="f.t")
                p.do("scalar", lambda e, rs=rs, t_=t_: e.activation(
                    rs, t_, AF.Exp, scale=-0.5), r="f.t", w="f.rs")
                margin("scalar")
                p.group("tensor", [lambda e, mu=mu: e.matmul(
                    psF[4][:], ones_row[:], mu, start=True, stop=True)],
                    r=("ones_row", "f.mu"), w="pf4")
                p.group("tensor", [lambda e, rs=rs: e.matmul(
                    psF[5][:], ones_row[:], rs, start=True, stop=True)],
                    r=("ones_row", "f.rs"), w="pf5")
                for g in range(G):
                    j = g % 2
                    p.dma("sync", lambda e, g=g, t0c=t0c, j=j, sub=sub: e.dma_start(
                        out=rx[j][:, 0:512],
                        in_=resd[g * 128:(g + 1) * 128,
                                 t0c + sub * 512:t0c + (sub + 1) * 512]),
                        r="resd%d.%d" % (g, t0c // 512 + sub), w="rx%d" % j)
                    p.do("vector", lambda e, j=j: e.tensor_tensor(
                        xtmp[:], rx[j][:, 0:512], psF[4][:], ALU.subtract),
                        r=("rx%d" % j, "pf4"), w="xtmp")
                    p.do("vector", lambda e, g=g, sub=sub: e.tensor_tensor(
                        xlnf[:, g * 1024 + sub * 512: g * 1024 + (sub + 1) * 512],
                        xtmp[:], psF[5][:], ALU.mult),
                        r=("xtmp", "pf5"), w="xlnf%d" % sub)
            for half in range(2):
                for ws in range(16):
                    c0 = half * 4096 + ws * 256
                    jj = ws % 2
                    wg = wsec if jj == 0 else wsec2
                    wu = wsec3 if jj == 0 else wsec4
                    gk, uk = "wsec" if jj == 0 else "wsec2", "wsec3" if jj == 0 else "wsec4"
                    p.dma("gpsimd", lambda e, c0=c0, wg=wg: e.dma_start(
                        out=wg[:].rearrange("p (a o) -> p a o", a=G),
                        in_=dram["fg_w"][:, c0:c0 + 256]
                        .rearrange("(a p) o -> p a o", p=128)), w=gk)
                    p.dma("gpsimd", lambda e, c0=c0, wu=wu: e.dma_start(
                        out=wu[:].rearrange("p (a o) -> p a o", a=G),
                        in_=dram["fu_w"][:, c0:c0 + 256]
                        .rearrange("(a p) o -> p a o", p=128)), w=uk)
                    for igl in range(2):
                        ig = ws * 2 + igl
                        iabs = half * 32 + ig
                        for sub in range(2):
                            p.group("tensor", [lambda e, igl=igl, fg=fg, sub=sub,
                                               wg=wg: e.matmul(
                                psF[6][:],
                                wg[:, fg * 256 + igl * 128: fg * 256 + (igl + 1) * 128],
                                xlnf[:, fg * 1024 + sub * 512: fg * 1024 + (sub + 1) * 512],
                                start=(fg == 0), stop=(fg == G - 1)) for fg in range(G)],
                                r=(gk, "xlnf%d" % sub), w="pf6")
                            p.do("scalar", lambda e, sub=sub, iabs=iabs: e.activation(
                                gbf[:], psF[6][:], AF.Silu,
                                scale=vecs["sgate"][:, iabs:iabs + 1]),
                                r="pf6", w="gbf")
                            p.group("tensor", [lambda e, igl=igl, fg=fg, sub=sub,
                                               wu=wu: e.matmul(
                                psF[7][:],
                                wu[:, fg * 256 + igl * 128: fg * 256 + (igl + 1) * 128],
                                xlnf[:, fg * 1024 + sub * 512: fg * 1024 + (sub + 1) * 512],
                                start=(fg == 0), stop=(fg == G - 1)) for fg in range(G)],
                                r=(uk, "xlnf%d" % sub), w="pf7")
                            p.do("scalar", lambda e, sub=sub, iabs=iabs: e.activation(
                                ubf[:], psF[7][:], AF.Identity,
                                scale=vecs["sup"][:, iabs:iabs + 1]),
                                r="pf7", w="ubf")
                            p.do("vector", lambda e, ig=ig, sub=sub: e.tensor_tensor(
                                fusef[:, ig * 1024 + sub * 512: ig * 1024 + (sub + 1) * 512],
                                gbf[:], ubf[:], ALU.mult),
                                r=("gbf", "ubf"), w="fusef")
                for osec in range(8):
                    oc0 = osec * 256
                    wkey = "wdn" if osec % 2 == 0 else "wdn2"
                    wbuf = wdn if osec % 2 == 0 else wdn2
                    p.dma("gpsimd", lambda e, half=half, oc0=oc0, wbuf=wbuf: e.dma_start(
                        out=wbuf[:].rearrange("p (a o) -> p a o", a=32),
                        in_=dram["fd_w"][half * 4096:(half + 1) * 4096, oc0:oc0 + 256]
                        .rearrange("(a p) o -> p a o", p=128)), w=wkey)
                    for ogl in range(2):
                        og = osec * 2 + ogl
                        for sub in range(2):
                            pd = "pf%d" % (sub % 2)
                            p.group("tensor", [lambda e, ogl=ogl, ig=ig, sub=sub,
                                               wbuf=wbuf: e.matmul(
                                psF[sub % 2][:],
                                wbuf[:, ig * 256 + ogl * 128: ig * 256 + (ogl + 1) * 128],
                                fusef[:, ig * 1024 + sub * 512: ig * 1024 + (sub + 1) * 512],
                                start=(ig == 0), stop=(ig == 31)) for ig in range(32)],
                                r=(wkey, "fusef"), w=pd)
                            if half == 0:
                                p.do("vector", lambda e, sub=sub: e.tensor_scalar(
                                    r2s[:], psF[sub % 2][:],
                                    1.0, None, ALU.mult), r=pd, w="r2s")
                                p.dma("scalar", lambda e, og=og, sub=sub: e.dma_start(
                                    out=dhalf[og * 128:(og + 1) * 128,
                                              sub * 512:(sub + 1) * 512],
                                    in_=r2s[:]),
                                    r="r2s", w="dhalf%d" % og)
                            else:
                                p.dma("sync", lambda e, og=og, sub=sub: e.dma_start(
                                    out=xtmp[:],
                                    in_=dhalf[og * 128:(og + 1) * 128,
                                              sub * 512:(sub + 1) * 512]),
                                    r="dhalf%d" % og, w="xtmp")
                                p.do("vector", lambda e, sub=sub: e.tensor_tensor(
                                    xtmp[:], xtmp[:], psF[sub % 2][:], ALU.add),
                                    r=("xtmp", pd), w="xtmp")
                                p.dma("sync", lambda e, og=og, sub=sub, tg=tg: e.dma_start(
                                    out=rx[sub][:, 0:512],
                                    in_=resd[og * 128:(og + 1) * 128,
                                             tg * 1024 + sub * 512:
                                             tg * 1024 + (sub + 1) * 512]),
                                    r="resd%d.%d" % (og, 2 * tg + sub), w="rx%d" % sub)
                                p.do("vector", lambda e, og=og, sub=sub: e.scalar_tensor_tensor(
                                    r2s[:], xtmp[:],
                                    vecs["sdown"][:, og:og + 1], rx[sub][:, 0:512],
                                    ALU.mult, ALU.add),
                                    r=("xtmp", "rx%d" % sub), w="r2s")
                                p.dma("scalar", lambda e, og=og, sub=sub, tg=tg:
                                      e.dma_start(
                                    out=resd[og * 128:(og + 1) * 128,
                                             tg * 1024 + sub * 512:
                                             tg * 1024 + (sub + 1) * 512],
                                    in_=r2s[:]),
                                    r="r2s", w="resd%d.%d" % (og, 2 * tg + sub))

        for tg in range(2):
            ffn_group(tg)
        p.barrier()
        pf.close()

    if do_p3:
        emit_ffn_phase(final=False)
        if "resid2" in dbg:
            p.dma("sync", lambda e: e.dma_start(out=dbg["resid2"], in_=resd),
                  r="resd0.0", w="dbg_resid2")
            p.barrier()

    if do_p4:
        p4 = ExitStack()
        SB4 = lambda name, shape, dt: p4.enter_context(nc.sbuf_tensor(name, shape, dt))
        PS4 = lambda name, shape, dt=F32: p4.enter_context(nc.psum_tensor(name, shape, dt))
        xin4 = SB4("xin4", [128, G * 512], BF16)
        xmix = SB4("xmix", [128, G * 512], BF16)
        powt = SB4("powt", [128, G * 512], F32)
        abuf = SB4("abuf", [128, G * 512], F32)
        pbuf = SB4("pbuf", [128, G * 512], F32)
        rbuf = SB4("rbuf", [128, G * 512], BF16)
        ltmp = SB4("ltmp", [128, 512], F32)
        yN = SB4("yN", [128, 513], F32)
        yD = SB4("yD", [128, 513], F32)
        w4 = [SB4("w4_%d" % j, [128, G * 256], BF16) for j in range(2)]
        rx4 = [SB4("rx4_%d" % j, [128, 512], F32) for j in range(2)]
        r3s = SB4("r3s", [128, 2 * 512], F32)
        junk4 = SB4("junk4", [128, 512], F32)
        e1sb = SB4("e1sb", [128, 64], F32)
        e1pick = SB4("e1pick", [128, 64], F32)
        e2sb = SB4("e2sb", [128, 32], F32)
        e2all = SB4("e2all", [128, 256], F32)
        sce = SB4("sce", [128, 32], F32)
        small = SB4("small", [128, 16 * 16], F32)
        SM = {nm: small[:, i * 16:(i + 1) * 16] for i, nm in enumerate(
            ("dcol", "wcol", "onemk", "onemv", "onemr", "d512", "carryN",
             "carryD", "prevcol", "sxc", "d1024a", "d1024b", "d512a",
             "d512b", "tmp1", "tmp2"))}
        gst = SB4("gst", [128, 4 * 32], F32)
        GQ = {nm: gst[:, i * 32:(i + 1) * 32] for i, nm in enumerate(
            ("Gcur", "Gprev", "Gd", "Gam"))}
        prow4 = SB4("prow4", [1, 1536], F32)
        pr4 = {"mu0": prow4[0:1, 0:512], "rs0": prow4[0:1, 512:1024],
               "t0": prow4[0:1, 1024:1536]}
        ps4 = [PS4("p4%d" % j, [128, 512]) for j in range(6)]

        p.do("vector", lambda e: e.memset(small[:], 0.0),
             w=tuple("sm." + nm for nm in (
                 "dcol", "wcol", "onemk", "onemv", "onemr", "d512", "carryN",
                 "carryD", "prevcol", "sxc", "d1024a", "d1024b", "d512a",
                 "d512b", "tmp1", "tmp2")))
        p.do("scalar", lambda e: e.activation(SM["dcol"], vecs["td"][:], AF.Exp),
             r="v_td", w="sm.dcol")
        p.do("scalar", lambda e: e.activation(SM["dcol"], SM["dcol"], AF.Exp,
             scale=-1.0), r="sm.dcol", w="sm.dcol")
        p.do("scalar", lambda e: e.activation(SM["wcol"], vecs["cw"][:], AF.Exp),
             r="v_cw", w="sm.wcol")
        for nm, src in (("onemk", "mixk"), ("onemv", "mixv"), ("onemr", "mixr")):
            p.do("vector", lambda e, nm=nm, src=src: e.tensor_scalar(
                SM[nm], vecs[src][:], -1.0, 1.0, ALU.mult, ALU.add),
                r="v_" + src, w="sm." + nm)
        p.do("vector", lambda e: e.memset(junk4[:], 0.0), w="junk4")
        p.do("vector", lambda e: e.memset(junk4[:, 0:1], 1.0), r="junk4", w="junk4")
        for g in range(G):
            p.do("vector", lambda e, g=g: e.tensor_tensor_scan(
                powt[:, g * 512:(g + 1) * 512],
                SM["dcol"][:, g:g + 1].broadcast_to([128, 512]),
                junk4[:], 0.0, ALU.mult, ALU.add),
                r=("sm.dcol", "junk4"), w="powt")
        for g in range(G):
            p.do("vector", lambda e, g=g: e.tensor_scalar(
                SM["d512"][:, g:g + 1], powt[:, g * 512 + 511:g * 512 + 512],
                SM["dcol"][:, g:g + 1], None, ALU.mult),
                r=("powt", "sm.dcol"), w="sm.d512")
        p.do("vector", lambda e: e.tensor_copy(SM["d512a"], SM["d512"]),
             r="sm.d512", w="sm.d512a")
        p.do("vector", lambda e: e.tensor_copy(SM["d512b"], SM["d512"]),
             r="sm.d512", w="sm.d512b")
        p.do("vector", lambda e: e.tensor_tensor(
            SM["d1024a"], SM["d512"], SM["d512"], ALU.mult),
            r="sm.d512", w="sm.d1024a")
        p.do("vector", lambda e: e.tensor_copy(SM["d1024b"], SM["d1024a"]),
             r="sm.d1024a", w="sm.d1024b")
        p.do("vector", lambda e: e.memset(gst[:], 0.0),
             w=("g.Gcur", "g.Gprev", "g.Gd", "g.Gam"))

        def p4_prep(cI):
            t0c = cI * 512
            sb0, sb1 = (0, 1) if cI % 2 == 0 else (4, 5)
            for g in range(G):
                j = g % 2
                p.dma("sync", lambda e, g=g, t0c=t0c, j=j: e.dma_start(
                    out=rx4[j][:], in_=resd[g * 128:(g + 1) * 128, t0c:t0c + 512]),
                    r="resd%d.%d" % (g, t0c // 512), w="rx4_%d" % j)
                p.group("tensor", [lambda e, j=j, g=g, sb0=sb0: e.matmul(
                    ps4[sb0][0:1, :], ones_col[:], rx4[j][:],
                    start=(g == 0), stop=(g == G - 1))],
                    r=("rx4_%d" % j, "ones_col"), w="p4%d" % sb0)
                p.do("scalar", lambda e, j=j: e.activation(
                    junk4[:], rx4[j][:], AF.Square), r="rx4_%d" % j, w="junk4")
                p.group("tensor", [lambda e, g=g, sb1=sb1: e.matmul(
                    ps4[sb1][0:1, :], ones_col[:], junk4[:],
                    start=(g == 0), stop=(g == G - 1))],
                    r=("junk4", "ones_col"), w="p4%d" % sb1)
            p.do("vector", lambda e, sb0=sb0: e.tensor_scalar(
                pr4["mu0"], ps4[sb0][0:1, :], 1.0 / D, None, ALU.mult),
                r="p4%d" % sb0, w="p4.mu")
            p.do("vector", lambda e, sb1=sb1: e.tensor_scalar(
                pr4["t0"], ps4[sb1][0:1, :], 1.0 / D, None, ALU.mult),
                r="p4%d" % sb1, w="p4.t")
            p.do("vector", lambda e: e.tensor_tensor(
                junk4[0:1, :], pr4["mu0"], pr4["mu0"], ALU.mult),
                r="p4.mu", w="junk4")
            p.do("vector", lambda e: e.tensor_tensor(
                pr4["t0"], pr4["t0"], junk4[0:1, :], ALU.subtract),
                r=("p4.t", "junk4"), w="p4.t")
            margin("vector")
            p.do("scalar", lambda e: e.activation(
                pr4["t0"], pr4["t0"], AF.Ln, bias=eps_ln[0:1, :]),
                r="p4.t", w="p4.t")
            p.do("scalar", lambda e: e.activation(
                pr4["rs0"], pr4["t0"], AF.Exp, scale=-0.5), r="p4.t", w="p4.rs")
            margin("scalar")
            p.group("tensor", [lambda e: e.matmul(
                ps4[2][:], ones_row[:], pr4["mu0"], start=True, stop=True)],
                r=("ones_row", "p4.mu"), w="p42")
            p.group("tensor", [lambda e: e.matmul(
                ps4[3][:], ones_row[:], pr4["rs0"], start=True, stop=True)],
                r=("ones_row", "p4.rs"), w="p43")
            for g in range(G):
                j = g % 2
                p.dma("sync", lambda e, g=g, t0c=t0c, j=j: e.dma_start(
                    out=rx4[j][:], in_=resd[g * 128:(g + 1) * 128, t0c:t0c + 512]),
                    r="resd%d.%d" % (g, t0c // 512), w="rx4_%d" % j)
                p.do("vector", lambda e, j=j: e.tensor_tensor(
                    junk4[:], rx4[j][:], ps4[2][:], ALU.subtract),
                    r=("rx4_%d" % j, "p42"), w="junk4")
                p.do("vector", lambda e: e.tensor_tensor(
                    junk4[:], junk4[:], ps4[3][:], ALU.mult),
                    r=("junk4", "p43"), w="junk4")
                p.do("scalar", lambda e, g=g: e.activation(
                    xin4[:, g * 512:(g + 1) * 512], junk4[:], AF.Identity,
                    scale=vecs["lnw"][:, g:g + 1], bias=vecs["lnb"][:, g:g + 1]),
                    r="junk4", w="xin4")
                p.do("vector", lambda e, g=g, cI=cI: e.tensor_copy(
                    e1sb[:, cI * 16 + g: cI * 16 + g + 1],
                    xin4[:, g * 512 + 511: g * 512 + 512]),
                    r="xin4", w="e1sb")
                p.do("vector", lambda e, g=g, j=j: e.tensor_tensor(
                    rx4[j][:], rx4[j][:], xin4[:, g * 512:(g + 1) * 512], ALU.add),
                    r=("rx4_%d" % j, "xin4"), w="rx4_%d" % j)
                p.dma("scalar", lambda e, g=g, t0c=t0c, j=j: e.dma_start(
                    out=resd[g * 128:(g + 1) * 128, t0c:t0c + 512], in_=rx4[j][:]),
                    r="rx4_%d" % j, w="resd%d.%d" % (g, t0c // 512))
            p.dma("scalar", lambda e, cI=cI: e.dma_start(
                out=xind[:, cI * 512:(cI + 1) * 512]
                .rearrange("(g p) t -> p g t", p=128),
                in_=xin4[:].rearrange("p (g t) -> p g t", g=G)),
                r="xin4", w="xind")

        for cI in range(4):
            p4_prep(cI)

        p.dma("gpsimd", lambda e: e.dma_start(out=e1i.ap(), in_=e1sb[:]),
              r="e1sb", w="e1i")
        p.do("gpsimd", lambda e: e.collective_compute(
            "AllGather", ALU.bypass, replica_groups=[list(range(NCORES))],
            ins=[e1i.ap().opt()], outs=[e1o.ap().opt()]),
            inc="CC", r="e1i", w="e1o")
        e1all = abuf[:, 0:512]
        p.dma("gpsimd", lambda e: e.dma_start(
            out=e1all.rearrange("p (r j) -> p r j", r=NCORES),
            in_=e1o.ap().rearrange("(r p) j -> p r j", p=128)),
            r="e1o", w="abuf")
        for cI in range(4):
            p.dma("sync", lambda e, cI=cI: e.dma_start(
                out=junk4[:], in_=dram["sxsel"][cI]), w="junk4")
            p.do("vector", lambda e, cI=cI: e.tensor_tensor(
                ltmp[:, 0:512], e1all, junk4[:], ALU.mult),
                r=("abuf", "junk4"), w="ltmp")
            p.do("vector", lambda e, cI=cI: e.reduce_sum(
                e1pick[:, cI * 16:(cI + 1) * 16],
                ltmp[:, 0:512].rearrange("p (r j g) -> p g j r", r=8, j=4),
                axis=AX.XY), r="ltmp", w="e1pick")

        def p4_front(cI):
            p.dma("sync", lambda e, cI=cI: e.dma_start(
                out=xin4[:].rearrange("p (g t) -> p g t", g=G),
                in_=xind[:, cI * 512:(cI + 1) * 512]
                .rearrange("(g p) t -> p g t", p=128)),
                r="xind", w="xin4")
            p.do("vector", lambda e, cI=cI: e.scalar_tensor_tensor(
                SM["sxc"], SM["prevcol"], vecs["sxown"][:, cI:cI + 1],
                e1pick[:, cI * 16:(cI + 1) * 16], ALU.mult, ALU.add),
                r=("sm.prevcol", "v_sxown", "e1pick"), w="sm.sxc")
            emit_mix(xmix, "xmix", "mixk", "onemk")

        def emit_mix(dstbuf, dstkey, mixv_, onem, extraw=()):
            for g in range(G):
                wk = (dstkey,) + tuple(e % g for e in extraw)
                p.do("vector", lambda e, g=g, mixv_=mixv_, dstbuf=dstbuf: e.tensor_scalar(
                    dstbuf[:, g * 512 + 1: (g + 1) * 512],
                    xin4[:, g * 512 + 1: (g + 1) * 512],
                    vecs[mixv_][:, g:g + 1], None, ALU.mult),
                    r=("xin4", "v_" + mixv_), w=wk)
                p.do("vector", lambda e, g=g, onem=onem, dstbuf=dstbuf: e.scalar_tensor_tensor(
                    dstbuf[:, g * 512 + 1: (g + 1) * 512],
                    xin4[:, g * 512: (g + 1) * 512 - 1],
                    SM[onem][:, g:g + 1],
                    dstbuf[:, g * 512 + 1: (g + 1) * 512], ALU.mult, ALU.add),
                    r=("xin4", "sm." + onem, dstkey), w=dstkey)
                p.do("vector", lambda e, g=g, mixv_=mixv_, dstbuf=dstbuf: e.tensor_scalar(
                    dstbuf[:, g * 512: g * 512 + 1],
                    xin4[:, g * 512: g * 512 + 1],
                    vecs[mixv_][:, g:g + 1], None, ALU.mult),
                    r=("xin4", "v_" + mixv_), w=dstkey)
                p.do("vector", lambda e, g=g, onem=onem, dstbuf=dstbuf: e.scalar_tensor_tensor(
                    dstbuf[:, g * 512: g * 512 + 1],
                    SM["sxc"][:, g:g + 1], SM[onem][:, g:g + 1],
                    dstbuf[:, g * 512: g * 512 + 1], ALU.mult, ALU.add),
                    r=("sm.sxc", "sm." + onem, dstkey), w=dstkey)

        def emit_mats(W, srcbuf, srckey, evac, wb=None, wn="w4_%d",
                      secs=range(8)):
            if wb is None:
                wb = w4
            for osec in secs:
                jj = osec % 2
                p.dma("gpsimd", lambda e, W=W, osec=osec, jj=jj, wb=wb,
                      wn=wn: e.dma_start(
                    out=wb[jj][:].rearrange("p (a o) -> p a o", a=G),
                    in_=dram[W][:, osec * 256:(osec + 1) * 256]
                    .rearrange("(a p) o -> p a o", p=128)), w=wn % jj)
                for ogl in range(2):
                    og = osec * 2 + ogl
                    pj = og % 2
                    p.group("tensor", [lambda e, ogl=ogl, fg=fg, jj=jj, pj=pj,
                                       srcbuf=srcbuf, wb=wb: e.matmul(
                        ps4[pj][:],
                        wb[jj][:, fg * 256 + ogl * 128: fg * 256 + (ogl + 1) * 128],
                        srcbuf[:, fg * 512:(fg + 1) * 512],
                        start=(fg == 0), stop=(fg == G - 1)) for fg in range(G)],
                        r=(wn % jj, srckey), w="p4%d" % pj)
                    if evac == "k":
                        p.do("scalar", lambda e, og=og, pj=pj: e.activation(
                            abuf[:, og * 512:(og + 1) * 512], ps4[pj][:], AF.Exp),
                            r="p4%d" % pj, w="abuf%d" % og)
                    elif evac == "v":
                        p.do("scalar", lambda e, og=og, pj=pj: e.activation(
                            pbuf[:, og * 512:(og + 1) * 512], ps4[pj][:], AF.Copy),
                            r="p4%d" % pj, w="pbuf%d" % og)
                    else:
                        p.do("scalar", lambda e, og=og, pj=pj: e.activation(
                            rbuf[:, og * 512:(og + 1) * 512], ps4[pj][:],
                            AF.Sigmoid), r="p4%d" % pj,
                            w=("rbuf%d" % og, "rbufmix"))

        def p4_scans(cI):
            for g in range(G):
                sl = slice(g * 512, (g + 1) * 512)
                p.do("vector", lambda e, sl=sl, g=g: e.tensor_tensor(
                    pbuf[:, sl], abuf[:, sl], pbuf[:, sl], ALU.mult),
                    r=("abuf%d" % g, "pbuf%d" % g), w="pbuf%d" % g)
            for g in range(G):
                sl = slice(g * 512, (g + 1) * 512)
                gc = slice(g, g + 1)
                p.do("vector", lambda e, gc=gc, cI=cI: e.tensor_scalar(
                    yN[:, 0:1], SM["carryN"][:, gc], vecs["carrymask"][:, cI:cI + 1],
                    None, ALU.mult), r=("sm.carryN", "v_carrymask"), w="yN")
                p.do("vector", lambda e, gc=gc, cI=cI: e.tensor_scalar(
                    yD[:, 0:1], SM["carryD"][:, gc], vecs["carrymask"][:, cI:cI + 1],
                    None, ALU.mult), r=("sm.carryD", "v_carrymask"), w="yD")
                p.do("vector", lambda e, sl=sl, gc=gc, g=g: e.tensor_tensor_scan(
                    yN[:, 1:513], SM["dcol"][:, gc].broadcast_to([128, 512]),
                    pbuf[:, sl], yN[:, 0:1], ALU.mult, ALU.add),
                    r=("sm.dcol", "pbuf%d" % g, "yN"), w="yN")
                p.do("vector", lambda e, sl=sl, gc=gc, g=g: e.tensor_tensor_scan(
                    yD[:, 1:513], SM["dcol"][:, gc].broadcast_to([128, 512]),
                    abuf[:, sl], yD[:, 0:1], ALU.mult, ALU.add),
                    r=("sm.dcol", "abuf%d" % g, "yD"), w="yD")
                p.do("vector", lambda e, gc=gc: e.tensor_copy(
                    SM["carryN"][:, gc], yN[:, 512:513]), r="yN", w="sm.carryN")
                p.do("vector", lambda e, gc=gc: e.tensor_copy(
                    SM["carryD"][:, gc], yD[:, 512:513]), r="yD", w="sm.carryD")
                p.do("vector", lambda e, sl=sl, gc=gc, g=g: e.scalar_tensor_tensor(
                    pbuf[:, sl], pbuf[:, sl], SM["wcol"][:, gc], yN[:, 0:512],
                    ALU.mult, ALU.add), r=("pbuf%d" % g, "sm.wcol", "yN"),
                    w="pbuf%d" % g)
                p.do("vector", lambda e, sl=sl, gc=gc, g=g: e.scalar_tensor_tensor(
                    abuf[:, sl], abuf[:, sl], SM["wcol"][:, gc], yD[:, 0:512],
                    ALU.mult, ALU.add), r=("abuf%d" % g, "sm.wcol", "yD"),
                    w="abuf%d" % g)

        def p4_exchange(cI):
            p.do("vector", lambda e: e.tensor_copy(e2sb[:, 0:16], SM["carryN"]),
                 r="sm.carryN", w="e2sb")
            p.do("vector", lambda e: e.tensor_copy(e2sb[:, 16:32], SM["carryD"]),
                 r="sm.carryD", w="e2sb")
            margin("vector")
            p.dma("gpsimd", lambda e: e.dma_start(out=e2i.ap(), in_=e2sb[:]),
                  r="e2sb", w="e2i")
            p.do("gpsimd", lambda e: e.collective_compute(
                "AllGather", ALU.bypass, replica_groups=[list(range(NCORES))],
                ins=[e2i.ap().opt()], outs=[e2o.ap().opt()]),
                inc="CC", r="e2i", w="e2o")
            p.dma("gpsimd", lambda e: e.dma_start(
                out=e2all[:].rearrange("p (r j) -> p r j", r=NCORES),
                in_=e2o.ap().rearrange("(r p) j -> p r j", p=128)),
                r="e2o", w="e2all")

        def p4_gamma(cI):
            p.dma("sync", lambda e, cI=cI: e.dma_start(
                out=junk4[:, 0:256], in_=dram["mselS"][cI]), r="junk4", w="junk4")
            p.do("vector", lambda e, cI=cI: e.tensor_tensor(
                ltmp[:, 0:256], e2all[:], junk4[:, 0:256],
                ALU.mult), r=("e2all", "junk4"), w="ltmp")
            p.do("vector", lambda e: e.reduce_sum(
                sce[:], ltmp[:, 0:256].rearrange("p (r j) -> p j r", r=8),
                axis=AX.X), r="ltmp", w="sce")
            p.do("vector", lambda e: e.tensor_copy(GQ["Gprev"], GQ["Gcur"]),
                 r="g.Gcur", w="g.Gprev")
            p.do("vector", lambda e: e.tensor_tensor(
                GQ["Gcur"], GQ["Gprev"], small[:, 10 * 16:12 * 16],
                ALU.mult), r=("g.Gprev", "sm.d1024a", "sm.d1024b"), w="g.Gcur")
            p.do("vector", lambda e: e.tensor_tensor(
                GQ["Gcur"], GQ["Gcur"], sce[:], ALU.add),
                r=("g.Gcur", "sce"), w="g.Gcur")
            p.do("vector", lambda e: e.tensor_tensor(
                GQ["Gd"], GQ["Gprev"], small[:, 12 * 16:14 * 16], ALU.mult),
                r=("g.Gprev", "sm.d512a", "sm.d512b"), w="g.Gd")
            p.do("vector", lambda e, cI=cI: e.tensor_scalar(
                GQ["Gam"], GQ["Gcur"], vecs["gmask_a"][:, cI:cI + 1], None,
                ALU.mult), r=("g.Gcur", "v_gmask_a"), w="g.Gam")
            p.do("vector", lambda e, cI=cI: e.scalar_tensor_tensor(
                GQ["Gam"], GQ["Gd"], vecs["gmask_e"][:, cI:cI + 1], GQ["Gam"],
                ALU.mult, ALU.add), r=("g.Gd", "v_gmask_e", "g.Gam"), w="g.Gam")

        def p4_ratio(cI, gs=range(16)):
            for g in gs:
                sl = slice(g * 512, (g + 1) * 512)
                p.do("vector", lambda e, sl=sl, g=g: e.scalar_tensor_tensor(
                    pbuf[:, sl], powt[:, sl], GQ["Gam"][:, g:g + 1], pbuf[:, sl],
                    ALU.mult, ALU.add), r=("powt", "g.Gam", "pbuf%d" % g),
                    w="pbuf%d" % g)
                p.do("vector", lambda e, sl=sl, g=g: e.scalar_tensor_tensor(
                    abuf[:, sl], powt[:, sl], GQ["Gam"][:, 16 + g:17 + g],
                    abuf[:, sl], ALU.mult, ALU.add),
                    r=("powt", "g.Gam", "abuf%d" % g), w="abuf%d" % g)
                p.do("scalar", lambda e, sl=sl, g=g: e.activation(
                    ltmp[:], abuf[:, sl], AF.Ln), r="abuf%d" % g, w="ltmp")
                p.do("scalar", lambda e, sl=sl, g=g: e.activation(
                    abuf[:, sl], ltmp[:], AF.Exp, scale=-1.0), r="ltmp",
                    w="abuf%d" % g)
                p.do("vector", lambda e, sl=sl, g=g: e.tensor_tensor(
                    pbuf[:, sl], pbuf[:, sl], abuf[:, sl], ALU.mult),
                    r=("pbuf%d" % g, "abuf%d" % g), w="pbuf%d" % g)
                p.do("vector", lambda e, sl=sl, g=g: e.tensor_tensor(
                    rbuf[:, sl], rbuf[:, sl], pbuf[:, sl], ALU.mult),
                    r=("rbuf%d" % g, "pbuf%d" % g), w="rbuf%d" % g)

        def p4_wo(cI):
            for osec in range(8):
                jj = osec % 2
                p.dma("gpsimd", lambda e, osec=osec, jj=jj: e.dma_start(
                    out=w4[jj][:].rearrange("p (a o) -> p a o", a=G),
                    in_=dram["wo_w"][:, osec * 256:(osec + 1) * 256]
                    .rearrange("(a p) o -> p a o", p=128)), w="w4_%d" % jj)
                for ogl in range(2):
                    og = osec * 2 + ogl
                    pj = og % 2
                    blds = [lambda e, ogl=ogl, fg=fg, jj=jj, pj=pj: e.matmul(
                        ps4[pj][:],
                        w4[jj][:, fg * 256 + ogl * 128: fg * 256 + (ogl + 1) * 128],
                        rbuf[:, fg * 512:(fg + 1) * 512],
                        start=(fg == 0), stop=(fg == G - 1)) for fg in range(G)]
                    p.group("tensor", blds,
                            r=tuple("rbuf%d" % g for g in range(G)) +
                              ("w4_%d" % jj,),
                            w="p4%d" % pj)
                    p.dma("sync", lambda e, og=og, cI=cI, pj=pj: e.dma_start(
                        out=rx4[pj][:],
                        in_=resd[og * 128:(og + 1) * 128, cI * 512:(cI + 1) * 512]),
                        r="resd%d.%d" % (og, cI), w="rx4_%d" % pj)
                    p.do("vector", lambda e, og=og, pj=pj: e.tensor_tensor(
                        r3s[:, pj * 512:(pj + 1) * 512], rx4[pj][:], ps4[pj][:],
                        ALU.add), r=("rx4_%d" % pj, "p4%d" % pj), w="r3s%d" % pj)
                    p.dma("scalar", lambda e, og=og, cI=cI, pj=pj: e.dma_start(
                        out=resd[og * 128:(og + 1) * 128, cI * 512:(cI + 1) * 512],
                        in_=r3s[:, pj * 512:(pj + 1) * 512]),
                        r="r3s%d" % pj, w="resd%d.%d" % (og, cI))

        p4_front(0)
        emit_mats("wk_w", xmix, "xmix", "k")
        for cI in range(4):
            emit_mix(rbuf, "rbufmix", "mixv", "onemv", extraw=("rbuf%d",))
            emit_mats("wv_w", rbuf, "rbufmix", "v")
            emit_mix(xmix, "xmix", "mixr", "onemr")
            p.do("vector", lambda e: e.tensor_copy(
                SM["prevcol"],
                xin4[:].rearrange("p (g t) -> p g t", g=G)[:, :, 511]),
                r="xin4", w="sm.prevcol")
            emit_mats("wr_w", xmix, "xmix", "r")
            p4_scans(cI)
            p4_exchange(cI)
            if cI < 3:
                p4_front(cI + 1)
            p4_gamma(cI)
            if cI < 3:
                for osec in range(4):
                    p4_ratio(cI, range(osec * 2, osec * 2 + 2))
                    emit_mats("wk_w", xmix, "xmix", "k", secs=[osec])
                p4_ratio(cI, range(8, 16))
            else:
                p4_ratio(cI)
            p4_wo(cI)
            if cI < 3:
                emit_mats("wk_w", xmix, "xmix", "k", secs=range(4, 8))

        p.barrier()
        if "resid3" in dbg:
            p.dma("sync", lambda e: e.dma_start(out=dbg["resid3"], in_=resd),
                  r="resd0.0", w="dbg_resid3")
            p.barrier()
        p4.close()

    if do_p5:
        emit_ffn_phase(final=True)
        p6 = ExitStack()
        tin = p6.enter_context(nc.sbuf_tensor("p6in", [128, G * 128], F32))
        tout = p6.enter_context(nc.sbuf_tensor("p6out", [128, 2048], F32))
        ps6 = [p6.enter_context(nc.psum_tensor("p6ps%d" % j, [128, 512], F32))
               for j in range(2)]
        for tt in range(16):
            p.dma("sync", lambda e, tt=tt: e.dma_start(
                out=tin[:].rearrange("p (g t) -> p g t", g=G),
                in_=resd[:, tt * 128:(tt + 1) * 128]
                .rearrange("(g p) t -> p g t", p=128)),
                r=tuple("resd%d.%d" % (g, tt // 4) for g in range(G)), w="p6in")
            for gb in range(4):
                for gl in range(4):
                    g = gb * 4 + gl
                    p.do("tensor", lambda e, g=g, gl=gl: e.transpose(
                        ps6[0][:, gl * 128:(gl + 1) * 128],
                        tin[:, g * 128:(g + 1) * 128], identf[:]),
                        r=("p6in", "identf"), w="p6a")
                for dd in range(2):
                    p.do("tensor", lambda e, dd=dd: e.transpose(
                        ps6[1][:, dd * 128:(dd + 1) * 128], identf[:], identf[:]),
                        r="identf", w="p6d")
                p.wait("scalar", "TE")
                for gl in range(4):
                    g = gb * 4 + gl
                    p.do("scalar", lambda e, g=g, gl=gl: e.activation(
                        tout[:, g * 128:(g + 1) * 128],
                        ps6[0][:, gl * 128:(gl + 1) * 128], AF.Copy),
                        r="p6a", w="p6out")
            p.dma("scalar", lambda e, tt=tt: e.dma_start(
                out=outy[tt * 128:(tt + 1) * 128, :], in_=tout[:]),
                r="p6out", w="y")
        p6.close()

    for j, (nm, src) in enumerate([("kdN", kdN), ("kdR", kdR), ("vd", vd), ("qd", qd)]):
        if nm in dbg:
            eng = "sync" if j % 2 == 0 else "gpsimd"
            p.dma(eng, lambda e, nm=nm, src=src: e.dma_start(out=dbg[nm], in_=src),
                  r=nm, w="dbg_" + nm)
    p.barrier()

    p.barrier()

    st.close()

    sem_stack = ExitStack()
    for s in p.cnt:
        p.sem[s] = sem_stack.enter_context(nc.semaphore(s))
    with nc.Block() as blk:
        for eng in ENGS:
            def make(eng):
                def run(e):
                    for g, fn in p.ops:
                        if g == eng:
                            fn(e)
                return run
            getattr(blk, eng)(make(eng))
    sem_stack.close()
    return nc



def kernel(**inputs):
    per_core = _host_prepare(inputs)
    nc = build_program()
    res = run_bass_kernel_spmd(nc, per_core, core_ids=list(range(NCORES)))
    out = np.zeros((B, SEQ, D), np.float32)
    for c in range(NCORES):
        b, h = c // 2, c % 2
        y = res.results[c]["y"]
        for i, g0 in enumerate(OWN[h]):
            out[b, g0 * 512:(g0 + 1) * 512, :] = y[i * 512:(i + 1) * 512, :]
    return out


# version 09
# speedup vs baseline: 1.0388x; 1.0388x over previous
import numpy as np
import ml_dtypes
from contextlib import ExitStack

import concourse.bass as bass
import concourse.mybir as mybir
from concourse.bass_utils import run_bass_kernel_spmd

F32 = mybir.dt.float32
BF16 = mybir.dt.bfloat16
I8 = mybir.dt.int8
I32 = mybir.dt.int32
AF = mybir.ActivationFunctionType
ALU = mybir.AluOpType
AX = mybir.AxisListType

NCORES = 8
B = 4
SEQ = 4096
D = 2048
G = 16
NBLK = 8
OWN = {0: [0, 3, 4, 7], 1: [1, 2, 5, 6]}
CH = 512
EPS_LN = 1e-5
EPS_RMS = 1.1920929e-07
ISQ = 0.08838834764831845
_LASTP = None
NEG = -1e9

BF = ml_dtypes.bfloat16



def _vec_fm(v, g):
    return np.ascontiguousarray(v.reshape(g, 128).T.astype(np.float32))


def _host_prepare(inp):
    W = {}
    W["qd_w"] = np.ascontiguousarray(inp["L0_attn_a_q_down_w"].T)
    W["qu_w"] = np.ascontiguousarray(inp["L0_attn_a_q_up_w"].T)
    W["kvd_w"] = np.ascontiguousarray(inp["L0_attn_a_kv_down_w"].T)
    kvu = inp["L0_attn_a_kv_up_w"]
    kvu3 = kvu.reshape(16, 192, 512)
    W["kwk_w"] = np.ascontiguousarray(
        kvu3[:, 0:64, :].reshape(1024, 512).T)
    W["kwv_w"] = np.ascontiguousarray(
        kvu3[:, 64:192, :].reshape(2048, 512).T)
    W["krope_w"] = np.ascontiguousarray(inp["L0_attn_a_k_rope_w"].T)
    W["ow_w"] = np.ascontiguousarray(inp["L0_attn_a_out_w"].T)
    W["wk_w"] = np.ascontiguousarray(inp["L0_attn_b_Wk"].T)
    W["wv_w"] = np.ascontiguousarray(inp["L0_attn_b_Wv"].T)
    W["wr_w"] = np.ascontiguousarray(inp["L0_attn_b_Wr"].T)
    W["wo_w"] = np.ascontiguousarray(inp["L0_attn_b_Wo"].T)
    W["fg_w"] = np.ascontiguousarray(inp["L0_ffn_w_gate"])
    W["fu_w"] = np.ascontiguousarray(inp["L0_ffn_w_up"])
    W["fd_w"] = np.ascontiguousarray(inp["L0_ffn_w_down"])

    W["qd_b"] = _vec_fm(inp["L0_attn_a_q_down_b"], 8)
    W["qnorm_w"] = _vec_fm(inp["L0_attn_a_q_norm_w"], 8)
    W["qu_b"] = _vec_fm(inp["L0_attn_a_q_up_b"], 16)
    W["kvd_b"] = _vec_fm(inp["L0_attn_a_kv_down_b"], 4)
    W["kvnorm_w"] = _vec_fm(inp["L0_attn_a_kv_norm_w"], 4)
    kvub = inp["L0_attn_a_kv_up_b"].reshape(16, 192)
    W["knope_b"] = _vec_fm(np.ascontiguousarray(kvub[:, 0:64]).reshape(1024), 8)
    W["krope_b"] = _vec_fm(inp["L0_attn_a_k_rope_b"], 8)
    W["vb_row"] = np.ascontiguousarray(
        kvub[:, 64:192].reshape(1, 2048).astype(np.float32))
    W["ow_b"] = _vec_fm(inp["L0_attn_a_out_b"], 16)
    W["lnw"] = _vec_fm(inp["L0_attn_b_ln_w"], 16)
    W["lnb"] = _vec_fm(inp["L0_attn_b_ln_b"], 16)
    W["mixk"] = _vec_fm(inp["L0_attn_b_mix_k"], 16)
    W["mixv"] = _vec_fm(inp["L0_attn_b_mix_v"], 16)
    W["mixr"] = _vec_fm(inp["L0_attn_b_mix_r"], 16)
    W["td"] = _vec_fm(inp["L0_attn_b_time_decay"], 16)
    W["cw"] = _vec_fm(inp["L0_attn_b_current_token_weight"], 16)
    W["sgate"] = _vec_fm(inp["L0_ffn_s_gate"], 64)
    W["sup"] = _vec_fm(inp["L0_ffn_s_up"], 64)
    W["sdown"] = _vec_fm(inp["L0_ffn_s_down"], 16)

    x = inp["L0_attn_a_x"]
    per_core = []
    for c in range(NCORES):
        b, h = c // 2, c % 2
        blocks = OWN[h]
        own_rows = np.concatenate(
            [np.arange(g0 * 512, (g0 + 1) * 512) for g0 in blocks])
        d = dict(W)
        d["x_all"] = np.ascontiguousarray(x[b])
        d["x_own"] = np.ascontiguousarray(x[b][own_rows])

        M = np.zeros((4, 8, 128, 512), np.float32)
        for i, g0 in enumerate(blocks):
            bound = 1024 * (i + 1)
            qglob = g0 * 512 + np.arange(512)[None, :]
            for s in range(8):
                k0 = bound - 1024 + s * 128
                kglob = (k0 + np.arange(128))[:, None]
                M[i, s] = np.where(kglob <= qglob, 0.0, NEG)
        d["mmask"] = M.astype(BF)

        cm = [0.0, 0.0, 1.0, 0.0] if h == 0 else [0.0, 1.0, 0.0, 1.0]
        d["carrymask"] = np.broadcast_to(
            np.array(cm, np.float32), (128, 4)).copy()
        a_own = [0.0, 0.0, 1.0, 0.0] if h == 0 else [0.0, 1.0, 0.0, 1.0]
        d["sxown"] = np.broadcast_to(
            np.array(a_own, np.float32), (128, 4)).copy()
        p = c ^ 1
        msel = np.zeros((4, 128, 8 * 64), np.float32)
        e1need = {0: {1: (p, 1), 3: (p, 3)}, 1: {0: (p, 0), 2: (p, 2)}}[h]
        for cI, (rk, j) in e1need.items():
            msel[cI, :, rk * 64 + j * 16: rk * 64 + (j + 1) * 16] = 1.0
        d["sxsel"] = msel
        mselS = np.zeros((4, 128, 8 * 32), np.float32)
        for cI in range(4):
            rk = 2 * b + (cI & 1)
            mselS[cI, :, rk * 32:(rk + 1) * 32] = 1.0
        d["mselS"] = mselS
        ga = [0.0, 1.0, 0.0, 1.0] if h == 0 else [1.0, 0.0, 1.0, 0.0]
        ge = [0.0, 0.0, 1.0, 0.0] if h == 0 else [0.0, 1.0, 0.0, 1.0]
        d["gmask_a"] = np.broadcast_to(np.array(ga, np.float32), (128, 4)).copy()
        d["gmask_e"] = np.broadcast_to(np.array(ge, np.float32), (128, 4)).copy()
        per_core.append(d)
    return per_core



ENGS = ("sync", "gpsimd", "vector", "scalar", "tensor")


class P:

    ESEM = {"sync": "DS0", "gpsimd": "GE", "vector": "VE", "scalar": "SE",
            "tensor": "TE"}
    NLANES = 4
    DQUEUES = {"sync": "DS", "gpsimd": "DG", "scalar": "DC"}
    DMASEMS = tuple("%s%d" % (q, l) for q in ("DS", "DG", "DC")
                    for l in range(4))

    def __init__(self, nc):
        self.nc = nc
        self.ops = []
        self.sem = {}
        self.cnt = {s: 0 for s in
                    P.DMASEMS + ("TE", "VE", "SE", "GE", "CC")}
        self.dma_rr = {"DS": 0, "DG": 0, "DC": 0}
        self.lastw = {}
        self.writer = {}
        self.readers = {}

    def emit(self, eng, fn):
        self.ops.append((eng, fn))

    def wait(self, eng, sem, val=None):
        v = self.cnt[sem] if val is None else val
        if v <= 0 or self.lastw.get((eng, sem), 0) >= v:
            return
        self.lastw[(eng, sem)] = v
        self.emit(eng, lambda e, s=sem, v=v: e.wait_ge(self.sem[s], v))

    def _dep_wait(self, eng, sem, val):
        self.wait(eng, sem, val)

    def _pre(self, eng, r, w):
        for key in r:
            wr = self.writer.get(key)
            if wr is not None:
                self._dep_wait(eng, wr[0], wr[1])
        for key in w:
            wr = self.writer.get(key)
            if wr is not None:
                self._dep_wait(eng, wr[0], wr[1])
            for sem, val in self.readers.get(key, {}).items():
                self._dep_wait(eng, sem, val)

    def _post(self, sem, r, w):
        v = self.cnt[sem]
        for key in w:
            self.writer[key] = (sem, v)
            self.readers[key] = {}
        for key in r:
            self.readers.setdefault(key, {})[sem] = v

    def do(self, eng, build, inc=None, n=None, r=(), w=()):
        if inc is None:
            inc = P.ESEM[eng]
        if n is None:
            n = 16 if inc.startswith(("DS", "DG", "DC")) else 1
        if isinstance(r, str):
            r = (r,)
        if isinstance(w, str):
            w = (w,)
        self._pre(eng, r, w)
        self.cnt[inc] += n
        def f(e, build=build, inc=inc, n=n):
            ins = build(e)
            ins.then_inc(self.sem[inc], n)
        self.emit(eng, f)
        self._post(inc, r, w)
        return self.cnt[inc]

    def dma(self, eng, build, r=(), w=()):
        q = P.DQUEUES[eng]
        lane = self.dma_rr[q]
        self.dma_rr[q] = (lane + 1) % P.NLANES
        sem = "%s%d" % (q, lane)
        self.wait(eng, sem, self.cnt[sem])
        return self.do(eng, build, inc=sem, n=16, r=r, w=w)

    def group(self, eng, builds, inc=None, n=None, r=(), w=()):
        if inc is None:
            inc = P.ESEM[eng]
        if n is None:
            n = 16 if inc in ("DS", "DG", "DC") else 1
        if isinstance(r, str):
            r = (r,)
        if isinstance(w, str):
            w = (w,)
        self._pre(eng, r, w)
        self.cnt[inc] += n
        def f(e, builds=builds, inc=inc, n=n):
            ins = None
            for b in builds:
                ins = b(e)
            ins.then_inc(self.sem[inc], n)
        self.emit(eng, f)
        self._post(inc, r, w)
        return self.cnt[inc]

    def barrier(self, engines=ENGS, sems=None):
        for eng in engines:
            for s in (sems or self.cnt.keys()):
                self.wait(eng, s)


def build_program(debug=(), kv_chunks=8, q_chunks=4, p1_stages=99, dump_at='after_rope', do_p2=True, do_p3=True, do_p4=True, do_p5=True):
    nc = bass.Bass("TRN2", target_bir_lowering=False)
    p = P(nc)
    global _LASTP
    _LASTP = p

    dram = {}
    def din(name, shape, dt):
        dram[name] = nc.dram_tensor(name, shape, dt, kind="ExternalInput").ap()
    din("x_all", [SEQ, D], F32)
    din("x_own", [D, D], F32)
    for nm, sh in [("qd_w", [2048, 1024]), ("qu_w", [1024, 2048]),
                   ("kvd_w", [2048, 512]), ("kwk_w", [512, 1024]),
                   ("kwv_w", [512, 2048]), ("krope_w", [2048, 1024]),
                   ("ow_w", [2048, 2048]), ("wk_w", [2048, 2048]),
                   ("wv_w", [2048, 2048]), ("wr_w", [2048, 2048]),
                   ("wo_w", [2048, 2048])]:
        din(nm, sh, F32)
    for nm in ("fg_w", "fu_w"):
        din(nm, [2048, 8192], I8)
    din("fd_w", [8192, 2048], I8)
    for nm, g in [("qd_b", 8), ("qnorm_w", 8), ("qu_b", 16), ("kvd_b", 4),
                  ("kvnorm_w", 4), ("knope_b", 8), ("krope_b", 8),
                  ("ow_b", 16), ("lnw", 16), ("lnb", 16), ("mixk", 16),
                  ("mixv", 16), ("mixr", 16), ("td", 16), ("cw", 16),
                  ("sgate", 64), ("sup", 64), ("sdown", 16)]:
        din(nm, [128, g], F32)
    din("vb_row", [1, 2048], F32)
    din("mmask", [4, 8, 128, 512], BF16)
    for nm in ("carrymask", "sxown", "gmask_a", "gmask_e"):
        din(nm, [128, 4], F32)
    din("sxsel", [4, 128, 512], F32)
    din("mselS", [4, 128, 256], F32)

    kdN = nc.dram_tensor("kdN", [1024, SEQ], BF16, kind="Internal").ap()
    kdR = nc.dram_tensor("kdR", [1024, SEQ], BF16, kind="Internal").ap()
    vd = nc.dram_tensor("vd", [SEQ, 2048], BF16, kind="Internal").ap()
    qd = nc.dram_tensor("qd", [2048, 2048], BF16, kind="Internal").ap()
    resd = nc.dram_tensor("resd", [2048, 2048], F32, kind="Internal").ap()
    xind = nc.dram_tensor("xind", [2048, 2048], BF16, kind="Internal").ap()
    e1i = nc.dram_tensor("e1i", [128, 64], F32, kind="Internal")
    e1o = nc.dram_tensor("e1o", [128 * NCORES, 64], F32, kind="Internal")
    e2i = nc.dram_tensor("e2i", [128, 32], F32, kind="Internal")
    e2o = nc.dram_tensor("e2o", [128 * NCORES, 32], F32, kind="Internal")

    outy = nc.dram_tensor("y", [2048, D], F32, kind="ExternalOutput").ap()
    dbg = {}
    for nm in debug:
        shp = {"ln0T": [128, 8192], "kdN": [1024, SEQ], "kdR": [1024, SEQ], "vd": [SEQ, 2048],
               "qd": [2048, 2048], "resid1": [2048, 2048], "resid2": [2048, 2048],
               "resid3": [2048, 2048], "xin": [2048, 2048]}[nm]
        dt = BF16 if nm in ("kdN", "kdR", "vd", "qd", "xin") else F32
        if nm == "ln0T":
            shp, dt = [128, 8192], BF16
        dbg[nm] = nc.dram_tensor("dbg_" + nm, shp, dt, kind="ExternalOutput").ap()

    st = ExitStack()
    SB = lambda name, shape, dt: st.enter_context(nc.sbuf_tensor(name, shape, dt))
    PS = lambda name, shape, dt=F32: st.enter_context(nc.psum_tensor(name, shape, dt))

    identb = SB("identb", [128, 128], BF16)
    identf = SB("identf", [128, 128], F32)
    iot = SB("iot", [128, 128], I32)
    ones_row = SB("ones_row", [1, 128], F32)
    ones_col = SB("ones_col", [128, 1], F32)
    ones_colb = SB("ones_colb", [128, 1], BF16)
    vecs = {}
    for nm, g in [("qd_b", 8), ("qnorm_w", 8), ("qu_b", 16), ("kvd_b", 4),
                  ("kvnorm_w", 4), ("knope_b", 8), ("krope_b", 8),
                  ("ow_b", 16), ("lnw", 16), ("lnb", 16), ("mixk", 16),
                  ("mixv", 16), ("mixr", 16), ("td", 16), ("cw", 16),
                  ("sgate", 64), ("sup", 64), ("sdown", 16),
                  ("carrymask", 4), ("sxown", 4), ("gmask_a", 4), ("gmask_e", 4)]:
        vecs[nm] = SB("v_" + nm, [128, g], F32)

    row_a = SB("row_a", [1, 512], F32)
    row_b = SB("row_b", [1, 512], F32)
    eps_ln = SB("eps_ln", [128, 1], F32)
    eps_rms = SB("eps_rms", [128, 1], F32)
    dumV = SB("dumV", [1, 512], F32)
    dumS = SB("dumS", [1, 512], F32)
    dumW = SB("dumW", [128, 512], F32)
    dumWb = SB("dumWb", [128, 512], BF16)

    pwu = ExitStack()
    psW = pwu.enter_context(nc.psum_tensor("psW", [128, 512], F32))
    p.do("vector", lambda e: e.memset(dumW[:], 1.25), w="dumW")
    p.do("vector", lambda e: e.memset(dumWb[:], 1.0), w="dumWb")
    p.do("vector", lambda e: e.memset(dumV[:], 0.0), w="dumV")
    p.do("gpsimd", lambda e: e.memset(dumS[:], 1.0), w="dumS")
    for i in range(24):
        p.do("vector", lambda e: e.tensor_tensor(dumW[:], dumW[:], dumW[:], ALU.max),
             r="dumW", w="dumW")
    for fn in (AF.Exp, AF.Ln, AF.Sigmoid, AF.Silu):
        p.do("scalar", lambda e, fn=fn: e.activation(dumS[:], dumS[:], fn),
             r="dumS", w="dumS")
    for i in range(16):
        p.do("scalar", lambda e: e.activation(dumS[:], dumS[:], AF.Square),
             r="dumS", w="dumS")
    for i in range(24):
        p.group("tensor", [lambda e, i=i: e.matmul(
            psW[:], dumWb[:, 0:128], dumWb[:], start=(i == 0), stop=(i == 23))],
            r="dumWb", w="psW")
    p.do("vector", lambda e: e.tensor_copy(dumW[:], psW[:]), r="psW", w="dumW")
    p.barrier()
    pwu.close()

    for nm in vecs:
        p.dma("sync", lambda e, nm=nm: e.dma_start(out=vecs[nm][:], in_=dram[nm]),
              w="v_" + nm)
    p.do("gpsimd", lambda e: e.iota(iot[:], [[1, 128]], base=0, channel_multiplier=-1),
         w="iot")
    p.do("vector", lambda e: e.tensor_single_scalar(identb[:], iot[:], 0, ALU.is_equal),
         r="iot", w="identb")
    p.do("vector", lambda e: e.tensor_single_scalar(identf[:], iot[:], 0, ALU.is_equal),
         r="iot", w="identf")
    p.do("vector", lambda e: e.memset(ones_row[:], 1.0), w="ones_row")
    p.do("vector", lambda e: e.memset(ones_col[:], 1.0), w="ones_col")
    p.do("vector", lambda e: e.memset(ones_colb[:], 1.0), w="ones_colb")
    p.do("vector", lambda e: e.memset(eps_ln[:], float(EPS_LN)), w="eps_ln")
    p.do("vector", lambda e: e.memset(eps_rms[:], float(EPS_RMS)), w="eps_rms")
    p.barrier()

    def margin(eng):
        if eng == "vector":
            p.do("vector", lambda e: e.tensor_copy(dumV[:], dumV[:]), r="dumV", w="dumV")
        elif eng == "scalar":
            p.do("scalar", lambda e: e.activation(dumS[:], dumS[:], AF.Copy),
                 r="dumS", w="dumS")

    def tokmajor_ln(xsb, ntile, lnout, scr, xkey, lnkey):
        col = lambda nm, tt: scr[nm][:, tt:tt + 1]
        K_ = lambda nm: "ln." + nm
        for tt in range(ntile):
            xs = xsb[:, tt * 2048:(tt + 1) * 2048]
            p.do("vector", lambda e, xs=xs, o=col("mu", tt): e.reduce_sum(o, xs, axis=AX.X),
                 r=xkey, w=K_("mu"))
            p.do("vector", lambda e, xs=xs: e.tensor_tensor(
                scr["junk"][:], xs, xs, ALU.mult), r=xkey, w="ln.junk")
            p.do("vector", lambda e, o=col("ss", tt): e.reduce_sum(
                o, scr["junk"][:], axis=AX.X), r="ln.junk", w=K_("ss"))
            p.do("vector", lambda e, o=col("mu", tt): e.tensor_scalar(
                o, o, 1.0 / D, None, ALU.mult), r=K_("mu"), w=K_("mu"))
            p.do("vector", lambda e, o=col("var", tt), i=col("ss", tt): e.tensor_scalar(
                o, i, 1.0 / D, None, ALU.mult), r=K_("ss"), w=K_("var"))
            p.do("vector", lambda e, o=col("m2", tt), i=col("mu", tt): e.tensor_tensor(
                o, i, i, ALU.mult), r=K_("mu"), w=K_("m2"))
            p.do("vector", lambda e, o=col("var", tt), i=col("m2", tt): e.tensor_tensor(
                o, o, i, ALU.subtract), r=(K_("var"), K_("m2")), w=K_("var"))
        margin("vector")
        for tt in range(ntile):
            p.do("scalar", lambda e, o=col("lnv", tt), i=col("var", tt): e.activation(
                o, i, AF.Ln, bias=eps_ln[:]), r=K_("var"), w=K_("lnv"))
        for tt in range(ntile):
            p.do("scalar", lambda e, o=col("rstd", tt), i=col("lnv", tt): e.activation(
                o, i, AF.Exp, scale=-0.5), r=K_("lnv"), w=K_("rstd"))
        margin("scalar")
        for tt in range(ntile):
            p.do("vector", lambda e, o=col("nb", tt), i=col("mu", tt), rr=col("rstd", tt):
                 e.tensor_scalar(o, i, rr, -1.0, ALU.mult, ALU.mult),
                 r=(K_("mu"), K_("rstd")), w=K_("nb"))
        margin("vector")
        for tt in range(ntile):
            p.do("scalar", lambda e, tt=tt, bb=col("nb", tt), rr=col("rstd", tt):
                 e.activation(
                     lnout[:, tt * 2048:(tt + 1) * 2048],
                     xsb[:, tt * 2048:(tt + 1) * 2048],
                     AF.Identity, bias=bb, scale=rr),
                 r=(xkey, K_("nb"), K_("rstd")), w=lnkey)

    def transpose_to_fm(lnout, ntile, dstT, ptr, Tcols, lnkey, dstkey):
        n = ntile * G
        se_after = {}
        def tr(k):
            tt, fg, slot = k // G, k % G, k % 4
            p.do("tensor", lambda e, tt=tt, fg=fg, slot=slot: e.transpose(
                ptr[slot][:],
                lnout[:, tt * 2048 + fg * 128: tt * 2048 + (fg + 1) * 128],
                identb[:]), r=lnkey, w="ptr%d" % slot)
            se_after[k] = p.cnt["TE"]
        def cp(k):
            tt, fg, slot = k // G, k % G, k % 4
            p.wait("scalar", "TE", se_after[min(k + 2, n + 1)])
            p.do("scalar", lambda e, tt=tt, fg=fg, slot=slot: e.activation(
                dstT[:, fg * Tcols + tt * 128: fg * Tcols + (tt + 1) * 128],
                ptr[slot][:], AF.Copy), r="ptr%d" % slot, w=dstkey)
        for k in range(n):
            tr(k)
            if k >= 2:
                cp(k - 2)
        for d in range(2):
            kk = n + d
            tt, fg, slot = 0, 0, kk % 4
            p.do("tensor", lambda e, slot=slot: e.transpose(
                ptr[slot][:], lnout[:, 0:128], identb[:]),
                r=lnkey, w="ptr%d" % slot)
            se_after[kk] = p.cnt["TE"]
            cp(n - 2 + d)

    def mm_rot(ngroups, emit_mms, emit_evac):
        for k in range(ngroups):
            ps, pskey = (psA, "psA") if k % 2 == 0 else (psB, "psB")
            emit_mms(k, ps, pskey)
            emit_evac(k, ps, pskey)

    p1 = ExitStack()
    SB1 = lambda name, shape, dt: p1.enter_context(nc.sbuf_tensor(name, shape, dt))
    PS1 = lambda name, shape, dt=F32: p1.enter_context(nc.psum_tensor(name, shape, dt))

    xsb = SB1("xsb", [128, 4 * 2048], F32)
    lnsb = SB1("lnsb", [128, 4 * 2048], BF16)
    ln0T2_ = [SB1("ln0T%d" % j, [128, G * 512], BF16) for j in range(2)]
    junk = SB1("junk", [128, 2048], F32)
    scr = {"junk": junk}
    for nm in ("mu", "ss", "var", "m2", "lnv", "rstd", "nb"):
        scr[nm] = SB1("scr_" + nm, [128, 4], F32)
    ckv = SB1("ckv", [128, 4 * 512], F32)
    ckvn = SB1("ckvn", [128, 4 * 512], BF16)
    ksb = SB1("ksb", [128, 2 * 512], BF16)
    vsb = SB1("vsb", [128, 2 * 512], BF16)
    p1a = ExitStack()
    SB1a = lambda name, shape, dt: p1a.enter_context(nc.sbuf_tensor(name, shape, dt))
    kvdw = SB1a("kvdw", [128, 16 * 512], BF16)
    vbrow = SB1a("vbrow", [1, 2048], F32)
    kwkw = SB1a("kwkw", [128, 4 * 1024], BF16)
    kwvw = SB1a("kwvw", [128, 4 * 2048], BF16)
    kropew = SB1a("kropew", [128, 16 * 1024], BF16)
    ptr = [PS1("ptr%d" % i, [128, 128], BF16) for i in range(4)]
    psA = PS1("psA", [128, 512])
    psB = PS1("psB", [128, 512])
    psS = PS1("psS", [1, 512])
    psBC = PS1("psBC", [128, 512])

    for nm, dst, na in [("kvd_w", kvdw, 16), ("kwk_w", kwkw, 4),
                        ("kwv_w", kwvw, 4), ("krope_w", kropew, 16)]:
        p.dma("gpsimd", lambda e, nm=nm, dst=dst, na=na: e.dma_start(
            out=dst[:].rearrange("p (a o) -> p a o", a=na),
            in_=dram[nm].rearrange("(a p) o -> p a o", p=128)), w=nm)
    p.dma("sync", lambda e: e.dma_start(out=vbrow[:], in_=dram["vb_row"]), w="vbrow")
    p.barrier()

    def rms_apply(nsub, src, dst, wvec, eps, srckey, dstkey):
        for sg in range(nsub):
            p.do("scalar", lambda e, sg=sg: e.activation(
                junk[:, 0:512], src[:, sg * 512:(sg + 1) * 512], AF.Square),
                r=srckey, w="rms.junk")
            p.group("tensor", [lambda e, sg=sg: e.matmul(
                psS[:], ones_col[:], junk[:, 0:512],
                start=(sg == 0), stop=(sg == nsub - 1))],
                r="rms.junk", w="psS" if sg == nsub - 1 else "psS.partial")
        p.do("vector", lambda e: e.tensor_scalar(
            row_a[:], psS[:], 1.0 / (nsub * 128), None, ALU.mult),
            r=("psS", "psS.partial"), w="row_a")
        margin("vector")
        p.do("scalar", lambda e, eps=eps: e.activation(
            row_b[:], row_a[:], AF.Ln,
            bias=(eps_ln if eps == EPS_LN else eps_rms)[0:1, :]),
            r="row_a", w="row_b")
        p.do("scalar", lambda e: e.activation(row_a[:], row_b[:], AF.Exp, scale=-0.5),
             r="row_b", w="row_a")
        margin("scalar")
        p.group("tensor", [lambda e: e.matmul(
            psBC[:], ones_row[:], row_a[:], start=True, stop=True)],
            r="row_a", w="psBC")
        for sg in range(nsub):
            p.do("vector", lambda e, sg=sg: e.scalar_tensor_tensor(
                dst[:, sg * 512:(sg + 1) * 512], src[:, sg * 512:(sg + 1) * 512],
                wvec[:, sg:sg + 1], psBC[:], ALU.mult, ALU.mult),
                r=(srckey, "psBC"), w=dstkey)

    def dump_ln0T(tag, ch):
        pass

    def p1_kv_chunk(ch):
        ln0T = ln0T2_[ch % 2]
        lnk = "ln0T%d" % (ch % 2)
        p.dma("sync", lambda e, ch=ch: e.dma_start(
            out=xsb[:].rearrange("p (a f) -> p a f", a=4),
            in_=dram["x_all"][ch * 512:(ch + 1) * 512, :]
            .rearrange("(a p) f -> p a f", p=128)), w="xsb")
        if p1_stages < 2: return
        tokmajor_ln(xsb, 4, lnsb, scr, "xsb", "lnsb")
        if p1_stages < 3: return
        transpose_to_fm(lnsb, 4, ln0T, ptr, 512, "lnsb", lnk)
        dump_ln0T("after_tr", ch)
        if p1_stages < 4: return
        def mm_ckv(dg, ps, pskey):
            p.group("tensor", [lambda e, ps=ps, dg=dg, fg=fg: e.matmul(
                ps[:], kvdw[:, fg * 512 + dg * 128: fg * 512 + (dg + 1) * 128],
                ln0T[:, fg * 512:(fg + 1) * 512],
                start=(fg == 0), stop=(fg == G - 1)) for fg in range(G)],
                r=(lnk, "kvd_w"), w=pskey)
        def ev_ckv(dg, ps, pskey):
            p.do("scalar", lambda e, ps=ps, dg=dg: e.activation(
                ckv[:, dg * 512:(dg + 1) * 512], ps[:], AF.Identity,
                bias=vecs["kvd_b"][:, dg:dg + 1]), r=pskey, w="ckv")
        mm_rot(4, mm_ckv, ev_ckv)
        dump_ln0T("after_ckv", ch)
        if p1_stages < 5: return
        rms_apply(4, ckv, ckvn, vecs["kvnorm_w"], EPS_RMS, "ckv", "ckvn")
        dump_ln0T("after_rms", ch)
        if p1_stages < 6: return
        def mm_kn(ng, ps, pskey):
            p.group("tensor", [lambda e, ps=ps, ng=ng, dg=dg: e.matmul(
                ps[:], kwkw[:, dg * 1024 + ng * 128: dg * 1024 + (ng + 1) * 128],
                ckvn[:, dg * 512:(dg + 1) * 512],
                start=(dg == 0), stop=(dg == 3)) for dg in range(4)],
                r=("ckvn", "kwk_w"), w=pskey)
        def ev_kn(ng, ps, pskey):
            slot = "ksb%d" % (ng % 2)
            buf = ksb[:, (ng % 2) * 512:(ng % 2 + 1) * 512]
            p.do("scalar", lambda e, ps=ps, ng=ng, buf=buf: e.activation(
                buf, ps[:], AF.Identity, bias=vecs["knope_b"][:, ng:ng + 1]),
                r=pskey, w=slot)
            p.dma("scalar", lambda e, ng=ng, buf=buf, ch=ch: e.dma_start(
                out=kdN[ng * 128:(ng + 1) * 128, ch * 512:(ch + 1) * 512], in_=buf),
                r=slot, w="kdN")
        mm_rot(8, mm_kn, ev_kn)
        dump_ln0T("after_nope", ch)
        if p1_stages < 7: return
        def mm_kr(rg, ps, pskey):
            p.group("tensor", [lambda e, ps=ps, rg=rg, fg=fg: e.matmul(
                ps[:], kropew[:, fg * 1024 + rg * 128: fg * 1024 + (rg + 1) * 128],
                ln0T[:, fg * 512:(fg + 1) * 512],
                start=(fg == 0), stop=(fg == G - 1)) for fg in range(G)],
                r=(lnk, "krope_w"), w=pskey)
        def ev_kr(rg, ps, pskey):
            slot = "ksb%d" % (rg % 2)
            buf = ksb[:, (rg % 2) * 512:(rg % 2 + 1) * 512]
            p.do("scalar", lambda e, ps=ps, rg=rg, buf=buf: e.activation(
                buf, ps[:], AF.Identity, bias=vecs["krope_b"][:, rg:rg + 1]),
                r=pskey, w=slot)
            p.dma("scalar", lambda e, rg=rg, buf=buf, ch=ch: e.dma_start(
                out=kdR[rg * 128:(rg + 1) * 128, ch * 512:(ch + 1) * 512], in_=buf),
                r=slot, w="kdR")
        mm_rot(8, mm_kr, ev_kr)
        dump_ln0T("after_rope", ch)
        if p1_stages < 8: return
        def mm_v(k, ps, pskey):
            tt, vc = k // 4, k % 4
            blds = [lambda e, ps=ps, tt=tt, vc=vc, dg=dg: e.matmul(
                ps[:], ckvn[:, dg * 512 + tt * 128: dg * 512 + (tt + 1) * 128],
                kwvw[:, dg * 2048 + vc * 512: dg * 2048 + (vc + 1) * 512],
                start=(dg == 0), stop=(dg == 3)) for dg in range(4)]
            blds.append(lambda e, ps=ps, vc=vc: e.matmul(
                ps[:], ones_row[:], vbrow[:, vc * 512:(vc + 1) * 512],
                start=False, stop=True, skip_group_check=True))
            p.group("tensor", blds, r=("ckvn", "kwv_w", "vbrow", "ones_row"), w=pskey)
        def ev_v(k, ps, pskey):
            tt, vc = k // 4, k % 4
            slot = "vsb%d" % (k % 2)
            buf = vsb[:, (k % 2) * 512:(k % 2 + 1) * 512]
            p.do("scalar", lambda e, ps=ps, buf=buf: e.activation(
                buf, ps[:], AF.Copy), r=pskey, w=slot)
            p.dma("scalar", lambda e, buf=buf, ch=ch, tt=tt, vc=vc: e.dma_start(
                out=vd[ch * 512 + tt * 128: ch * 512 + (tt + 1) * 128,
                       vc * 512:(vc + 1) * 512], in_=buf), r=slot, w="vd")
        mm_rot(16, mm_v, ev_v)

    for ch in range(kv_chunks):
        p1_kv_chunk(ch)
    p.barrier()

    p1a.close()
    qdw = SB1("qdw", [128, 16 * 1024], BF16)
    quw = SB1("quw", [128, 8 * 2048], BF16)
    cq = SB1("cq", [128, 8 * 512], F32)
    cqn = SB1("cqn", [128, 8 * 512], BF16)
    for eng in ("gpsimd",):
        for sm in ("TE", "SE", "VE"):
            p.wait(eng, sm)
    for nm, dst, na in [("qd_w", qdw, 16), ("qu_w", quw, 8)]:
        p.dma("gpsimd", lambda e, nm=nm, dst=dst, na=na: e.dma_start(
            out=dst[:].rearrange("p (a o) -> p a o", a=na),
            in_=dram[nm].rearrange("(a p) o -> p a o", p=128)), w=nm)

    def p1_q_chunk(qc):
        ln0T = ln0T2_[qc % 2]
        lnk = "ln0T%d" % (qc % 2)
        p.dma("sync", lambda e, qc=qc: e.dma_start(
            out=xsb[:].rearrange("p (a f) -> p a f", a=4),
            in_=dram["x_own"][qc * 512:(qc + 1) * 512, :]
            .rearrange("(a p) f -> p a f", p=128)), w="xsb")
        tokmajor_ln(xsb, 4, lnsb, scr, "xsb", "lnsb")
        transpose_to_fm(lnsb, 4, ln0T, ptr, 512, "lnsb", lnk)
        def mm_cq(qg, ps, pskey):
            p.group("tensor", [lambda e, ps=ps, qg=qg, fg=fg: e.matmul(
                ps[:], qdw[:, fg * 1024 + qg * 128: fg * 1024 + (qg + 1) * 128],
                ln0T[:, fg * 512:(fg + 1) * 512],
                start=(fg == 0), stop=(fg == G - 1)) for fg in range(G)],
                r=(lnk, "qd_w"), w=pskey)
        def ev_cq(qg, ps, pskey):
            p.do("scalar", lambda e, ps=ps, qg=qg: e.activation(
                cq[:, qg * 512:(qg + 1) * 512], ps[:], AF.Identity,
                bias=vecs["qd_b"][:, qg:qg + 1]), r=pskey, w="cq")
        mm_rot(8, mm_cq, ev_cq)
        rms_apply(8, cq, cqn, vecs["qnorm_w"], EPS_RMS, "cq", "cqn")
        def mm_q(og, ps, pskey):
            p.group("tensor", [lambda e, ps=ps, og=og, qg=qg: e.matmul(
                ps[:], quw[:, qg * 2048 + og * 128: qg * 2048 + (og + 1) * 128],
                cqn[:, qg * 512:(qg + 1) * 512],
                start=(qg == 0), stop=(qg == 7)) for qg in range(8)],
                r=("cqn", "qu_w"), w=pskey)
        def ev_q(og, ps, pskey):
            slot = "ksb%d" % (og % 2)
            buf = ksb[:, (og % 2) * 512:(og % 2 + 1) * 512]
            p.do("scalar", lambda e, ps=ps, og=og, buf=buf: e.activation(
                buf, ps[:], AF.Identity, bias=vecs["qu_b"][:, og:og + 1]),
                r=pskey, w=slot)
            p.dma("scalar", lambda e, og=og, buf=buf, qc=qc: e.dma_start(
                out=qd[og * 128:(og + 1) * 128, qc * 512:(qc + 1) * 512], in_=buf),
                r=slot, w="qd")
        mm_rot(16, mm_q, ev_q)

    for qc in range(q_chunks):
        p1_q_chunk(qc)

    p.barrier()
    p1.close()

    if do_p2:
        p2 = ExitStack()
        SB2 = lambda name, shape, dt: p2.enter_context(nc.sbuf_tensor(name, shape, dt))
        PS2 = lambda name, shape, dt=F32: p2.enter_context(nc.psum_tensor(name, shape, dt))
        qsb = SB2("qsb", [128, G * 512], BF16)
        kb = [SB2("kb%d" % j, [128, 4 * 512], BF16) for j in range(2)]
        vb = [SB2("vb%d" % j, [128, 4 * 512], BF16) for j in range(2)]
        pb = [SB2("pb%d" % j, [128, 512], BF16) for j in range(2)]
        attnbf = SB2("attnbf", [128, G * 512], BF16)
        msb = SB2("msb", [128, 8 * 512], BF16)
        xosb = SB2("xosb", [128, 4 * 2048], F32)
        owsb = SB2("owsb", [128, G * 2048], BF16)
        otmp = SB2("otmp", [128, 512], F32)
        drow = SB2("drow", [1, 512], F32)
        r1sb = SB2("r1sb", [128, 2 * 512], F32)
        psS2 = [PS2("p2S%d" % j, [128, 512]) for j in range(2)]
        psO = [PS2("p2O%d" % j, [128, 512]) for j in range(4)]
        psD = [PS2("p2D%d" % j, [33, 512]) for j in range(2)]

        p.dma("gpsimd", lambda e: e.dma_start(
            out=owsb[:].rearrange("p (a o) -> p a o", a=G),
            in_=dram["ow_w"].rearrange("(a p) o -> p a o", p=128)), w="owsb")

        def p2_block(i):
            nkc = 2 * (i + 1)
            p.dma("sync", lambda e, i=i: e.dma_start(
                out=qsb[:].rearrange("p (g t) -> p g t", g=G),
                in_=qd[:, i * 512:(i + 1) * 512].rearrange("(g p) t -> p g t", p=128)),
                r="qd", w="qsb")
            p.dma("sync", lambda e, i=i: e.dma_start(
                out=msb[:].rearrange("p (s t) -> p s t", s=8),
                in_=dram["mmask"][i].rearrange("s p t -> p s t")), w="msb")
            p.dma("sync", lambda e, i=i: e.dma_start(
                out=xosb[:].rearrange("p (a f) -> p a f", a=4),
                in_=dram["x_own"][i * 512:(i + 1) * 512, :]
                .rearrange("(a p) f -> p a f", p=128)), w="xosb")
            for hp in range(4):
                for kc in range(nkc):
                    j = kc % 2
                    p.dma("sync", lambda e, hp=hp, kc=kc, j=j: e.dma_start(
                        out=kb[j][0:64, :].rearrange("p (h t) -> p h t", h=4),
                        in_=kdN[hp * 256:(hp + 1) * 256, kc * 512:(kc + 1) * 512]
                        .rearrange("(h p) t -> p h t", p=64)),
                        r="kdN", w="kb%d" % j)
                    p.dma("sync", lambda e, hp=hp, kc=kc, j=j: e.dma_start(
                        out=kb[j][64:128, :].rearrange("p (h t) -> p h t", h=4),
                        in_=kdR[hp * 256:(hp + 1) * 256, kc * 512:(kc + 1) * 512]
                        .rearrange("(h p) t -> p h t", p=64)),
                        r="kdR", w="kb%d" % j)
                    p.dma("sync", lambda e, i=i, hp=hp, kc=kc, j=j: e.dma_start(
                        out=vb[j][:].rearrange("p (a v) -> p a v", a=4),
                        in_=vd[kc * 512:(kc + 1) * 512, hp * 512:(hp + 1) * 512]
                        .rearrange("(a p) v -> p a v", p=128)),
                        r="vd", w="vb%d" % j)
                    masked = kc >= nkc - 2

                    def emit_S(hi, ks):
                        h = hp * 4 + hi
                        sj = (hi * 4 + ks) % 2
                        pj = sj
                        skey, pkey = "p2S%d" % sj, "pb%d" % pj
                        blds = [lambda e, j=j, hi=hi, ks=ks, h=h, sj=sj: e.matmul(
                            psS2[sj][:],
                            kb[j][:, hi * 512 + ks * 128: hi * 512 + (ks + 1) * 128],
                            qsb[:, h * 512:(h + 1) * 512],
                            start=True, stop=True)]
                        if masked:
                            sidx = (kc - (nkc - 2)) * 4 + ks
                            blds.append(lambda e, sj=sj, sidx=sidx: e.matmul(
                                psS2[sj][:], identb[:],
                                msb[:, sidx * 512:(sidx + 1) * 512],
                                start=False, stop=True, skip_group_check=True))
                        p.group("tensor", blds,
                                r=("kb%d" % j, "qsb", "msb", "identb"), w=skey)
                        p.do("scalar", lambda e, sj=sj, pj=pj: e.activation(
                            pb[pj][:], psS2[sj][:], AF.Exp, scale=ISQ),
                            r=skey, w=pkey)

                    def emit_AVD(hi, ks):
                        pj = (hi * 4 + ks) % 2
                        pkey = "pb%d" % pj
                        first = (kc == 0 and ks == 0)
                        last = (kc == nkc - 1 and ks == 3)
                        p.group("tensor", [lambda e, j=j, hi=hi, ks=ks, pj=pj,
                                           first=first, last=last: e.matmul(
                            psO[hi][:],
                            vb[j][:, ks * 512 + hi * 128: ks * 512 + (hi + 1) * 128],
                            pb[pj][:], start=first, stop=last)],
                            r=("vb%d" % j, pkey), w="p2O%d" % hi)
                        p.group("tensor", [lambda e, hi=hi, pj=pj,
                                           first=first, last=last: e.matmul(
                            psD[hi // 2][(hi % 2) * 32:(hi % 2) * 32 + 1, :],
                            ones_colb[:], pb[pj][:], start=first, stop=last)],
                            r=("ones_colb", pkey), w="p2D%d" % hi)

                    prev = None
                    for hi in range(4):
                        for ks in range(4):
                            emit_S(hi, ks)
                            if prev is not None:
                                emit_AVD(*prev)
                            prev = (hi, ks)
                    emit_AVD(*prev)
                for hi in range(4):
                    h = hp * 4 + hi
                    p.do("scalar", lambda e, hi=hi: e.activation(
                        drow[:], psD[hi // 2][(hi % 2) * 32:(hi % 2) * 32 + 1, :],
                        AF.Ln), r="p2D%d" % hi, w="drow")
                    p.do("scalar", lambda e: e.activation(
                        drow[:], drow[:], AF.Exp, scale=-1.0), r="drow", w="drow")
                    margin("scalar")
                    p.group("tensor", [lambda e: e.matmul(
                        psS2[0][:], ones_row[:], drow[:], start=True, stop=True)],
                        r=("ones_row", "drow"), w="p2S0")
                    p.do("scalar", lambda e, hi=hi: e.activation(
                        otmp[:], psO[hi][:], AF.Copy), r="p2O%d" % hi, w="otmp")
                    p.do("vector", lambda e, h=h: e.tensor_tensor(
                        attnbf[:, h * 512:(h + 1) * 512], otmp[:], psS2[0][:],
                        ALU.mult), r=("otmp", "p2S0"), w="attnbf")
            for og in range(G):
                blds = [lambda e, og=og, fg=fg: e.matmul(
                    psO[og % 2][:],
                    owsb[:, fg * 2048 + og * 128: fg * 2048 + (og + 1) * 128],
                    attnbf[:, fg * 512:(fg + 1) * 512],
                    start=(fg == 0), stop=(fg == G - 1)) for fg in range(G)]
                for tsub in range(4):
                    blds.append(lambda e, og=og, tsub=tsub: e.matmul(
                        psO[og % 2][:, tsub * 128:(tsub + 1) * 128],
                        xosb[:, tsub * 2048 + og * 128: tsub * 2048 + (og + 1) * 128],
                        identf[:], start=False, stop=True,
                        skip_group_check=True))
                p.group("tensor", blds, r=("owsb", "attnbf", "xosb", "identf"),
                        w="p2O%d" % (og % 2))
                slot = "r1sb%d" % (og % 2)
                buf = r1sb[:, (og % 2) * 512:(og % 2 + 1) * 512]
                p.do("scalar", lambda e, og=og, buf=buf: e.activation(
                    buf, psO[og % 2][:], AF.Identity, bias=vecs["ow_b"][:, og:og + 1]),
                    r="p2O%d" % (og % 2), w=slot)
                p.dma("scalar", lambda e, og=og, buf=buf, i=i: e.dma_start(
                    out=resd[og * 128:(og + 1) * 128, i * 512:(i + 1) * 512], in_=buf),
                    r=slot, w="resd%d.%d" % (og, i))

        for i in range(4):
            p2_block(i)
        p.barrier()
        if "resid1" in dbg:
            p.dma("sync", lambda e: e.dma_start(out=dbg["resid1"], in_=resd),
                  r="resd", w="dbg_resid1")
            p.barrier()
        p2.close()

    p.barrier()

    dhalf = nc.dram_tensor("dhalf", [2048, 1024], F32, kind="Internal").ap()

    def emit_ffn_phase(final):
        pf = ExitStack()
        sfx = "_f1" if final else "_f0"
        SBF = lambda name, shape, dt: pf.enter_context(
            nc.sbuf_tensor(name + sfx, shape, dt))
        PSF = lambda name, shape, dt=F32: pf.enter_context(
            nc.psum_tensor(name + sfx, shape, dt))
        xlnf = SBF("xlnf", [128, G * 1024], BF16)
        fusef = SBF("fusef", [128, 32 * 1024], BF16)
        wsec = SBF("wsec", [128, G * 256], BF16)
        wsec2 = SBF("wsec2", [128, G * 256], BF16)
        wsec3 = SBF("wsec3", [128, G * 256], BF16)
        wsec4 = SBF("wsec4", [128, G * 256], BF16)
        wdn = SBF("wdn", [128, 8192], BF16)
        wdn2 = SBF("wdn2", [128, 8192], BF16)
        rx = [SBF("rx%d" % j, [128, 1024], F32) for j in range(2)]
        junk2 = SBF("junk2", [128, 1024], F32)
        xtmp = SBF("xtmp", [128, 512], F32)
        gbf = SBF("gbf", [128, 512], BF16)
        ubf = SBF("ubf", [128, 512], BF16)
        r2s = SBF("r2s", [128, 512], F32)
        prowt = SBF("prowt", [1, 1536], F32)
        prow = {"mu": prowt[0:1, 0:512], "rs": prowt[0:1, 512:1024],
                "t": prowt[0:1, 1024:1536]}
        psF = [PSF("pf%d" % j, [128, 512]) for j in range(8)]

        def ffn_group(tg):
            t0c = tg * 1024
            for g in range(G):
                j = g % 2
                p.dma("sync", lambda e, g=g, t0c=t0c, j=j: e.dma_start(
                    out=rx[j][:], in_=resd[g * 128:(g + 1) * 128, t0c:t0c + 1024]),
                    r=("resd%d.%d" % (g, t0c // 512), "resd%d.%d" % (g, t0c // 512 + 1)),
                    w="rx%d" % j)
                for sub in range(2):
                    p.group("tensor", [lambda e, j=j, sub=sub, g=g: e.matmul(
                        psF[sub][0:1, :], ones_col[:],
                        rx[j][:, sub * 512:(sub + 1) * 512],
                        start=(g == 0), stop=(g == G - 1))],
                        r=("rx%d" % j, "ones_col"), w="pf%d" % sub)
                p.do("scalar", lambda e, j=j: e.activation(
                    junk2[:], rx[j][:], AF.Square), r="rx%d" % j, w="junk2")
                for sub in range(2):
                    p.group("tensor", [lambda e, sub=sub, g=g: e.matmul(
                        psF[2 + sub][0:1, :], ones_col[:],
                        junk2[:, sub * 512:(sub + 1) * 512],
                        start=(g == 0), stop=(g == G - 1))],
                        r=("junk2", "ones_col"), w="pf%d" % (2 + sub))
            for sub in range(2):
                mu, rs, t_ = prow["mu"], prow["rs"], prow["t"]
                p.do("vector", lambda e, mu=mu, sub=sub: e.tensor_scalar(
                    mu, psF[sub][0:1, :], 1.0 / D, None, ALU.mult),
                    r="pf%d" % sub, w="f.mu")
                p.do("vector", lambda e, t_=t_, sub=sub: e.tensor_scalar(
                    t_, psF[2 + sub][0:1, :], 1.0 / D, None, ALU.mult),
                    r="pf%d" % (2 + sub), w="f.t")
                p.do("vector", lambda e, mu=mu: e.tensor_tensor(
                    junk2[0:1, 0:512], mu, mu, ALU.mult),
                    r="f.mu", w="junk2")
                p.do("vector", lambda e, t_=t_: e.tensor_tensor(
                    t_, t_, junk2[0:1, 0:512], ALU.subtract),
                    r=("f.t", "junk2"), w="f.t")
                margin("vector")
                p.do("scalar", lambda e, t_=t_: e.activation(
                    t_, t_, AF.Ln, bias=eps_ln[0:1, :]),
                    r="f.t", w="f.t")
                p.do("scalar", lambda e, rs=rs, t_=t_: e.activation(
                    rs, t_, AF.Exp, scale=-0.5), r="f.t", w="f.rs")
                margin("scalar")
                p.group("tensor", [lambda e, mu=mu: e.matmul(
                    psF[4][:], ones_row[:], mu, start=True, stop=True)],
                    r=("ones_row", "f.mu"), w="pf4")
                p.group("tensor", [lambda e, rs=rs: e.matmul(
                    psF[5][:], ones_row[:], rs, start=True, stop=True)],
                    r=("ones_row", "f.rs"), w="pf5")
                for g in range(G):
                    j = g % 2
                    p.dma("sync", lambda e, g=g, t0c=t0c, j=j, sub=sub: e.dma_start(
                        out=rx[j][:, 0:512],
                        in_=resd[g * 128:(g + 1) * 128,
                                 t0c + sub * 512:t0c + (sub + 1) * 512]),
                        r="resd%d.%d" % (g, t0c // 512 + sub), w="rx%d" % j)
                    p.do("vector", lambda e, j=j: e.tensor_tensor(
                        xtmp[:], rx[j][:, 0:512], psF[4][:], ALU.subtract),
                        r=("rx%d" % j, "pf4"), w="xtmp")
                    p.do("vector", lambda e, g=g, sub=sub: e.tensor_tensor(
                        xlnf[:, g * 1024 + sub * 512: g * 1024 + (sub + 1) * 512],
                        xtmp[:], psF[5][:], ALU.mult),
                        r=("xtmp", "pf5"), w="xlnf%d" % sub)
            for half in range(2):
                for ws in range(16):
                    c0 = half * 4096 + ws * 256
                    jj = ws % 2
                    wg = wsec if jj == 0 else wsec2
                    wu = wsec3 if jj == 0 else wsec4
                    gk, uk = "wsec" if jj == 0 else "wsec2", "wsec3" if jj == 0 else "wsec4"
                    p.dma("gpsimd", lambda e, c0=c0, wg=wg: e.dma_start(
                        out=wg[:].rearrange("p (a o) -> p a o", a=G),
                        in_=dram["fg_w"][:, c0:c0 + 256]
                        .rearrange("(a p) o -> p a o", p=128)), w=gk)
                    p.dma("gpsimd", lambda e, c0=c0, wu=wu: e.dma_start(
                        out=wu[:].rearrange("p (a o) -> p a o", a=G),
                        in_=dram["fu_w"][:, c0:c0 + 256]
                        .rearrange("(a p) o -> p a o", p=128)), w=uk)
                    for igl in range(2):
                        ig = ws * 2 + igl
                        iabs = half * 32 + ig
                        for sub in range(2):
                            p.group("tensor", [lambda e, igl=igl, fg=fg, sub=sub,
                                               wg=wg: e.matmul(
                                psF[6][:],
                                wg[:, fg * 256 + igl * 128: fg * 256 + (igl + 1) * 128],
                                xlnf[:, fg * 1024 + sub * 512: fg * 1024 + (sub + 1) * 512],
                                start=(fg == 0), stop=(fg == G - 1)) for fg in range(G)],
                                r=(gk, "xlnf%d" % sub), w="pf6")
                            p.do("scalar", lambda e, sub=sub, iabs=iabs: e.activation(
                                gbf[:], psF[6][:], AF.Silu,
                                scale=vecs["sgate"][:, iabs:iabs + 1]),
                                r="pf6", w="gbf")
                            p.group("tensor", [lambda e, igl=igl, fg=fg, sub=sub,
                                               wu=wu: e.matmul(
                                psF[7][:],
                                wu[:, fg * 256 + igl * 128: fg * 256 + (igl + 1) * 128],
                                xlnf[:, fg * 1024 + sub * 512: fg * 1024 + (sub + 1) * 512],
                                start=(fg == 0), stop=(fg == G - 1)) for fg in range(G)],
                                r=(uk, "xlnf%d" % sub), w="pf7")
                            p.do("scalar", lambda e, sub=sub, iabs=iabs: e.activation(
                                ubf[:], psF[7][:], AF.Identity,
                                scale=vecs["sup"][:, iabs:iabs + 1]),
                                r="pf7", w="ubf")
                            p.do("vector", lambda e, ig=ig, sub=sub: e.tensor_tensor(
                                fusef[:, ig * 1024 + sub * 512: ig * 1024 + (sub + 1) * 512],
                                gbf[:], ubf[:], ALU.mult),
                                r=("gbf", "ubf"), w="fusef")
                for osec in range(8):
                    oc0 = osec * 256
                    wkey = "wdn" if osec % 2 == 0 else "wdn2"
                    wbuf = wdn if osec % 2 == 0 else wdn2
                    p.dma("gpsimd", lambda e, half=half, oc0=oc0, wbuf=wbuf: e.dma_start(
                        out=wbuf[:].rearrange("p (a o) -> p a o", a=32),
                        in_=dram["fd_w"][half * 4096:(half + 1) * 4096, oc0:oc0 + 256]
                        .rearrange("(a p) o -> p a o", p=128)), w=wkey)
                    for ogl in range(2):
                        og = osec * 2 + ogl
                        for sub in range(2):
                            pd = "pf%d" % (sub % 2)
                            p.group("tensor", [lambda e, ogl=ogl, ig=ig, sub=sub,
                                               wbuf=wbuf: e.matmul(
                                psF[sub % 2][:],
                                wbuf[:, ig * 256 + ogl * 128: ig * 256 + (ogl + 1) * 128],
                                fusef[:, ig * 1024 + sub * 512: ig * 1024 + (sub + 1) * 512],
                                start=(ig == 0), stop=(ig == 31)) for ig in range(32)],
                                r=(wkey, "fusef"), w=pd)
                            if half == 0:
                                p.do("vector", lambda e, sub=sub: e.tensor_scalar(
                                    r2s[:], psF[sub % 2][:],
                                    1.0, None, ALU.mult), r=pd, w="r2s")
                                p.dma("scalar", lambda e, og=og, sub=sub: e.dma_start(
                                    out=dhalf[og * 128:(og + 1) * 128,
                                              sub * 512:(sub + 1) * 512],
                                    in_=r2s[:]),
                                    r="r2s", w="dhalf%d" % og)
                            else:
                                p.dma("sync", lambda e, og=og, sub=sub: e.dma_start(
                                    out=xtmp[:],
                                    in_=dhalf[og * 128:(og + 1) * 128,
                                              sub * 512:(sub + 1) * 512]),
                                    r="dhalf%d" % og, w="xtmp")
                                p.do("vector", lambda e, sub=sub: e.tensor_tensor(
                                    xtmp[:], xtmp[:], psF[sub % 2][:], ALU.add),
                                    r=("xtmp", pd), w="xtmp")
                                p.dma("sync", lambda e, og=og, sub=sub, tg=tg: e.dma_start(
                                    out=rx[sub][:, 0:512],
                                    in_=resd[og * 128:(og + 1) * 128,
                                             tg * 1024 + sub * 512:
                                             tg * 1024 + (sub + 1) * 512]),
                                    r="resd%d.%d" % (og, 2 * tg + sub), w="rx%d" % sub)
                                p.do("vector", lambda e, og=og, sub=sub: e.scalar_tensor_tensor(
                                    r2s[:], xtmp[:],
                                    vecs["sdown"][:, og:og + 1], rx[sub][:, 0:512],
                                    ALU.mult, ALU.add),
                                    r=("xtmp", "rx%d" % sub), w="r2s")
                                p.dma("scalar", lambda e, og=og, sub=sub, tg=tg:
                                      e.dma_start(
                                    out=resd[og * 128:(og + 1) * 128,
                                             tg * 1024 + sub * 512:
                                             tg * 1024 + (sub + 1) * 512],
                                    in_=r2s[:]),
                                    r="r2s", w="resd%d.%d" % (og, 2 * tg + sub))

        for tg in range(2):
            ffn_group(tg)
        p.barrier()
        pf.close()

    if do_p3:
        emit_ffn_phase(final=False)
        if "resid2" in dbg:
            p.dma("sync", lambda e: e.dma_start(out=dbg["resid2"], in_=resd),
                  r="resd0.0", w="dbg_resid2")
            p.barrier()

    if do_p4:
        p4 = ExitStack()
        SB4 = lambda name, shape, dt: p4.enter_context(nc.sbuf_tensor(name, shape, dt))
        PS4 = lambda name, shape, dt=F32: p4.enter_context(nc.psum_tensor(name, shape, dt))
        xin4 = SB4("xin4", [128, G * 512], BF16)
        xmix = SB4("xmix", [128, G * 512], BF16)
        powt = SB4("powt", [128, G * 512], F32)
        abuf = SB4("abuf", [128, G * 512], F32)
        pbuf = SB4("pbuf", [128, G * 512], F32)
        rbuf = SB4("rbuf", [128, G * 512], BF16)
        ltmp = SB4("ltmp", [128, 512], F32)
        yN = SB4("yN", [128, 513], F32)
        yD = SB4("yD", [128, 513], F32)
        w4 = [SB4("w4_%d" % j, [128, G * 256], BF16) for j in range(2)]
        rx4 = [SB4("rx4_%d" % j, [128, 512], F32) for j in range(2)]
        r3s = SB4("r3s", [128, 2 * 512], F32)
        junk4 = SB4("junk4", [128, 512], F32)
        e1sb = SB4("e1sb", [128, 64], F32)
        e1pick = SB4("e1pick", [128, 64], F32)
        e2sb = SB4("e2sb", [128, 32], F32)
        e2all = SB4("e2all", [128, 256], F32)
        sce = SB4("sce", [128, 32], F32)
        small = SB4("small", [128, 16 * 16], F32)
        SM = {nm: small[:, i * 16:(i + 1) * 16] for i, nm in enumerate(
            ("dcol", "wcol", "onemk", "onemv", "onemr", "d512", "carryN",
             "carryD", "prevcol", "sxc", "d1024a", "d1024b", "d512a",
             "d512b", "tmp1", "tmp2"))}
        gst = SB4("gst", [128, 4 * 32], F32)
        GQ = {nm: gst[:, i * 32:(i + 1) * 32] for i, nm in enumerate(
            ("Gcur", "Gprev", "Gd", "Gam"))}
        prow4 = SB4("prow4", [1, 1536], F32)
        pr4 = {"mu0": prow4[0:1, 0:512], "rs0": prow4[0:1, 512:1024],
               "t0": prow4[0:1, 1024:1536]}
        ps4 = [PS4("p4%d" % j, [128, 512]) for j in range(6)]

        p.do("vector", lambda e: e.memset(small[:], 0.0),
             w=tuple("sm." + nm for nm in (
                 "dcol", "wcol", "onemk", "onemv", "onemr", "d512", "carryN",
                 "carryD", "prevcol", "sxc", "d1024a", "d1024b", "d512a",
                 "d512b", "tmp1", "tmp2")))
        p.do("scalar", lambda e: e.activation(SM["dcol"], vecs["td"][:], AF.Exp),
             r="v_td", w="sm.dcol")
        p.do("scalar", lambda e: e.activation(SM["dcol"], SM["dcol"], AF.Exp,
             scale=-1.0), r="sm.dcol", w="sm.dcol")
        p.do("scalar", lambda e: e.activation(SM["wcol"], vecs["cw"][:], AF.Exp),
             r="v_cw", w="sm.wcol")
        for nm, src in (("onemk", "mixk"), ("onemv", "mixv"), ("onemr", "mixr")):
            p.do("vector", lambda e, nm=nm, src=src: e.tensor_scalar(
                SM[nm], vecs[src][:], -1.0, 1.0, ALU.mult, ALU.add),
                r="v_" + src, w="sm." + nm)
        p.do("vector", lambda e: e.memset(junk4[:], 0.0), w="junk4")
        p.do("vector", lambda e: e.memset(junk4[:, 0:1], 1.0), r="junk4", w="junk4")
        for g in range(G):
            p.do("vector", lambda e, g=g: e.tensor_tensor_scan(
                powt[:, g * 512:(g + 1) * 512],
                SM["dcol"][:, g:g + 1].broadcast_to([128, 512]),
                junk4[:], 0.0, ALU.mult, ALU.add),
                r=("sm.dcol", "junk4"), w="powt")
        for g in range(G):
            p.do("vector", lambda e, g=g: e.tensor_scalar(
                SM["d512"][:, g:g + 1], powt[:, g * 512 + 511:g * 512 + 512],
                SM["dcol"][:, g:g + 1], None, ALU.mult),
                r=("powt", "sm.dcol"), w="sm.d512")
        p.do("vector", lambda e: e.tensor_copy(SM["d512a"], SM["d512"]),
             r="sm.d512", w="sm.d512a")
        p.do("vector", lambda e: e.tensor_copy(SM["d512b"], SM["d512"]),
             r="sm.d512", w="sm.d512b")
        p.do("vector", lambda e: e.tensor_tensor(
            SM["d1024a"], SM["d512"], SM["d512"], ALU.mult),
            r="sm.d512", w="sm.d1024a")
        p.do("vector", lambda e: e.tensor_copy(SM["d1024b"], SM["d1024a"]),
             r="sm.d1024a", w="sm.d1024b")
        p.do("vector", lambda e: e.memset(gst[:], 0.0),
             w=("g.Gcur", "g.Gprev", "g.Gd", "g.Gam"))

        def p4_prep(cI):
            t0c = cI * 512
            sb0, sb1 = (0, 1) if cI % 2 == 0 else (4, 5)
            for g in range(G):
                j = g % 2
                p.dma("sync", lambda e, g=g, t0c=t0c, j=j: e.dma_start(
                    out=rx4[j][:], in_=resd[g * 128:(g + 1) * 128, t0c:t0c + 512]),
                    r="resd%d.%d" % (g, t0c // 512), w="rx4_%d" % j)
                p.group("tensor", [lambda e, j=j, g=g, sb0=sb0: e.matmul(
                    ps4[sb0][0:1, :], ones_col[:], rx4[j][:],
                    start=(g == 0), stop=(g == G - 1))],
                    r=("rx4_%d" % j, "ones_col"), w="p4%d" % sb0)
                p.do("scalar", lambda e, j=j: e.activation(
                    junk4[:], rx4[j][:], AF.Square), r="rx4_%d" % j, w="junk4")
                p.group("tensor", [lambda e, g=g, sb1=sb1: e.matmul(
                    ps4[sb1][0:1, :], ones_col[:], junk4[:],
                    start=(g == 0), stop=(g == G - 1))],
                    r=("junk4", "ones_col"), w="p4%d" % sb1)
            p.do("vector", lambda e, sb0=sb0: e.tensor_scalar(
                pr4["mu0"], ps4[sb0][0:1, :], 1.0 / D, None, ALU.mult),
                r="p4%d" % sb0, w="p4.mu")
            p.do("vector", lambda e, sb1=sb1: e.tensor_scalar(
                pr4["t0"], ps4[sb1][0:1, :], 1.0 / D, None, ALU.mult),
                r="p4%d" % sb1, w="p4.t")
            p.do("vector", lambda e: e.tensor_tensor(
                junk4[0:1, :], pr4["mu0"], pr4["mu0"], ALU.mult),
                r="p4.mu", w="junk4")
            p.do("vector", lambda e: e.tensor_tensor(
                pr4["t0"], pr4["t0"], junk4[0:1, :], ALU.subtract),
                r=("p4.t", "junk4"), w="p4.t")
            margin("vector")
            p.do("scalar", lambda e: e.activation(
                pr4["t0"], pr4["t0"], AF.Ln, bias=eps_ln[0:1, :]),
                r="p4.t", w="p4.t")
            p.do("scalar", lambda e: e.activation(
                pr4["rs0"], pr4["t0"], AF.Exp, scale=-0.5), r="p4.t", w="p4.rs")
            margin("scalar")
            p.group("tensor", [lambda e: e.matmul(
                ps4[2][:], ones_row[:], pr4["mu0"], start=True, stop=True)],
                r=("ones_row", "p4.mu"), w="p42")
            p.group("tensor", [lambda e: e.matmul(
                ps4[3][:], ones_row[:], pr4["rs0"], start=True, stop=True)],
                r=("ones_row", "p4.rs"), w="p43")
            for g in range(G):
                j = g % 2
                p.dma("sync", lambda e, g=g, t0c=t0c, j=j: e.dma_start(
                    out=rx4[j][:], in_=resd[g * 128:(g + 1) * 128, t0c:t0c + 512]),
                    r="resd%d.%d" % (g, t0c // 512), w="rx4_%d" % j)
                p.do("vector", lambda e, j=j: e.tensor_tensor(
                    junk4[:], rx4[j][:], ps4[2][:], ALU.subtract),
                    r=("rx4_%d" % j, "p42"), w="junk4")
                p.do("vector", lambda e: e.tensor_tensor(
                    junk4[:], junk4[:], ps4[3][:], ALU.mult),
                    r=("junk4", "p43"), w="junk4")
                p.do("scalar", lambda e, g=g: e.activation(
                    xin4[:, g * 512:(g + 1) * 512], junk4[:], AF.Identity,
                    scale=vecs["lnw"][:, g:g + 1], bias=vecs["lnb"][:, g:g + 1]),
                    r="junk4", w="xin4")
                p.do("vector", lambda e, g=g, cI=cI: e.tensor_copy(
                    e1sb[:, cI * 16 + g: cI * 16 + g + 1],
                    xin4[:, g * 512 + 511: g * 512 + 512]),
                    r="xin4", w="e1sb")
                p.do("vector", lambda e, g=g, j=j: e.tensor_tensor(
                    rx4[j][:], rx4[j][:], xin4[:, g * 512:(g + 1) * 512], ALU.add),
                    r=("rx4_%d" % j, "xin4"), w="rx4_%d" % j)
                p.dma("scalar", lambda e, g=g, t0c=t0c, j=j: e.dma_start(
                    out=resd[g * 128:(g + 1) * 128, t0c:t0c + 512], in_=rx4[j][:]),
                    r="rx4_%d" % j, w="resd%d.%d" % (g, t0c // 512))
            p.dma("scalar", lambda e, cI=cI: e.dma_start(
                out=xind[:, cI * 512:(cI + 1) * 512]
                .rearrange("(g p) t -> p g t", p=128),
                in_=xin4[:].rearrange("p (g t) -> p g t", g=G)),
                r="xin4", w="xind")

        for cI in range(4):
            p4_prep(cI)

        p.dma("gpsimd", lambda e: e.dma_start(out=e1i.ap(), in_=e1sb[:]),
              r="e1sb", w="e1i")
        p.do("gpsimd", lambda e: e.collective_compute(
            "AllGather", ALU.bypass, replica_groups=[list(range(NCORES))],
            ins=[e1i.ap().opt()], outs=[e1o.ap().opt()]),
            inc="CC", r="e1i", w="e1o")
        e1all = abuf[:, 0:512]
        p.dma("gpsimd", lambda e: e.dma_start(
            out=e1all.rearrange("p (r j) -> p r j", r=NCORES),
            in_=e1o.ap().rearrange("(r p) j -> p r j", p=128)),
            r="e1o", w="abuf")
        for cI in range(4):
            p.dma("sync", lambda e, cI=cI: e.dma_start(
                out=junk4[:], in_=dram["sxsel"][cI]), w="junk4")
            p.do("vector", lambda e, cI=cI: e.tensor_tensor(
                ltmp[:, 0:512], e1all, junk4[:], ALU.mult),
                r=("abuf", "junk4"), w="ltmp")
            p.do("vector", lambda e, cI=cI: e.reduce_sum(
                e1pick[:, cI * 16:(cI + 1) * 16],
                ltmp[:, 0:512].rearrange("p (r j g) -> p g j r", r=8, j=4),
                axis=AX.XY), r="ltmp", w="e1pick")

        def p4_front(cI):
            p.dma("sync", lambda e, cI=cI: e.dma_start(
                out=xin4[:].rearrange("p (g t) -> p g t", g=G),
                in_=xind[:, cI * 512:(cI + 1) * 512]
                .rearrange("(g p) t -> p g t", p=128)),
                r="xind", w="xin4")
            p.do("vector", lambda e, cI=cI: e.scalar_tensor_tensor(
                SM["sxc"], SM["prevcol"], vecs["sxown"][:, cI:cI + 1],
                e1pick[:, cI * 16:(cI + 1) * 16], ALU.mult, ALU.add),
                r=("sm.prevcol", "v_sxown", "e1pick"), w="sm.sxc")
            emit_mix(xmix, "xmix", "mixk", "onemk")

        def emit_mix(dstbuf, dstkey, mixv_, onem, extraw=()):
            for g in range(G):
                pk = "%s_p%d" % (dstkey, g % 2)
                wk = (pk,) + tuple(e % g for e in extraw)
                p.do("scalar", lambda e, g=g, mixv_=mixv_, dstbuf=dstbuf: e.activation(
                    dstbuf[:, g * 512 + 1: (g + 1) * 512],
                    xin4[:, g * 512 + 1: (g + 1) * 512],
                    AF.Identity, bias=SM["tmp1"][:, 0:1],
                    scale=vecs[mixv_][:, g:g + 1]),
                    r=("xin4", "v_" + mixv_, "sm.tmp1"), w=wk)
                p.do("vector", lambda e, g=g, onem=onem, dstbuf=dstbuf: e.scalar_tensor_tensor(
                    dstbuf[:, g * 512 + 1: (g + 1) * 512],
                    xin4[:, g * 512: (g + 1) * 512 - 1],
                    SM[onem][:, g:g + 1],
                    dstbuf[:, g * 512 + 1: (g + 1) * 512], ALU.mult, ALU.add),
                    r=("xin4", "sm." + onem, pk), w=pk)
                p.do("scalar", lambda e, g=g, mixv_=mixv_, dstbuf=dstbuf: e.activation(
                    dstbuf[:, g * 512: g * 512 + 1],
                    xin4[:, g * 512: g * 512 + 1],
                    AF.Identity, bias=SM["tmp1"][:, 0:1],
                    scale=vecs[mixv_][:, g:g + 1]),
                    r=("xin4", "v_" + mixv_, "sm.tmp1", pk), w=pk)
                p.do("vector", lambda e, g=g, onem=onem, dstbuf=dstbuf: e.scalar_tensor_tensor(
                    dstbuf[:, g * 512: g * 512 + 1],
                    SM["sxc"][:, g:g + 1], SM[onem][:, g:g + 1],
                    dstbuf[:, g * 512: g * 512 + 1], ALU.mult, ALU.add),
                    r=("sm.sxc", "sm." + onem, pk), w=pk)

        def emit_mats(W, srcbuf, srckey, evac, wb=None, wn="w4_%d",
                      secs=range(8)):
            if wb is None:
                wb = w4
            for osec in secs:
                jj = osec % 2
                p.dma("gpsimd", lambda e, W=W, osec=osec, jj=jj, wb=wb,
                      wn=wn: e.dma_start(
                    out=wb[jj][:].rearrange("p (a o) -> p a o", a=G),
                    in_=dram[W][:, osec * 256:(osec + 1) * 256]
                    .rearrange("(a p) o -> p a o", p=128)), w=wn % jj)
                for ogl in range(2):
                    og = osec * 2 + ogl
                    pj = og % 2
                    p.group("tensor", [lambda e, ogl=ogl, fg=fg, jj=jj, pj=pj,
                                       srcbuf=srcbuf, wb=wb: e.matmul(
                        ps4[pj][:],
                        wb[jj][:, fg * 256 + ogl * 128: fg * 256 + (ogl + 1) * 128],
                        srcbuf[:, fg * 512:(fg + 1) * 512],
                        start=(fg == 0), stop=(fg == G - 1)) for fg in range(G)],
                        r=(wn % jj, srckey + "_p0", srckey + "_p1"),
                        w="p4%d" % pj)
                    if evac == "k":
                        p.do("scalar", lambda e, og=og, pj=pj: e.activation(
                            abuf[:, og * 512:(og + 1) * 512], ps4[pj][:], AF.Exp),
                            r="p4%d" % pj, w="abuf%d" % og)
                    elif evac == "v":
                        p.do("scalar", lambda e, og=og, pj=pj: e.activation(
                            pbuf[:, og * 512:(og + 1) * 512], ps4[pj][:], AF.Copy),
                            r="p4%d" % pj, w="pbuf%d" % og)
                    else:
                        p.do("scalar", lambda e, og=og, pj=pj: e.activation(
                            rbuf[:, og * 512:(og + 1) * 512], ps4[pj][:],
                            AF.Sigmoid), r="p4%d" % pj,
                            w=("rbuf%d" % og, "rbufmix_p0", "rbufmix_p1"))

        def p4_scans(cI):
            for g in range(G):
                sl = slice(g * 512, (g + 1) * 512)
                p.do("vector", lambda e, sl=sl, g=g: e.tensor_tensor(
                    pbuf[:, sl], abuf[:, sl], pbuf[:, sl], ALU.mult),
                    r=("abuf%d" % g, "pbuf%d" % g), w="pbuf%d" % g)
            for g in range(G):
                sl = slice(g * 512, (g + 1) * 512)
                gc = slice(g, g + 1)
                p.do("vector", lambda e, gc=gc, cI=cI: e.tensor_scalar(
                    yN[:, 0:1], SM["carryN"][:, gc], vecs["carrymask"][:, cI:cI + 1],
                    None, ALU.mult), r=("sm.carryN", "v_carrymask"), w="yN")
                p.do("vector", lambda e, gc=gc, cI=cI: e.tensor_scalar(
                    yD[:, 0:1], SM["carryD"][:, gc], vecs["carrymask"][:, cI:cI + 1],
                    None, ALU.mult), r=("sm.carryD", "v_carrymask"), w="yD")
                p.do("vector", lambda e, sl=sl, gc=gc, g=g: e.tensor_tensor_scan(
                    yN[:, 1:513], SM["dcol"][:, gc].broadcast_to([128, 512]),
                    pbuf[:, sl], yN[:, 0:1], ALU.mult, ALU.add),
                    r=("sm.dcol", "pbuf%d" % g, "yN"), w="yN")
                p.do("vector", lambda e, sl=sl, gc=gc, g=g: e.tensor_tensor_scan(
                    yD[:, 1:513], SM["dcol"][:, gc].broadcast_to([128, 512]),
                    abuf[:, sl], yD[:, 0:1], ALU.mult, ALU.add),
                    r=("sm.dcol", "abuf%d" % g, "yD"), w="yD")
                p.do("vector", lambda e, gc=gc: e.tensor_copy(
                    SM["carryN"][:, gc], yN[:, 512:513]), r="yN", w="sm.carryN")
                p.do("vector", lambda e, gc=gc: e.tensor_copy(
                    SM["carryD"][:, gc], yD[:, 512:513]), r="yD", w="sm.carryD")
                p.do("vector", lambda e, sl=sl, gc=gc, g=g: e.scalar_tensor_tensor(
                    pbuf[:, sl], pbuf[:, sl], SM["wcol"][:, gc], yN[:, 0:512],
                    ALU.mult, ALU.add), r=("pbuf%d" % g, "sm.wcol", "yN"),
                    w="pbuf%d" % g)
                p.do("vector", lambda e, sl=sl, gc=gc, g=g: e.scalar_tensor_tensor(
                    abuf[:, sl], abuf[:, sl], SM["wcol"][:, gc], yD[:, 0:512],
                    ALU.mult, ALU.add), r=("abuf%d" % g, "sm.wcol", "yD"),
                    w="abuf%d" % g)

        def p4_exchange(cI):
            p.do("vector", lambda e: e.tensor_copy(e2sb[:, 0:16], SM["carryN"]),
                 r="sm.carryN", w="e2sb")
            p.do("vector", lambda e: e.tensor_copy(e2sb[:, 16:32], SM["carryD"]),
                 r="sm.carryD", w="e2sb")
            margin("vector")
            p.dma("gpsimd", lambda e: e.dma_start(out=e2i.ap(), in_=e2sb[:]),
                  r="e2sb", w="e2i")
            p.do("gpsimd", lambda e: e.collective_compute(
                "AllGather", ALU.bypass, replica_groups=[list(range(NCORES))],
                ins=[e2i.ap().opt()], outs=[e2o.ap().opt()]),
                inc="CC", r="e2i", w="e2o")
            p.dma("gpsimd", lambda e: e.dma_start(
                out=e2all[:].rearrange("p (r j) -> p r j", r=NCORES),
                in_=e2o.ap().rearrange("(r p) j -> p r j", p=128)),
                r="e2o", w="e2all")

        def p4_gamma(cI):
            p.dma("sync", lambda e, cI=cI: e.dma_start(
                out=junk4[:, 0:256], in_=dram["mselS"][cI]), r="junk4", w="junk4")
            p.do("vector", lambda e, cI=cI: e.tensor_tensor(
                ltmp[:, 0:256], e2all[:], junk4[:, 0:256],
                ALU.mult), r=("e2all", "junk4"), w="ltmp")
            p.do("vector", lambda e: e.reduce_sum(
                sce[:], ltmp[:, 0:256].rearrange("p (r j) -> p j r", r=8),
                axis=AX.X), r="ltmp", w="sce")
            p.do("vector", lambda e: e.tensor_copy(GQ["Gprev"], GQ["Gcur"]),
                 r="g.Gcur", w="g.Gprev")
            p.do("vector", lambda e: e.tensor_tensor(
                GQ["Gcur"], GQ["Gprev"], small[:, 10 * 16:12 * 16],
                ALU.mult), r=("g.Gprev", "sm.d1024a", "sm.d1024b"), w="g.Gcur")
            p.do("vector", lambda e: e.tensor_tensor(
                GQ["Gcur"], GQ["Gcur"], sce[:], ALU.add),
                r=("g.Gcur", "sce"), w="g.Gcur")
            p.do("vector", lambda e: e.tensor_tensor(
                GQ["Gd"], GQ["Gprev"], small[:, 12 * 16:14 * 16], ALU.mult),
                r=("g.Gprev", "sm.d512a", "sm.d512b"), w="g.Gd")
            p.do("vector", lambda e, cI=cI: e.tensor_scalar(
                GQ["Gam"], GQ["Gcur"], vecs["gmask_a"][:, cI:cI + 1], None,
                ALU.mult), r=("g.Gcur", "v_gmask_a"), w="g.Gam")
            p.do("vector", lambda e, cI=cI: e.scalar_tensor_tensor(
                GQ["Gam"], GQ["Gd"], vecs["gmask_e"][:, cI:cI + 1], GQ["Gam"],
                ALU.mult, ALU.add), r=("g.Gd", "v_gmask_e", "g.Gam"), w="g.Gam")

        def p4_ratio(cI, gs=range(16)):
            for g in gs:
                sl = slice(g * 512, (g + 1) * 512)
                p.do("vector", lambda e, sl=sl, g=g: e.scalar_tensor_tensor(
                    pbuf[:, sl], powt[:, sl], GQ["Gam"][:, g:g + 1], pbuf[:, sl],
                    ALU.mult, ALU.add), r=("powt", "g.Gam", "pbuf%d" % g),
                    w="pbuf%d" % g)
                p.do("vector", lambda e, sl=sl, g=g: e.scalar_tensor_tensor(
                    abuf[:, sl], powt[:, sl], GQ["Gam"][:, 16 + g:17 + g],
                    abuf[:, sl], ALU.mult, ALU.add),
                    r=("powt", "g.Gam", "abuf%d" % g), w="abuf%d" % g)
                p.do("scalar", lambda e, sl=sl, g=g: e.activation(
                    ltmp[:], abuf[:, sl], AF.Ln), r="abuf%d" % g, w="ltmp")
                p.do("scalar", lambda e, sl=sl, g=g: e.activation(
                    abuf[:, sl], ltmp[:], AF.Exp, scale=-1.0), r="ltmp",
                    w="abuf%d" % g)
                p.do("vector", lambda e, sl=sl, g=g: e.tensor_tensor(
                    pbuf[:, sl], pbuf[:, sl], abuf[:, sl], ALU.mult),
                    r=("pbuf%d" % g, "abuf%d" % g), w="pbuf%d" % g)
                p.do("vector", lambda e, sl=sl, g=g: e.tensor_tensor(
                    rbuf[:, sl], rbuf[:, sl], pbuf[:, sl], ALU.mult),
                    r=("rbuf%d" % g, "pbuf%d" % g), w="rbuf%d" % g)

        def p4_wo(cI):
            for osec in range(8):
                jj = osec % 2
                p.dma("gpsimd", lambda e, osec=osec, jj=jj: e.dma_start(
                    out=w4[jj][:].rearrange("p (a o) -> p a o", a=G),
                    in_=dram["wo_w"][:, osec * 256:(osec + 1) * 256]
                    .rearrange("(a p) o -> p a o", p=128)), w="w4_%d" % jj)
                for ogl in range(2):
                    og = osec * 2 + ogl
                    pj = og % 2
                    blds = [lambda e, ogl=ogl, fg=fg, jj=jj, pj=pj: e.matmul(
                        ps4[pj][:],
                        w4[jj][:, fg * 256 + ogl * 128: fg * 256 + (ogl + 1) * 128],
                        rbuf[:, fg * 512:(fg + 1) * 512],
                        start=(fg == 0), stop=(fg == G - 1)) for fg in range(G)]
                    p.group("tensor", blds,
                            r=tuple("rbuf%d" % g for g in range(G)) +
                              ("w4_%d" % jj,),
                            w="p4%d" % pj)
                    p.dma("sync", lambda e, og=og, cI=cI, pj=pj: e.dma_start(
                        out=rx4[pj][:],
                        in_=resd[og * 128:(og + 1) * 128, cI * 512:(cI + 1) * 512]),
                        r="resd%d.%d" % (og, cI), w="rx4_%d" % pj)
                    p.do("vector", lambda e, og=og, pj=pj: e.tensor_tensor(
                        r3s[:, pj * 512:(pj + 1) * 512], rx4[pj][:], ps4[pj][:],
                        ALU.add), r=("rx4_%d" % pj, "p4%d" % pj), w="r3s%d" % pj)
                    p.dma("scalar", lambda e, og=og, cI=cI, pj=pj: e.dma_start(
                        out=resd[og * 128:(og + 1) * 128, cI * 512:(cI + 1) * 512],
                        in_=r3s[:, pj * 512:(pj + 1) * 512]),
                        r="r3s%d" % pj, w="resd%d.%d" % (og, cI))

        p4_front(0)
        emit_mats("wk_w", xmix, "xmix", "k")
        for cI in range(4):
            emit_mix(rbuf, "rbufmix", "mixv", "onemv", extraw=("rbuf%d",))
            emit_mats("wv_w", rbuf, "rbufmix", "v")
            emit_mix(xmix, "xmix", "mixr", "onemr")
            p.do("vector", lambda e: e.tensor_copy(
                SM["prevcol"],
                xin4[:].rearrange("p (g t) -> p g t", g=G)[:, :, 511]),
                r="xin4", w="sm.prevcol")
            emit_mats("wr_w", xmix, "xmix", "r")
            p4_scans(cI)
            p4_exchange(cI)
            if cI < 3:
                p4_front(cI + 1)
            p4_gamma(cI)
            if cI < 3:
                for osec in range(4):
                    p4_ratio(cI, range(osec * 2, osec * 2 + 2))
                    emit_mats("wk_w", xmix, "xmix", "k", secs=[osec])
                p4_ratio(cI, range(8, 16))
            else:
                p4_ratio(cI)
            p4_wo(cI)
            if cI < 3:
                emit_mats("wk_w", xmix, "xmix", "k", secs=range(4, 8))

        p.barrier()
        if "resid3" in dbg:
            p.dma("sync", lambda e: e.dma_start(out=dbg["resid3"], in_=resd),
                  r="resd0.0", w="dbg_resid3")
            p.barrier()
        p4.close()

    if do_p5:
        emit_ffn_phase(final=True)
        p6 = ExitStack()
        tin = p6.enter_context(nc.sbuf_tensor("p6in", [128, G * 128], F32))
        tout = p6.enter_context(nc.sbuf_tensor("p6out", [128, 2048], F32))
        ps6 = [p6.enter_context(nc.psum_tensor("p6ps%d" % j, [128, 512], F32))
               for j in range(2)]
        for tt in range(16):
            p.dma("sync", lambda e, tt=tt: e.dma_start(
                out=tin[:].rearrange("p (g t) -> p g t", g=G),
                in_=resd[:, tt * 128:(tt + 1) * 128]
                .rearrange("(g p) t -> p g t", p=128)),
                r=tuple("resd%d.%d" % (g, tt // 4) for g in range(G)), w="p6in")
            for gb in range(4):
                for gl in range(4):
                    g = gb * 4 + gl
                    p.do("tensor", lambda e, g=g, gl=gl: e.transpose(
                        ps6[0][:, gl * 128:(gl + 1) * 128],
                        tin[:, g * 128:(g + 1) * 128], identf[:]),
                        r=("p6in", "identf"), w="p6a")
                for dd in range(2):
                    p.do("tensor", lambda e, dd=dd: e.transpose(
                        ps6[1][:, dd * 128:(dd + 1) * 128], identf[:], identf[:]),
                        r="identf", w="p6d")
                p.wait("scalar", "TE")
                for gl in range(4):
                    g = gb * 4 + gl
                    p.do("scalar", lambda e, g=g, gl=gl: e.activation(
                        tout[:, g * 128:(g + 1) * 128],
                        ps6[0][:, gl * 128:(gl + 1) * 128], AF.Copy),
                        r="p6a", w="p6out")
            p.dma("scalar", lambda e, tt=tt: e.dma_start(
                out=outy[tt * 128:(tt + 1) * 128, :], in_=tout[:]),
                r="p6out", w="y")
        p6.close()

    for j, (nm, src) in enumerate([("kdN", kdN), ("kdR", kdR), ("vd", vd), ("qd", qd)]):
        if nm in dbg:
            eng = "sync" if j % 2 == 0 else "gpsimd"
            p.dma(eng, lambda e, nm=nm, src=src: e.dma_start(out=dbg[nm], in_=src),
                  r=nm, w="dbg_" + nm)
    p.barrier()

    p.barrier()

    st.close()

    sem_stack = ExitStack()
    for s in p.cnt:
        p.sem[s] = sem_stack.enter_context(nc.semaphore(s))
    with nc.Block() as blk:
        for eng in ENGS:
            def make(eng):
                def run(e):
                    for g, fn in p.ops:
                        if g == eng:
                            fn(e)
                return run
            getattr(blk, eng)(make(eng))
    sem_stack.close()
    return nc



def kernel(**inputs):
    per_core = _host_prepare(inputs)
    nc = build_program()
    res = run_bass_kernel_spmd(nc, per_core, core_ids=list(range(NCORES)))
    out = np.zeros((B, SEQ, D), np.float32)
    for c in range(NCORES):
        b, h = c // 2, c % 2
        y = res.results[c]["y"]
        for i, g0 in enumerate(OWN[h]):
            out[b, g0 * 512:(g0 + 1) * 512, :] = y[i * 512:(i + 1) * 512, :]
    return out


# version 10 (final)
# speedup vs baseline: 1.0842x; 1.0436x over previous
import numpy as np
import ml_dtypes
from contextlib import ExitStack

import concourse.bass as bass
import concourse.mybir as mybir
from concourse.bass_utils import run_bass_kernel_spmd

F32 = mybir.dt.float32
BF16 = mybir.dt.bfloat16
I8 = mybir.dt.int8
I32 = mybir.dt.int32
AF = mybir.ActivationFunctionType
ALU = mybir.AluOpType
AX = mybir.AxisListType

NCORES = 8
B = 4
SEQ = 4096
D = 2048
G = 16
NBLK = 8
OWN = {0: [0, 3, 4, 7], 1: [1, 2, 5, 6]}
CH = 512
EPS_LN = 1e-5
EPS_RMS = 1.1920929e-07
ISQ = 0.08838834764831845
_LASTP = None
NEG = -1e9

BF = ml_dtypes.bfloat16



def _vec_fm(v, g):
    return np.ascontiguousarray(v.reshape(g, 128).T.astype(np.float32))


def _host_prepare(inp):
    W = {}
    W["qd_w"] = np.ascontiguousarray(inp["L0_attn_a_q_down_w"].T)
    W["qu_w"] = np.ascontiguousarray(inp["L0_attn_a_q_up_w"].T)
    W["kvd_w"] = np.ascontiguousarray(inp["L0_attn_a_kv_down_w"].T)
    kvu = inp["L0_attn_a_kv_up_w"]
    kvu3 = kvu.reshape(16, 192, 512)
    W["kwk_w"] = np.ascontiguousarray(
        kvu3[:, 0:64, :].reshape(1024, 512).T)
    W["kwv_w"] = np.ascontiguousarray(
        kvu3[:, 64:192, :].reshape(2048, 512).T)
    W["krope_w"] = np.ascontiguousarray(inp["L0_attn_a_k_rope_w"].T)
    W["ow_w"] = np.ascontiguousarray(inp["L0_attn_a_out_w"].T)
    W["wk_w"] = np.ascontiguousarray(inp["L0_attn_b_Wk"].T).astype(ml_dtypes.bfloat16)
    W["wv_w"] = np.ascontiguousarray(inp["L0_attn_b_Wv"].T).astype(ml_dtypes.bfloat16)
    W["wr_w"] = np.ascontiguousarray(inp["L0_attn_b_Wr"].T).astype(ml_dtypes.bfloat16)
    W["wo_w"] = np.ascontiguousarray(inp["L0_attn_b_Wo"].T).astype(ml_dtypes.bfloat16)
    W["fg_w"] = np.ascontiguousarray(inp["L0_ffn_w_gate"])
    W["fu_w"] = np.ascontiguousarray(inp["L0_ffn_w_up"])
    W["fd_w"] = np.ascontiguousarray(inp["L0_ffn_w_down"])

    W["qd_b"] = _vec_fm(inp["L0_attn_a_q_down_b"], 8)
    W["qnorm_w"] = _vec_fm(inp["L0_attn_a_q_norm_w"], 8)
    W["qu_b"] = _vec_fm(inp["L0_attn_a_q_up_b"], 16)
    W["kvd_b"] = _vec_fm(inp["L0_attn_a_kv_down_b"], 4)
    W["kvnorm_w"] = _vec_fm(inp["L0_attn_a_kv_norm_w"], 4)
    kvub = inp["L0_attn_a_kv_up_b"].reshape(16, 192)
    W["knope_b"] = _vec_fm(np.ascontiguousarray(kvub[:, 0:64]).reshape(1024), 8)
    W["krope_b"] = _vec_fm(inp["L0_attn_a_k_rope_b"], 8)
    W["vb_row"] = np.ascontiguousarray(
        kvub[:, 64:192].reshape(1, 2048).astype(np.float32))
    W["ow_b"] = _vec_fm(inp["L0_attn_a_out_b"], 16)
    W["lnw"] = _vec_fm(inp["L0_attn_b_ln_w"], 16)
    W["lnb"] = _vec_fm(inp["L0_attn_b_ln_b"], 16)
    W["mixk"] = _vec_fm(inp["L0_attn_b_mix_k"], 16)
    W["mixv"] = _vec_fm(inp["L0_attn_b_mix_v"], 16)
    W["mixr"] = _vec_fm(inp["L0_attn_b_mix_r"], 16)
    W["td"] = _vec_fm(inp["L0_attn_b_time_decay"], 16)
    W["cw"] = _vec_fm(inp["L0_attn_b_current_token_weight"], 16)
    W["sgate"] = _vec_fm(inp["L0_ffn_s_gate"], 64)
    W["sup"] = _vec_fm(inp["L0_ffn_s_up"], 64)
    W["sdown"] = _vec_fm(inp["L0_ffn_s_down"], 16)

    x = inp["L0_attn_a_x"]
    per_core = []
    for c in range(NCORES):
        b, h = c // 2, c % 2
        blocks = OWN[h]
        own_rows = np.concatenate(
            [np.arange(g0 * 512, (g0 + 1) * 512) for g0 in blocks])
        d = dict(W)
        d["x_all"] = np.ascontiguousarray(x[b])
        d["x_own"] = np.ascontiguousarray(x[b][own_rows])

        M = np.zeros((4, 8, 128, 512), np.float32)
        for i, g0 in enumerate(blocks):
            bound = 1024 * (i + 1)
            qglob = g0 * 512 + np.arange(512)[None, :]
            for s in range(8):
                k0 = bound - 1024 + s * 128
                kglob = (k0 + np.arange(128))[:, None]
                M[i, s] = np.where(kglob <= qglob, 0.0, NEG)
        d["mmask"] = M.astype(BF)

        cm = [0.0, 0.0, 1.0, 0.0] if h == 0 else [0.0, 1.0, 0.0, 1.0]
        d["carrymask"] = np.broadcast_to(
            np.array(cm, np.float32), (128, 4)).copy()
        a_own = [0.0, 0.0, 1.0, 0.0] if h == 0 else [0.0, 1.0, 0.0, 1.0]
        d["sxown"] = np.broadcast_to(
            np.array(a_own, np.float32), (128, 4)).copy()
        p = c ^ 1
        msel = np.zeros((4, 128, 8 * 64), np.float32)
        e1need = {0: {1: (p, 1), 3: (p, 3)}, 1: {0: (p, 0), 2: (p, 2)}}[h]
        for cI, (rk, j) in e1need.items():
            msel[cI, :, rk * 64 + j * 16: rk * 64 + (j + 1) * 16] = 1.0
        d["sxsel"] = msel
        mselS = np.zeros((4, 128, 8 * 32), np.float32)
        for cI in range(4):
            rk = 2 * b + (cI & 1)
            mselS[cI, :, rk * 32:(rk + 1) * 32] = 1.0
        d["mselS"] = mselS
        ga = [0.0, 1.0, 0.0, 1.0] if h == 0 else [1.0, 0.0, 1.0, 0.0]
        ge = [0.0, 0.0, 1.0, 0.0] if h == 0 else [0.0, 1.0, 0.0, 1.0]
        d["gmask_a"] = np.broadcast_to(np.array(ga, np.float32), (128, 4)).copy()
        d["gmask_e"] = np.broadcast_to(np.array(ge, np.float32), (128, 4)).copy()
        per_core.append(d)
    return per_core



ENGS = ("sync", "gpsimd", "vector", "scalar", "tensor")


class P:

    ESEM = {"sync": "DS0", "gpsimd": "GE", "vector": "VE", "scalar": "SE",
            "tensor": "TE"}
    NLANES = 4
    DQUEUES = {"sync": "DS", "gpsimd": "DG", "scalar": "DC"}
    DMASEMS = tuple("%s%d" % (q, l) for q in ("DS", "DG", "DC")
                    for l in range(4))

    def __init__(self, nc):
        self.nc = nc
        self.ops = []
        self.sem = {}
        self.cnt = {s: 0 for s in
                    P.DMASEMS + ("TE", "VE", "SE", "GE", "CC")}
        self.dma_rr = {"DS": 0, "DG": 0, "DC": 0}
        self.lastw = {}
        self.writer = {}
        self.readers = {}

    def emit(self, eng, fn):
        self.ops.append((eng, fn))

    def wait(self, eng, sem, val=None):
        v = self.cnt[sem] if val is None else val
        if v <= 0 or self.lastw.get((eng, sem), 0) >= v:
            return
        self.lastw[(eng, sem)] = v
        self.emit(eng, lambda e, s=sem, v=v: e.wait_ge(self.sem[s], v))

    def _dep_wait(self, eng, sem, val):
        self.wait(eng, sem, val)

    def _pre(self, eng, r, w):
        for key in r:
            wr = self.writer.get(key)
            if wr is not None:
                self._dep_wait(eng, wr[0], wr[1])
        for key in w:
            wr = self.writer.get(key)
            if wr is not None:
                self._dep_wait(eng, wr[0], wr[1])
            for sem, val in self.readers.get(key, {}).items():
                self._dep_wait(eng, sem, val)

    def _post(self, sem, r, w):
        v = self.cnt[sem]
        for key in w:
            self.writer[key] = (sem, v)
            self.readers[key] = {}
        for key in r:
            self.readers.setdefault(key, {})[sem] = v

    def do(self, eng, build, inc=None, n=None, r=(), w=()):
        if inc is None:
            inc = P.ESEM[eng]
        if n is None:
            n = 16 if inc.startswith(("DS", "DG", "DC")) else 1
        if isinstance(r, str):
            r = (r,)
        if isinstance(w, str):
            w = (w,)
        self._pre(eng, r, w)
        self.cnt[inc] += n
        def f(e, build=build, inc=inc, n=n):
            ins = build(e)
            ins.then_inc(self.sem[inc], n)
        self.emit(eng, f)
        self._post(inc, r, w)
        return self.cnt[inc]

    def dma(self, eng, build, r=(), w=()):
        q = P.DQUEUES[eng]
        lane = self.dma_rr[q]
        self.dma_rr[q] = (lane + 1) % P.NLANES
        sem = "%s%d" % (q, lane)
        self.wait(eng, sem, self.cnt[sem])
        return self.do(eng, build, inc=sem, n=16, r=r, w=w)

    def group(self, eng, builds, inc=None, n=None, r=(), w=()):
        if inc is None:
            inc = P.ESEM[eng]
        if n is None:
            n = 16 if inc in ("DS", "DG", "DC") else 1
        if isinstance(r, str):
            r = (r,)
        if isinstance(w, str):
            w = (w,)
        self._pre(eng, r, w)
        self.cnt[inc] += n
        def f(e, builds=builds, inc=inc, n=n):
            ins = None
            for b in builds:
                ins = b(e)
            ins.then_inc(self.sem[inc], n)
        self.emit(eng, f)
        self._post(inc, r, w)
        return self.cnt[inc]

    def barrier(self, engines=ENGS, sems=None):
        for eng in engines:
            for s in (sems or self.cnt.keys()):
                self.wait(eng, s)


def build_program(debug=(), kv_chunks=8, q_chunks=4, p1_stages=99, dump_at='after_rope', do_p2=True, do_p3=True, do_p4=True, do_p5=True):
    nc = bass.Bass("TRN2", target_bir_lowering=False)
    p = P(nc)
    global _LASTP
    _LASTP = p

    dram = {}
    def din(name, shape, dt):
        dram[name] = nc.dram_tensor(name, shape, dt, kind="ExternalInput").ap()
    din("x_all", [SEQ, D], F32)
    din("x_own", [D, D], F32)
    for nm, sh in [("qd_w", [2048, 1024]), ("qu_w", [1024, 2048]),
                   ("kvd_w", [2048, 512]), ("kwk_w", [512, 1024]),
                   ("kwv_w", [512, 2048]), ("krope_w", [2048, 1024]),
                   ("ow_w", [2048, 2048])]:
        din(nm, sh, F32)
    for nm in ("wk_w", "wv_w", "wr_w", "wo_w"):
        din(nm, [2048, 2048], BF16)
    for nm in ("fg_w", "fu_w"):
        din(nm, [2048, 8192], I8)
    din("fd_w", [8192, 2048], I8)
    for nm, g in [("qd_b", 8), ("qnorm_w", 8), ("qu_b", 16), ("kvd_b", 4),
                  ("kvnorm_w", 4), ("knope_b", 8), ("krope_b", 8),
                  ("ow_b", 16), ("lnw", 16), ("lnb", 16), ("mixk", 16),
                  ("mixv", 16), ("mixr", 16), ("td", 16), ("cw", 16),
                  ("sgate", 64), ("sup", 64), ("sdown", 16)]:
        din(nm, [128, g], F32)
    din("vb_row", [1, 2048], F32)
    din("mmask", [4, 8, 128, 512], BF16)
    for nm in ("carrymask", "sxown", "gmask_a", "gmask_e"):
        din(nm, [128, 4], F32)
    din("sxsel", [4, 128, 512], F32)
    din("mselS", [4, 128, 256], F32)

    kdN = nc.dram_tensor("kdN", [1024, SEQ], BF16, kind="Internal").ap()
    kdR = nc.dram_tensor("kdR", [1024, SEQ], BF16, kind="Internal").ap()
    vd = nc.dram_tensor("vd", [SEQ, 2048], BF16, kind="Internal").ap()
    qd = nc.dram_tensor("qd", [2048, 2048], BF16, kind="Internal").ap()
    resd = nc.dram_tensor("resd", [2048, 2048], F32, kind="Internal").ap()
    xind = nc.dram_tensor("xind", [2048, 2048], BF16, kind="Internal").ap()
    e1i = nc.dram_tensor("e1i", [128, 64], F32, kind="Internal")
    e1o = nc.dram_tensor("e1o", [128 * NCORES, 64], F32, kind="Internal")
    e2i = nc.dram_tensor("e2i", [128, 32], F32, kind="Internal")
    e2o = nc.dram_tensor("e2o", [128 * NCORES, 32], F32, kind="Internal")

    outy = nc.dram_tensor("y", [2048, D], F32, kind="ExternalOutput").ap()
    dbg = {}
    for nm in debug:
        shp = {"ln0T": [128, 8192], "kdN": [1024, SEQ], "kdR": [1024, SEQ], "vd": [SEQ, 2048],
               "qd": [2048, 2048], "resid1": [2048, 2048], "resid2": [2048, 2048],
               "resid3": [2048, 2048], "xin": [2048, 2048]}[nm]
        dt = BF16 if nm in ("kdN", "kdR", "vd", "qd", "xin") else F32
        if nm == "ln0T":
            shp, dt = [128, 8192], BF16
        dbg[nm] = nc.dram_tensor("dbg_" + nm, shp, dt, kind="ExternalOutput").ap()

    st = ExitStack()
    SB = lambda name, shape, dt: st.enter_context(nc.sbuf_tensor(name, shape, dt))
    PS = lambda name, shape, dt=F32: st.enter_context(nc.psum_tensor(name, shape, dt))

    identb = SB("identb", [128, 128], BF16)
    identf = SB("identf", [128, 128], F32)
    iot = SB("iot", [128, 128], I32)
    ones_row = SB("ones_row", [1, 128], F32)
    ones_col = SB("ones_col", [128, 1], F32)
    ones_colb = SB("ones_colb", [128, 1], BF16)
    vecs = {}
    for nm, g in [("qd_b", 8), ("qnorm_w", 8), ("qu_b", 16), ("kvd_b", 4),
                  ("kvnorm_w", 4), ("knope_b", 8), ("krope_b", 8),
                  ("ow_b", 16), ("lnw", 16), ("lnb", 16), ("mixk", 16),
                  ("mixv", 16), ("mixr", 16), ("td", 16), ("cw", 16),
                  ("sgate", 64), ("sup", 64), ("sdown", 16),
                  ("carrymask", 4), ("sxown", 4), ("gmask_a", 4), ("gmask_e", 4)]:
        vecs[nm] = SB("v_" + nm, [128, g], F32)

    row_a = SB("row_a", [1, 512], F32)
    row_b = SB("row_b", [1, 512], F32)
    eps_ln = SB("eps_ln", [128, 1], F32)
    eps_rms = SB("eps_rms", [128, 1], F32)
    dumV = SB("dumV", [1, 512], F32)
    dumS = SB("dumS", [1, 512], F32)
    dumW = SB("dumW", [128, 512], F32)
    dumWb = SB("dumWb", [128, 512], BF16)

    pwu = ExitStack()
    psW = pwu.enter_context(nc.psum_tensor("psW", [128, 512], F32))
    p.do("vector", lambda e: e.memset(dumW[:], 1.25), w="dumW")
    p.do("vector", lambda e: e.memset(dumWb[:], 1.0), w="dumWb")
    p.do("vector", lambda e: e.memset(dumV[:], 0.0), w="dumV")
    p.do("gpsimd", lambda e: e.memset(dumS[:], 1.0), w="dumS")
    for i in range(24):
        p.do("vector", lambda e: e.tensor_tensor(dumW[:], dumW[:], dumW[:], ALU.max),
             r="dumW", w="dumW")
    for fn in (AF.Exp, AF.Ln, AF.Sigmoid, AF.Silu):
        p.do("scalar", lambda e, fn=fn: e.activation(dumS[:], dumS[:], fn),
             r="dumS", w="dumS")
    for i in range(16):
        p.do("scalar", lambda e: e.activation(dumS[:], dumS[:], AF.Square),
             r="dumS", w="dumS")
    for i in range(24):
        p.group("tensor", [lambda e, i=i: e.matmul(
            psW[:], dumWb[:, 0:128], dumWb[:], start=(i == 0), stop=(i == 23))],
            r="dumWb", w="psW")
    p.do("vector", lambda e: e.tensor_copy(dumW[:], psW[:]), r="psW", w="dumW")
    p.barrier()
    pwu.close()

    for nm in vecs:
        p.dma("sync", lambda e, nm=nm: e.dma_start(out=vecs[nm][:], in_=dram[nm]),
              w="v_" + nm)
    p.do("gpsimd", lambda e: e.iota(iot[:], [[1, 128]], base=0, channel_multiplier=-1),
         w="iot")
    p.do("vector", lambda e: e.tensor_single_scalar(identb[:], iot[:], 0, ALU.is_equal),
         r="iot", w="identb")
    p.do("vector", lambda e: e.tensor_single_scalar(identf[:], iot[:], 0, ALU.is_equal),
         r="iot", w="identf")
    p.do("vector", lambda e: e.memset(ones_row[:], 1.0), w="ones_row")
    p.do("vector", lambda e: e.memset(ones_col[:], 1.0), w="ones_col")
    p.do("vector", lambda e: e.memset(ones_colb[:], 1.0), w="ones_colb")
    p.do("vector", lambda e: e.memset(eps_ln[:], float(EPS_LN)), w="eps_ln")
    p.do("vector", lambda e: e.memset(eps_rms[:], float(EPS_RMS)), w="eps_rms")
    p.barrier()

    def margin(eng):
        if eng == "vector":
            p.do("vector", lambda e: e.tensor_copy(dumV[:], dumV[:]), r="dumV", w="dumV")
        elif eng == "scalar":
            p.do("scalar", lambda e: e.activation(dumS[:], dumS[:], AF.Copy),
                 r="dumS", w="dumS")

    def tokmajor_ln(xsb, ntile, lnout, scr, xkey, lnkey):
        col = lambda nm, tt: scr[nm][:, tt:tt + 1]
        K_ = lambda nm: "ln." + nm
        for tt in range(ntile):
            xs = xsb[:, tt * 2048:(tt + 1) * 2048]
            p.do("vector", lambda e, xs=xs, o=col("mu", tt): e.reduce_sum(o, xs, axis=AX.X),
                 r=xkey, w=K_("mu"))
            p.do("vector", lambda e, xs=xs: e.tensor_tensor(
                scr["junk"][:], xs, xs, ALU.mult), r=xkey, w="ln.junk")
            p.do("vector", lambda e, o=col("ss", tt): e.reduce_sum(
                o, scr["junk"][:], axis=AX.X), r="ln.junk", w=K_("ss"))
            p.do("vector", lambda e, o=col("mu", tt): e.tensor_scalar(
                o, o, 1.0 / D, None, ALU.mult), r=K_("mu"), w=K_("mu"))
            p.do("vector", lambda e, o=col("var", tt), i=col("ss", tt): e.tensor_scalar(
                o, i, 1.0 / D, None, ALU.mult), r=K_("ss"), w=K_("var"))
            p.do("vector", lambda e, o=col("m2", tt), i=col("mu", tt): e.tensor_tensor(
                o, i, i, ALU.mult), r=K_("mu"), w=K_("m2"))
            p.do("vector", lambda e, o=col("var", tt), i=col("m2", tt): e.tensor_tensor(
                o, o, i, ALU.subtract), r=(K_("var"), K_("m2")), w=K_("var"))
        margin("vector")
        for tt in range(ntile):
            p.do("scalar", lambda e, o=col("lnv", tt), i=col("var", tt): e.activation(
                o, i, AF.Ln, bias=eps_ln[:]), r=K_("var"), w=K_("lnv"))
        for tt in range(ntile):
            p.do("scalar", lambda e, o=col("rstd", tt), i=col("lnv", tt): e.activation(
                o, i, AF.Exp, scale=-0.5), r=K_("lnv"), w=K_("rstd"))
        margin("scalar")
        for tt in range(ntile):
            p.do("vector", lambda e, o=col("nb", tt), i=col("mu", tt), rr=col("rstd", tt):
                 e.tensor_scalar(o, i, rr, -1.0, ALU.mult, ALU.mult),
                 r=(K_("mu"), K_("rstd")), w=K_("nb"))
        margin("vector")
        for tt in range(ntile):
            p.do("scalar", lambda e, tt=tt, bb=col("nb", tt), rr=col("rstd", tt):
                 e.activation(
                     lnout[:, tt * 2048:(tt + 1) * 2048],
                     xsb[:, tt * 2048:(tt + 1) * 2048],
                     AF.Identity, bias=bb, scale=rr),
                 r=(xkey, K_("nb"), K_("rstd")), w=lnkey)

    def transpose_to_fm(lnout, ntile, dstT, ptr, Tcols, lnkey, dstkey):
        n = ntile * G
        se_after = {}
        def tr(k):
            tt, fg, slot = k // G, k % G, k % 4
            p.do("tensor", lambda e, tt=tt, fg=fg, slot=slot: e.transpose(
                ptr[slot][:],
                lnout[:, tt * 2048 + fg * 128: tt * 2048 + (fg + 1) * 128],
                identb[:]), r=lnkey, w="ptr%d" % slot)
            se_after[k] = p.cnt["TE"]
        def cp(k):
            tt, fg, slot = k // G, k % G, k % 4
            p.wait("scalar", "TE", se_after[min(k + 2, n + 1)])
            p.do("scalar", lambda e, tt=tt, fg=fg, slot=slot: e.activation(
                dstT[:, fg * Tcols + tt * 128: fg * Tcols + (tt + 1) * 128],
                ptr[slot][:], AF.Copy), r="ptr%d" % slot, w=dstkey)
        for k in range(n):
            tr(k)
            if k >= 2:
                cp(k - 2)
        for d in range(2):
            kk = n + d
            tt, fg, slot = 0, 0, kk % 4
            p.do("tensor", lambda e, slot=slot: e.transpose(
                ptr[slot][:], lnout[:, 0:128], identb[:]),
                r=lnkey, w="ptr%d" % slot)
            se_after[kk] = p.cnt["TE"]
            cp(n - 2 + d)

    def mm_rot(ngroups, emit_mms, emit_evac):
        for k in range(ngroups):
            ps, pskey = (psA, "psA") if k % 2 == 0 else (psB, "psB")
            emit_mms(k, ps, pskey)
            emit_evac(k, ps, pskey)

    p1 = ExitStack()
    SB1 = lambda name, shape, dt: p1.enter_context(nc.sbuf_tensor(name, shape, dt))
    PS1 = lambda name, shape, dt=F32: p1.enter_context(nc.psum_tensor(name, shape, dt))

    xsb = SB1("xsb", [128, 4 * 2048], F32)
    lnsb = SB1("lnsb", [128, 4 * 2048], BF16)
    ln0T2_ = [SB1("ln0T%d" % j, [128, G * 512], BF16) for j in range(2)]
    junk = SB1("junk", [128, 2048], F32)
    scr = {"junk": junk}
    for nm in ("mu", "ss", "var", "m2", "lnv", "rstd", "nb"):
        scr[nm] = SB1("scr_" + nm, [128, 4], F32)
    ckv = SB1("ckv", [128, 4 * 512], F32)
    ckvn = SB1("ckvn", [128, 4 * 512], BF16)
    ksb = SB1("ksb", [128, 2 * 512], BF16)
    vsb = SB1("vsb", [128, 2 * 512], BF16)
    p1a = ExitStack()
    SB1a = lambda name, shape, dt: p1a.enter_context(nc.sbuf_tensor(name, shape, dt))
    kvdw = SB1a("kvdw", [128, 16 * 512], BF16)
    vbrow = SB1a("vbrow", [1, 2048], F32)
    kwkw = SB1a("kwkw", [128, 4 * 1024], BF16)
    kwvw = SB1a("kwvw", [128, 4 * 2048], BF16)
    kropew = SB1a("kropew", [128, 16 * 1024], BF16)
    ptr = [PS1("ptr%d" % i, [128, 128], BF16) for i in range(4)]
    psA = PS1("psA", [128, 512])
    psB = PS1("psB", [128, 512])
    psS = PS1("psS", [1, 512])
    psBC = PS1("psBC", [128, 512])

    for nm, dst, na in [("kvd_w", kvdw, 16), ("kwk_w", kwkw, 4),
                        ("kwv_w", kwvw, 4), ("krope_w", kropew, 16)]:
        p.dma("gpsimd", lambda e, nm=nm, dst=dst, na=na: e.dma_start(
            out=dst[:].rearrange("p (a o) -> p a o", a=na),
            in_=dram[nm].rearrange("(a p) o -> p a o", p=128)), w=nm)
    p.dma("sync", lambda e: e.dma_start(out=vbrow[:], in_=dram["vb_row"]), w="vbrow")
    p.barrier()

    def rms_apply(nsub, src, dst, wvec, eps, srckey, dstkey):
        for sg in range(nsub):
            p.do("scalar", lambda e, sg=sg: e.activation(
                junk[:, 0:512], src[:, sg * 512:(sg + 1) * 512], AF.Square),
                r=srckey, w="rms.junk")
            p.group("tensor", [lambda e, sg=sg: e.matmul(
                psS[:], ones_col[:], junk[:, 0:512],
                start=(sg == 0), stop=(sg == nsub - 1))],
                r="rms.junk", w="psS" if sg == nsub - 1 else "psS.partial")
        p.do("vector", lambda e: e.tensor_scalar(
            row_a[:], psS[:], 1.0 / (nsub * 128), None, ALU.mult),
            r=("psS", "psS.partial"), w="row_a")
        margin("vector")
        p.do("scalar", lambda e, eps=eps: e.activation(
            row_b[:], row_a[:], AF.Ln,
            bias=(eps_ln if eps == EPS_LN else eps_rms)[0:1, :]),
            r="row_a", w="row_b")
        p.do("scalar", lambda e: e.activation(row_a[:], row_b[:], AF.Exp, scale=-0.5),
             r="row_b", w="row_a")
        margin("scalar")
        p.group("tensor", [lambda e: e.matmul(
            psBC[:], ones_row[:], row_a[:], start=True, stop=True)],
            r="row_a", w="psBC")
        for sg in range(nsub):
            p.do("vector", lambda e, sg=sg: e.scalar_tensor_tensor(
                dst[:, sg * 512:(sg + 1) * 512], src[:, sg * 512:(sg + 1) * 512],
                wvec[:, sg:sg + 1], psBC[:], ALU.mult, ALU.mult),
                r=(srckey, "psBC"), w=dstkey)

    def dump_ln0T(tag, ch):
        pass

    def p1_kv_chunk(ch):
        ln0T = ln0T2_[ch % 2]
        lnk = "ln0T%d" % (ch % 2)
        p.dma("sync", lambda e, ch=ch: e.dma_start(
            out=xsb[:].rearrange("p (a f) -> p a f", a=4),
            in_=dram["x_all"][ch * 512:(ch + 1) * 512, :]
            .rearrange("(a p) f -> p a f", p=128)), w="xsb")
        if p1_stages < 2: return
        tokmajor_ln(xsb, 4, lnsb, scr, "xsb", "lnsb")
        if p1_stages < 3: return
        transpose_to_fm(lnsb, 4, ln0T, ptr, 512, "lnsb", lnk)
        dump_ln0T("after_tr", ch)
        if p1_stages < 4: return
        def mm_ckv(dg, ps, pskey):
            p.group("tensor", [lambda e, ps=ps, dg=dg, fg=fg: e.matmul(
                ps[:], kvdw[:, fg * 512 + dg * 128: fg * 512 + (dg + 1) * 128],
                ln0T[:, fg * 512:(fg + 1) * 512],
                start=(fg == 0), stop=(fg == G - 1)) for fg in range(G)],
                r=(lnk, "kvd_w"), w=pskey)
        def ev_ckv(dg, ps, pskey):
            p.do("scalar", lambda e, ps=ps, dg=dg: e.activation(
                ckv[:, dg * 512:(dg + 1) * 512], ps[:], AF.Identity,
                bias=vecs["kvd_b"][:, dg:dg + 1]), r=pskey, w="ckv")
        mm_rot(4, mm_ckv, ev_ckv)
        dump_ln0T("after_ckv", ch)
        if p1_stages < 5: return
        rms_apply(4, ckv, ckvn, vecs["kvnorm_w"], EPS_RMS, "ckv", "ckvn")
        dump_ln0T("after_rms", ch)
        if p1_stages < 6: return
        def mm_kn(ng, ps, pskey):
            p.group("tensor", [lambda e, ps=ps, ng=ng, dg=dg: e.matmul(
                ps[:], kwkw[:, dg * 1024 + ng * 128: dg * 1024 + (ng + 1) * 128],
                ckvn[:, dg * 512:(dg + 1) * 512],
                start=(dg == 0), stop=(dg == 3)) for dg in range(4)],
                r=("ckvn", "kwk_w"), w=pskey)
        def ev_kn(ng, ps, pskey):
            slot = "ksb%d" % (ng % 2)
            buf = ksb[:, (ng % 2) * 512:(ng % 2 + 1) * 512]
            p.do("scalar", lambda e, ps=ps, ng=ng, buf=buf: e.activation(
                buf, ps[:], AF.Identity, bias=vecs["knope_b"][:, ng:ng + 1]),
                r=pskey, w=slot)
            p.dma("scalar", lambda e, ng=ng, buf=buf, ch=ch: e.dma_start(
                out=kdN[ng * 128:(ng + 1) * 128, ch * 512:(ch + 1) * 512], in_=buf),
                r=slot, w="kdN")
        mm_rot(8, mm_kn, ev_kn)
        dump_ln0T("after_nope", ch)
        if p1_stages < 7: return
        def mm_kr(rg, ps, pskey):
            p.group("tensor", [lambda e, ps=ps, rg=rg, fg=fg: e.matmul(
                ps[:], kropew[:, fg * 1024 + rg * 128: fg * 1024 + (rg + 1) * 128],
                ln0T[:, fg * 512:(fg + 1) * 512],
                start=(fg == 0), stop=(fg == G - 1)) for fg in range(G)],
                r=(lnk, "krope_w"), w=pskey)
        def ev_kr(rg, ps, pskey):
            slot = "ksb%d" % (rg % 2)
            buf = ksb[:, (rg % 2) * 512:(rg % 2 + 1) * 512]
            p.do("scalar", lambda e, ps=ps, rg=rg, buf=buf: e.activation(
                buf, ps[:], AF.Identity, bias=vecs["krope_b"][:, rg:rg + 1]),
                r=pskey, w=slot)
            p.dma("scalar", lambda e, rg=rg, buf=buf, ch=ch: e.dma_start(
                out=kdR[rg * 128:(rg + 1) * 128, ch * 512:(ch + 1) * 512], in_=buf),
                r=slot, w="kdR")
        mm_rot(8, mm_kr, ev_kr)
        dump_ln0T("after_rope", ch)
        if p1_stages < 8: return
        def mm_v(k, ps, pskey):
            tt, vc = k // 4, k % 4
            blds = [lambda e, ps=ps, tt=tt, vc=vc, dg=dg: e.matmul(
                ps[:], ckvn[:, dg * 512 + tt * 128: dg * 512 + (tt + 1) * 128],
                kwvw[:, dg * 2048 + vc * 512: dg * 2048 + (vc + 1) * 512],
                start=(dg == 0), stop=(dg == 3)) for dg in range(4)]
            blds.append(lambda e, ps=ps, vc=vc: e.matmul(
                ps[:], ones_row[:], vbrow[:, vc * 512:(vc + 1) * 512],
                start=False, stop=True, skip_group_check=True))
            p.group("tensor", blds, r=("ckvn", "kwv_w", "vbrow", "ones_row"), w=pskey)
        def ev_v(k, ps, pskey):
            tt, vc = k // 4, k % 4
            slot = "vsb%d" % (k % 2)
            buf = vsb[:, (k % 2) * 512:(k % 2 + 1) * 512]
            p.do("scalar", lambda e, ps=ps, buf=buf: e.activation(
                buf, ps[:], AF.Copy), r=pskey, w=slot)
            p.dma("scalar", lambda e, buf=buf, ch=ch, tt=tt, vc=vc: e.dma_start(
                out=vd[ch * 512 + tt * 128: ch * 512 + (tt + 1) * 128,
                       vc * 512:(vc + 1) * 512], in_=buf), r=slot, w="vd")
        mm_rot(16, mm_v, ev_v)

    for ch in range(kv_chunks):
        p1_kv_chunk(ch)
    p.barrier()

    p1a.close()
    qdw = SB1("qdw", [128, 16 * 1024], BF16)
    quw = SB1("quw", [128, 8 * 2048], BF16)
    cq = SB1("cq", [128, 8 * 512], F32)
    cqn = SB1("cqn", [128, 8 * 512], BF16)
    for eng in ("gpsimd",):
        for sm in ("TE", "SE", "VE"):
            p.wait(eng, sm)
    for nm, dst, na in [("qd_w", qdw, 16), ("qu_w", quw, 8)]:
        p.dma("gpsimd", lambda e, nm=nm, dst=dst, na=na: e.dma_start(
            out=dst[:].rearrange("p (a o) -> p a o", a=na),
            in_=dram[nm].rearrange("(a p) o -> p a o", p=128)), w=nm)

    def p1_q_chunk(qc):
        ln0T = ln0T2_[qc % 2]
        lnk = "ln0T%d" % (qc % 2)
        p.dma("sync", lambda e, qc=qc: e.dma_start(
            out=xsb[:].rearrange("p (a f) -> p a f", a=4),
            in_=dram["x_own"][qc * 512:(qc + 1) * 512, :]
            .rearrange("(a p) f -> p a f", p=128)), w="xsb")
        tokmajor_ln(xsb, 4, lnsb, scr, "xsb", "lnsb")
        transpose_to_fm(lnsb, 4, ln0T, ptr, 512, "lnsb", lnk)
        def mm_cq(qg, ps, pskey):
            p.group("tensor", [lambda e, ps=ps, qg=qg, fg=fg: e.matmul(
                ps[:], qdw[:, fg * 1024 + qg * 128: fg * 1024 + (qg + 1) * 128],
                ln0T[:, fg * 512:(fg + 1) * 512],
                start=(fg == 0), stop=(fg == G - 1)) for fg in range(G)],
                r=(lnk, "qd_w"), w=pskey)
        def ev_cq(qg, ps, pskey):
            p.do("scalar", lambda e, ps=ps, qg=qg: e.activation(
                cq[:, qg * 512:(qg + 1) * 512], ps[:], AF.Identity,
                bias=vecs["qd_b"][:, qg:qg + 1]), r=pskey, w="cq")
        mm_rot(8, mm_cq, ev_cq)
        rms_apply(8, cq, cqn, vecs["qnorm_w"], EPS_RMS, "cq", "cqn")
        def mm_q(og, ps, pskey):
            p.group("tensor", [lambda e, ps=ps, og=og, qg=qg: e.matmul(
                ps[:], quw[:, qg * 2048 + og * 128: qg * 2048 + (og + 1) * 128],
                cqn[:, qg * 512:(qg + 1) * 512],
                start=(qg == 0), stop=(qg == 7)) for qg in range(8)],
                r=("cqn", "qu_w"), w=pskey)
        def ev_q(og, ps, pskey):
            slot = "ksb%d" % (og % 2)
            buf = ksb[:, (og % 2) * 512:(og % 2 + 1) * 512]
            p.do("scalar", lambda e, ps=ps, og=og, buf=buf: e.activation(
                buf, ps[:], AF.Identity, bias=vecs["qu_b"][:, og:og + 1]),
                r=pskey, w=slot)
            p.dma("scalar", lambda e, og=og, buf=buf, qc=qc: e.dma_start(
                out=qd[og * 128:(og + 1) * 128, qc * 512:(qc + 1) * 512], in_=buf),
                r=slot, w="qd")
        mm_rot(16, mm_q, ev_q)

    for qc in range(q_chunks):
        p1_q_chunk(qc)

    p.barrier()
    p1.close()

    if do_p2:
        p2 = ExitStack()
        SB2 = lambda name, shape, dt: p2.enter_context(nc.sbuf_tensor(name, shape, dt))
        PS2 = lambda name, shape, dt=F32: p2.enter_context(nc.psum_tensor(name, shape, dt))
        qsb = SB2("qsb", [128, G * 512], BF16)
        kb = [SB2("kb%d" % j, [128, 4 * 512], BF16) for j in range(2)]
        vb = [SB2("vb%d" % j, [128, 4 * 512], BF16) for j in range(2)]
        pb = [SB2("pb%d" % j, [128, 512], BF16) for j in range(2)]
        attnbf = SB2("attnbf", [128, G * 512], BF16)
        msb = SB2("msb", [128, 8 * 512], BF16)
        xosb = SB2("xosb", [128, 4 * 2048], F32)
        owsb = SB2("owsb", [128, G * 2048], BF16)
        otmp = SB2("otmp", [128, 512], F32)
        drow = SB2("drow", [1, 512], F32)
        r1sb = SB2("r1sb", [128, 2 * 512], F32)
        psS2 = [PS2("p2S%d" % j, [128, 512]) for j in range(2)]
        psO = [PS2("p2O%d" % j, [128, 512]) for j in range(4)]
        psD = [PS2("p2D%d" % j, [33, 512]) for j in range(2)]

        p.dma("gpsimd", lambda e: e.dma_start(
            out=owsb[:].rearrange("p (a o) -> p a o", a=G),
            in_=dram["ow_w"].rearrange("(a p) o -> p a o", p=128)), w="owsb")

        def p2_block(i):
            nkc = 2 * (i + 1)
            p.dma("sync", lambda e, i=i: e.dma_start(
                out=qsb[:].rearrange("p (g t) -> p g t", g=G),
                in_=qd[:, i * 512:(i + 1) * 512].rearrange("(g p) t -> p g t", p=128)),
                r="qd", w="qsb")
            p.dma("sync", lambda e, i=i: e.dma_start(
                out=msb[:].rearrange("p (s t) -> p s t", s=8),
                in_=dram["mmask"][i].rearrange("s p t -> p s t")), w="msb")
            p.dma("sync", lambda e, i=i: e.dma_start(
                out=xosb[:].rearrange("p (a f) -> p a f", a=4),
                in_=dram["x_own"][i * 512:(i + 1) * 512, :]
                .rearrange("(a p) f -> p a f", p=128)), w="xosb")
            for hp in range(4):
                for kc in range(nkc):
                    j = kc % 2
                    p.dma("sync", lambda e, hp=hp, kc=kc, j=j: e.dma_start(
                        out=kb[j][0:64, :].rearrange("p (h t) -> p h t", h=4),
                        in_=kdN[hp * 256:(hp + 1) * 256, kc * 512:(kc + 1) * 512]
                        .rearrange("(h p) t -> p h t", p=64)),
                        r="kdN", w="kb%d" % j)
                    p.dma("sync", lambda e, hp=hp, kc=kc, j=j: e.dma_start(
                        out=kb[j][64:128, :].rearrange("p (h t) -> p h t", h=4),
                        in_=kdR[hp * 256:(hp + 1) * 256, kc * 512:(kc + 1) * 512]
                        .rearrange("(h p) t -> p h t", p=64)),
                        r="kdR", w="kb%d" % j)
                    p.dma("sync", lambda e, i=i, hp=hp, kc=kc, j=j: e.dma_start(
                        out=vb[j][:].rearrange("p (a v) -> p a v", a=4),
                        in_=vd[kc * 512:(kc + 1) * 512, hp * 512:(hp + 1) * 512]
                        .rearrange("(a p) v -> p a v", p=128)),
                        r="vd", w="vb%d" % j)
                    masked = kc >= nkc - 2

                    def emit_S(hi, ks):
                        h = hp * 4 + hi
                        sj = (hi * 4 + ks) % 2
                        pj = sj
                        skey, pkey = "p2S%d" % sj, "pb%d" % pj
                        blds = [lambda e, j=j, hi=hi, ks=ks, h=h, sj=sj: e.matmul(
                            psS2[sj][:],
                            kb[j][:, hi * 512 + ks * 128: hi * 512 + (ks + 1) * 128],
                            qsb[:, h * 512:(h + 1) * 512],
                            start=True, stop=True)]
                        if masked:
                            sidx = (kc - (nkc - 2)) * 4 + ks
                            blds.append(lambda e, sj=sj, sidx=sidx: e.matmul(
                                psS2[sj][:], identb[:],
                                msb[:, sidx * 512:(sidx + 1) * 512],
                                start=False, stop=True, skip_group_check=True))
                        p.group("tensor", blds,
                                r=("kb%d" % j, "qsb", "msb", "identb"), w=skey)
                        p.do("scalar", lambda e, sj=sj, pj=pj: e.activation(
                            pb[pj][:], psS2[sj][:], AF.Exp, scale=ISQ),
                            r=skey, w=pkey)

                    def emit_AVD(hi, ks):
                        pj = (hi * 4 + ks) % 2
                        pkey = "pb%d" % pj
                        first = (kc == 0 and ks == 0)
                        last = (kc == nkc - 1 and ks == 3)
                        p.group("tensor", [lambda e, j=j, hi=hi, ks=ks, pj=pj,
                                           first=first, last=last: e.matmul(
                            psO[hi][:],
                            vb[j][:, ks * 512 + hi * 128: ks * 512 + (hi + 1) * 128],
                            pb[pj][:], start=first, stop=last)],
                            r=("vb%d" % j, pkey), w="p2O%d" % hi)
                        p.group("tensor", [lambda e, hi=hi, pj=pj,
                                           first=first, last=last: e.matmul(
                            psD[hi // 2][(hi % 2) * 32:(hi % 2) * 32 + 1, :],
                            ones_colb[:], pb[pj][:], start=first, stop=last)],
                            r=("ones_colb", pkey), w="p2D%d" % hi)

                    prev = None
                    for hi in range(4):
                        for ks in range(4):
                            emit_S(hi, ks)
                            if prev is not None:
                                emit_AVD(*prev)
                            prev = (hi, ks)
                    emit_AVD(*prev)
                for hi in range(4):
                    h = hp * 4 + hi
                    p.do("scalar", lambda e, hi=hi: e.activation(
                        drow[:], psD[hi // 2][(hi % 2) * 32:(hi % 2) * 32 + 1, :],
                        AF.Ln), r="p2D%d" % hi, w="drow")
                    p.do("scalar", lambda e: e.activation(
                        drow[:], drow[:], AF.Exp, scale=-1.0), r="drow", w="drow")
                    margin("scalar")
                    p.group("tensor", [lambda e: e.matmul(
                        psS2[0][:], ones_row[:], drow[:], start=True, stop=True)],
                        r=("ones_row", "drow"), w="p2S0")
                    p.do("scalar", lambda e, hi=hi: e.activation(
                        otmp[:], psO[hi][:], AF.Copy), r="p2O%d" % hi, w="otmp")
                    p.do("vector", lambda e, h=h: e.tensor_tensor(
                        attnbf[:, h * 512:(h + 1) * 512], otmp[:], psS2[0][:],
                        ALU.mult), r=("otmp", "p2S0"), w="attnbf")
            for og in range(G):
                blds = [lambda e, og=og, fg=fg: e.matmul(
                    psO[og % 2][:],
                    owsb[:, fg * 2048 + og * 128: fg * 2048 + (og + 1) * 128],
                    attnbf[:, fg * 512:(fg + 1) * 512],
                    start=(fg == 0), stop=(fg == G - 1)) for fg in range(G)]
                for tsub in range(4):
                    blds.append(lambda e, og=og, tsub=tsub: e.matmul(
                        psO[og % 2][:, tsub * 128:(tsub + 1) * 128],
                        xosb[:, tsub * 2048 + og * 128: tsub * 2048 + (og + 1) * 128],
                        identf[:], start=False, stop=True,
                        skip_group_check=True))
                p.group("tensor", blds, r=("owsb", "attnbf", "xosb", "identf"),
                        w="p2O%d" % (og % 2))
                slot = "r1sb%d" % (og % 2)
                buf = r1sb[:, (og % 2) * 512:(og % 2 + 1) * 512]
                p.do("scalar", lambda e, og=og, buf=buf: e.activation(
                    buf, psO[og % 2][:], AF.Identity, bias=vecs["ow_b"][:, og:og + 1]),
                    r="p2O%d" % (og % 2), w=slot)
                p.dma("scalar", lambda e, og=og, buf=buf, i=i: e.dma_start(
                    out=resd[og * 128:(og + 1) * 128, i * 512:(i + 1) * 512], in_=buf),
                    r=slot, w="resd%d.%d" % (og, i))

        for i in range(4):
            p2_block(i)
        p.barrier()
        if "resid1" in dbg:
            p.dma("sync", lambda e: e.dma_start(out=dbg["resid1"], in_=resd),
                  r="resd", w="dbg_resid1")
            p.barrier()
        p2.close()

    p.barrier()

    dhalf = nc.dram_tensor("dhalf", [2048, 1024], F32, kind="Internal").ap()

    def emit_ffn_phase(final):
        pf = ExitStack()
        sfx = "_f1" if final else "_f0"
        SBF = lambda name, shape, dt: pf.enter_context(
            nc.sbuf_tensor(name + sfx, shape, dt))
        PSF = lambda name, shape, dt=F32: pf.enter_context(
            nc.psum_tensor(name + sfx, shape, dt))
        xlnf = SBF("xlnf", [128, G * 1024], BF16)
        fusef = SBF("fusef", [128, 32 * 1024], BF16)
        wsec = SBF("wsec", [128, G * 256], BF16)
        wsec2 = SBF("wsec2", [128, G * 256], BF16)
        wsec3 = SBF("wsec3", [128, G * 256], BF16)
        wsec4 = SBF("wsec4", [128, G * 256], BF16)
        wdn = SBF("wdn", [128, 8192], BF16)
        wdn2 = SBF("wdn2", [128, 8192], BF16)
        rx = [SBF("rx%d" % j, [128, 1024], F32) for j in range(2)]
        junk2 = SBF("junk2", [128, 1024], F32)
        xtmp = SBF("xtmp", [128, 512], F32)
        gbf = SBF("gbf", [128, 512], BF16)
        ubf = SBF("ubf", [128, 512], BF16)
        r2s = SBF("r2s", [128, 512], F32)
        prowt = SBF("prowt", [1, 1536], F32)
        prow = {"mu": prowt[0:1, 0:512], "rs": prowt[0:1, 512:1024],
                "t": prowt[0:1, 1024:1536]}
        psF = [PSF("pf%d" % j, [128, 512]) for j in range(8)]

        def ffn_group(tg):
            t0c = tg * 1024
            for g in range(G):
                j = g % 2
                p.dma("sync", lambda e, g=g, t0c=t0c, j=j: e.dma_start(
                    out=rx[j][:], in_=resd[g * 128:(g + 1) * 128, t0c:t0c + 1024]),
                    r=("resd%d.%d" % (g, t0c // 512), "resd%d.%d" % (g, t0c // 512 + 1)),
                    w="rx%d" % j)
                for sub in range(2):
                    p.group("tensor", [lambda e, j=j, sub=sub, g=g: e.matmul(
                        psF[sub][0:1, :], ones_col[:],
                        rx[j][:, sub * 512:(sub + 1) * 512],
                        start=(g == 0), stop=(g == G - 1))],
                        r=("rx%d" % j, "ones_col"), w="pf%d" % sub)
                p.do("scalar", lambda e, j=j: e.activation(
                    junk2[:], rx[j][:], AF.Square), r="rx%d" % j, w="junk2")
                for sub in range(2):
                    p.group("tensor", [lambda e, sub=sub, g=g: e.matmul(
                        psF[2 + sub][0:1, :], ones_col[:],
                        junk2[:, sub * 512:(sub + 1) * 512],
                        start=(g == 0), stop=(g == G - 1))],
                        r=("junk2", "ones_col"), w="pf%d" % (2 + sub))
            for sub in range(2):
                mu, rs, t_ = prow["mu"], prow["rs"], prow["t"]
                p.do("vector", lambda e, mu=mu, sub=sub: e.tensor_scalar(
                    mu, psF[sub][0:1, :], 1.0 / D, None, ALU.mult),
                    r="pf%d" % sub, w="f.mu")
                p.do("vector", lambda e, t_=t_, sub=sub: e.tensor_scalar(
                    t_, psF[2 + sub][0:1, :], 1.0 / D, None, ALU.mult),
                    r="pf%d" % (2 + sub), w="f.t")
                p.do("vector", lambda e, mu=mu: e.tensor_tensor(
                    junk2[0:1, 0:512], mu, mu, ALU.mult),
                    r="f.mu", w="junk2")
                p.do("vector", lambda e, t_=t_: e.tensor_tensor(
                    t_, t_, junk2[0:1, 0:512], ALU.subtract),
                    r=("f.t", "junk2"), w="f.t")
                margin("vector")
                p.do("scalar", lambda e, t_=t_: e.activation(
                    t_, t_, AF.Ln, bias=eps_ln[0:1, :]),
                    r="f.t", w="f.t")
                p.do("scalar", lambda e, rs=rs, t_=t_: e.activation(
                    rs, t_, AF.Exp, scale=-0.5), r="f.t", w="f.rs")
                margin("scalar")
                p.group("tensor", [lambda e, mu=mu: e.matmul(
                    psF[4][:], ones_row[:], mu, start=True, stop=True)],
                    r=("ones_row", "f.mu"), w="pf4")
                p.group("tensor", [lambda e, rs=rs: e.matmul(
                    psF[5][:], ones_row[:], rs, start=True, stop=True)],
                    r=("ones_row", "f.rs"), w="pf5")
                for g in range(G):
                    j = g % 2
                    p.dma("sync", lambda e, g=g, t0c=t0c, j=j, sub=sub: e.dma_start(
                        out=rx[j][:, 0:512],
                        in_=resd[g * 128:(g + 1) * 128,
                                 t0c + sub * 512:t0c + (sub + 1) * 512]),
                        r="resd%d.%d" % (g, t0c // 512 + sub), w="rx%d" % j)
                    p.do("vector", lambda e, j=j: e.tensor_tensor(
                        xtmp[:], rx[j][:, 0:512], psF[4][:], ALU.subtract),
                        r=("rx%d" % j, "pf4"), w="xtmp")
                    p.do("vector", lambda e, g=g, sub=sub: e.tensor_tensor(
                        xlnf[:, g * 1024 + sub * 512: g * 1024 + (sub + 1) * 512],
                        xtmp[:], psF[5][:], ALU.mult),
                        r=("xtmp", "pf5"), w="xlnf%d" % sub)
            for half in range(2):
                for ws in range(16):
                    c0 = half * 4096 + ws * 256
                    jj = ws % 2
                    wg = wsec if jj == 0 else wsec2
                    wu = wsec3 if jj == 0 else wsec4
                    gk, uk = "wsec" if jj == 0 else "wsec2", "wsec3" if jj == 0 else "wsec4"
                    p.dma("gpsimd", lambda e, c0=c0, wg=wg: e.dma_start(
                        out=wg[:].rearrange("p (a o) -> p a o", a=G),
                        in_=dram["fg_w"][:, c0:c0 + 256]
                        .rearrange("(a p) o -> p a o", p=128)), w=gk)
                    p.dma("gpsimd", lambda e, c0=c0, wu=wu: e.dma_start(
                        out=wu[:].rearrange("p (a o) -> p a o", a=G),
                        in_=dram["fu_w"][:, c0:c0 + 256]
                        .rearrange("(a p) o -> p a o", p=128)), w=uk)
                    for igl in range(2):
                        ig = ws * 2 + igl
                        iabs = half * 32 + ig
                        for sub in range(2):
                            p.group("tensor", [lambda e, igl=igl, fg=fg, sub=sub,
                                               wg=wg: e.matmul(
                                psF[6][:],
                                wg[:, fg * 256 + igl * 128: fg * 256 + (igl + 1) * 128],
                                xlnf[:, fg * 1024 + sub * 512: fg * 1024 + (sub + 1) * 512],
                                start=(fg == 0), stop=(fg == G - 1)) for fg in range(G)],
                                r=(gk, "xlnf%d" % sub), w="pf6")
                            p.do("scalar", lambda e, sub=sub, iabs=iabs: e.activation(
                                gbf[:], psF[6][:], AF.Silu,
                                scale=vecs["sgate"][:, iabs:iabs + 1]),
                                r="pf6", w="gbf")
                            p.group("tensor", [lambda e, igl=igl, fg=fg, sub=sub,
                                               wu=wu: e.matmul(
                                psF[7][:],
                                wu[:, fg * 256 + igl * 128: fg * 256 + (igl + 1) * 128],
                                xlnf[:, fg * 1024 + sub * 512: fg * 1024 + (sub + 1) * 512],
                                start=(fg == 0), stop=(fg == G - 1)) for fg in range(G)],
                                r=(uk, "xlnf%d" % sub), w="pf7")
                            p.do("scalar", lambda e, sub=sub, iabs=iabs: e.activation(
                                ubf[:], psF[7][:], AF.Identity,
                                scale=vecs["sup"][:, iabs:iabs + 1]),
                                r="pf7", w="ubf")
                            p.do("vector", lambda e, ig=ig, sub=sub: e.tensor_tensor(
                                fusef[:, ig * 1024 + sub * 512: ig * 1024 + (sub + 1) * 512],
                                gbf[:], ubf[:], ALU.mult),
                                r=("gbf", "ubf"), w="fusef")
                for osec in range(8):
                    oc0 = osec * 256
                    wkey = "wdn" if osec % 2 == 0 else "wdn2"
                    wbuf = wdn if osec % 2 == 0 else wdn2
                    p.dma("gpsimd", lambda e, half=half, oc0=oc0, wbuf=wbuf: e.dma_start(
                        out=wbuf[:].rearrange("p (a o) -> p a o", a=32),
                        in_=dram["fd_w"][half * 4096:(half + 1) * 4096, oc0:oc0 + 256]
                        .rearrange("(a p) o -> p a o", p=128)), w=wkey)
                    for ogl in range(2):
                        og = osec * 2 + ogl
                        for sub in range(2):
                            pd = "pf%d" % (sub % 2)
                            p.group("tensor", [lambda e, ogl=ogl, ig=ig, sub=sub,
                                               wbuf=wbuf: e.matmul(
                                psF[sub % 2][:],
                                wbuf[:, ig * 256 + ogl * 128: ig * 256 + (ogl + 1) * 128],
                                fusef[:, ig * 1024 + sub * 512: ig * 1024 + (sub + 1) * 512],
                                start=(ig == 0), stop=(ig == 31)) for ig in range(32)],
                                r=(wkey, "fusef"), w=pd)
                            if half == 0:
                                p.do("vector", lambda e, sub=sub: e.tensor_scalar(
                                    r2s[:], psF[sub % 2][:],
                                    1.0, None, ALU.mult), r=pd, w="r2s")
                                p.dma("scalar", lambda e, og=og, sub=sub: e.dma_start(
                                    out=dhalf[og * 128:(og + 1) * 128,
                                              sub * 512:(sub + 1) * 512],
                                    in_=r2s[:]),
                                    r="r2s", w="dhalf%d" % og)
                            else:
                                p.dma("sync", lambda e, og=og, sub=sub: e.dma_start(
                                    out=xtmp[:],
                                    in_=dhalf[og * 128:(og + 1) * 128,
                                              sub * 512:(sub + 1) * 512]),
                                    r="dhalf%d" % og, w="xtmp")
                                p.do("vector", lambda e, sub=sub: e.tensor_tensor(
                                    xtmp[:], xtmp[:], psF[sub % 2][:], ALU.add),
                                    r=("xtmp", pd), w="xtmp")
                                p.dma("sync", lambda e, og=og, sub=sub, tg=tg: e.dma_start(
                                    out=rx[sub][:, 0:512],
                                    in_=resd[og * 128:(og + 1) * 128,
                                             tg * 1024 + sub * 512:
                                             tg * 1024 + (sub + 1) * 512]),
                                    r="resd%d.%d" % (og, 2 * tg + sub), w="rx%d" % sub)
                                p.do("vector", lambda e, og=og, sub=sub: e.scalar_tensor_tensor(
                                    r2s[:], xtmp[:],
                                    vecs["sdown"][:, og:og + 1], rx[sub][:, 0:512],
                                    ALU.mult, ALU.add),
                                    r=("xtmp", "rx%d" % sub), w="r2s")
                                p.dma("scalar", lambda e, og=og, sub=sub, tg=tg:
                                      e.dma_start(
                                    out=resd[og * 128:(og + 1) * 128,
                                             tg * 1024 + sub * 512:
                                             tg * 1024 + (sub + 1) * 512],
                                    in_=r2s[:]),
                                    r="r2s", w="resd%d.%d" % (og, 2 * tg + sub))

        for tg in range(2):
            ffn_group(tg)
        p.barrier()
        pf.close()

    if do_p3:
        emit_ffn_phase(final=False)
        if "resid2" in dbg:
            p.dma("sync", lambda e: e.dma_start(out=dbg["resid2"], in_=resd),
                  r="resd0.0", w="dbg_resid2")
            p.barrier()

    if do_p4:
        p4 = ExitStack()
        SB4 = lambda name, shape, dt: p4.enter_context(nc.sbuf_tensor(name, shape, dt))
        PS4 = lambda name, shape, dt=F32: p4.enter_context(nc.psum_tensor(name, shape, dt))
        xin4 = SB4("xin4", [128, G * 512], BF16)
        xmix = SB4("xmix", [128, G * 512], BF16)
        powt = SB4("powt", [128, G * 512], F32)
        abuf = SB4("abuf", [128, G * 512], F32)
        pbuf = SB4("pbuf", [128, G * 512], F32)
        rbuf = SB4("rbuf", [128, G * 512], BF16)
        ltmp = SB4("ltmp", [128, 512], F32)
        yN = SB4("yN", [128, 513], F32)
        yD = SB4("yD", [128, 513], F32)
        w4 = [SB4("w4_%d" % j, [128, G * 256], BF16) for j in range(2)]
        rx4 = [SB4("rx4_%d" % j, [128, 512], F32) for j in range(2)]
        r3s = SB4("r3s", [128, 2 * 512], F32)
        junk4 = SB4("junk4", [128, 512], F32)
        e1sb = SB4("e1sb", [128, 64], F32)
        e1pick = SB4("e1pick", [128, 64], F32)
        e2sb = SB4("e2sb", [128, 32], F32)
        e2all = SB4("e2all", [128, 256], F32)
        sce = SB4("sce", [128, 32], F32)
        small = SB4("small", [128, 16 * 16], F32)
        SM = {nm: small[:, i * 16:(i + 1) * 16] for i, nm in enumerate(
            ("dcol", "wcol", "onemk", "onemv", "onemr", "d512", "carryN",
             "carryD", "prevcol", "sxc", "d1024a", "d1024b", "d512a",
             "d512b", "tmp1", "tmp2"))}
        gst = SB4("gst", [128, 4 * 32], F32)
        GQ = {nm: gst[:, i * 32:(i + 1) * 32] for i, nm in enumerate(
            ("Gcur", "Gprev", "Gd", "Gam"))}
        prow4 = SB4("prow4", [1, 1536], F32)
        pr4 = {"mu0": prow4[0:1, 0:512], "rs0": prow4[0:1, 512:1024],
               "t0": prow4[0:1, 1024:1536]}
        ps4 = [PS4("p4%d" % j, [128, 512]) for j in range(6)]

        p.do("vector", lambda e: e.memset(small[:], 0.0),
             w=tuple("sm." + nm for nm in (
                 "dcol", "wcol", "onemk", "onemv", "onemr", "d512", "carryN",
                 "carryD", "prevcol", "sxc", "d1024a", "d1024b", "d512a",
                 "d512b", "tmp1", "tmp2")))
        p.do("scalar", lambda e: e.activation(SM["dcol"], vecs["td"][:], AF.Exp),
             r="v_td", w="sm.dcol")
        p.do("scalar", lambda e: e.activation(SM["dcol"], SM["dcol"], AF.Exp,
             scale=-1.0), r="sm.dcol", w="sm.dcol")
        p.do("scalar", lambda e: e.activation(SM["wcol"], vecs["cw"][:], AF.Exp),
             r="v_cw", w="sm.wcol")
        for nm, src in (("onemk", "mixk"), ("onemv", "mixv"), ("onemr", "mixr")):
            p.do("vector", lambda e, nm=nm, src=src: e.tensor_scalar(
                SM[nm], vecs[src][:], -1.0, 1.0, ALU.mult, ALU.add),
                r="v_" + src, w="sm." + nm)
        p.do("vector", lambda e: e.memset(junk4[:], 0.0), w="junk4")
        p.do("vector", lambda e: e.memset(junk4[:, 0:1], 1.0), r="junk4", w="junk4")
        for g in range(G):
            p.do("vector", lambda e, g=g: e.tensor_tensor_scan(
                powt[:, g * 512:(g + 1) * 512],
                SM["dcol"][:, g:g + 1].broadcast_to([128, 512]),
                junk4[:], 0.0, ALU.mult, ALU.add),
                r=("sm.dcol", "junk4"), w="powt")
        for g in range(G):
            p.do("vector", lambda e, g=g: e.tensor_scalar(
                SM["d512"][:, g:g + 1], powt[:, g * 512 + 511:g * 512 + 512],
                SM["dcol"][:, g:g + 1], None, ALU.mult),
                r=("powt", "sm.dcol"), w="sm.d512")
        p.do("vector", lambda e: e.tensor_copy(SM["d512a"], SM["d512"]),
             r="sm.d512", w="sm.d512a")
        p.do("vector", lambda e: e.tensor_copy(SM["d512b"], SM["d512"]),
             r="sm.d512", w="sm.d512b")
        p.do("vector", lambda e: e.tensor_tensor(
            SM["d1024a"], SM["d512"], SM["d512"], ALU.mult),
            r="sm.d512", w="sm.d1024a")
        p.do("vector", lambda e: e.tensor_copy(SM["d1024b"], SM["d1024a"]),
             r="sm.d1024a", w="sm.d1024b")
        p.do("vector", lambda e: e.memset(gst[:], 0.0),
             w=("g.Gcur", "g.Gprev", "g.Gd", "g.Gam"))

        def p4_prep(cI):
            t0c = cI * 512
            sb0, sb1 = (0, 1) if cI % 2 == 0 else (4, 5)
            for g in range(G):
                j = g % 2
                p.dma("sync", lambda e, g=g, t0c=t0c, j=j: e.dma_start(
                    out=rx4[j][:], in_=resd[g * 128:(g + 1) * 128, t0c:t0c + 512]),
                    r="resd%d.%d" % (g, t0c // 512), w="rx4_%d" % j)
                p.group("tensor", [lambda e, j=j, g=g, sb0=sb0: e.matmul(
                    ps4[sb0][0:1, :], ones_col[:], rx4[j][:],
                    start=(g == 0), stop=(g == G - 1))],
                    r=("rx4_%d" % j, "ones_col"), w="p4%d" % sb0)
                p.do("scalar", lambda e, j=j: e.activation(
                    junk4[:], rx4[j][:], AF.Square), r="rx4_%d" % j, w="junk4")
                p.group("tensor", [lambda e, g=g, sb1=sb1: e.matmul(
                    ps4[sb1][0:1, :], ones_col[:], junk4[:],
                    start=(g == 0), stop=(g == G - 1))],
                    r=("junk4", "ones_col"), w="p4%d" % sb1)
            p.do("vector", lambda e, sb0=sb0: e.tensor_scalar(
                pr4["mu0"], ps4[sb0][0:1, :], 1.0 / D, None, ALU.mult),
                r="p4%d" % sb0, w="p4.mu")
            p.do("vector", lambda e, sb1=sb1: e.tensor_scalar(
                pr4["t0"], ps4[sb1][0:1, :], 1.0 / D, None, ALU.mult),
                r="p4%d" % sb1, w="p4.t")
            p.do("vector", lambda e: e.tensor_tensor(
                junk4[0:1, :], pr4["mu0"], pr4["mu0"], ALU.mult),
                r="p4.mu", w="junk4")
            p.do("vector", lambda e: e.tensor_tensor(
                pr4["t0"], pr4["t0"], junk4[0:1, :], ALU.subtract),
                r=("p4.t", "junk4"), w="p4.t")
            margin("vector")
            p.do("scalar", lambda e: e.activation(
                pr4["t0"], pr4["t0"], AF.Ln, bias=eps_ln[0:1, :]),
                r="p4.t", w="p4.t")
            p.do("scalar", lambda e: e.activation(
                pr4["rs0"], pr4["t0"], AF.Exp, scale=-0.5), r="p4.t", w="p4.rs")
            margin("scalar")
            p.group("tensor", [lambda e: e.matmul(
                ps4[2][:], ones_row[:], pr4["mu0"], start=True, stop=True)],
                r=("ones_row", "p4.mu"), w="p42")
            p.group("tensor", [lambda e: e.matmul(
                ps4[3][:], ones_row[:], pr4["rs0"], start=True, stop=True)],
                r=("ones_row", "p4.rs"), w="p43")
            for g in range(G):
                j = g % 2
                p.dma("sync", lambda e, g=g, t0c=t0c, j=j: e.dma_start(
                    out=rx4[j][:], in_=resd[g * 128:(g + 1) * 128, t0c:t0c + 512]),
                    r="resd%d.%d" % (g, t0c // 512), w="rx4_%d" % j)
                p.do("vector", lambda e, j=j: e.tensor_tensor(
                    junk4[:], rx4[j][:], ps4[2][:], ALU.subtract),
                    r=("rx4_%d" % j, "p42"), w="junk4")
                p.do("vector", lambda e: e.tensor_tensor(
                    junk4[:], junk4[:], ps4[3][:], ALU.mult),
                    r=("junk4", "p43"), w="junk4")
                p.do("scalar", lambda e, g=g: e.activation(
                    xin4[:, g * 512:(g + 1) * 512], junk4[:], AF.Identity,
                    scale=vecs["lnw"][:, g:g + 1], bias=vecs["lnb"][:, g:g + 1]),
                    r="junk4", w="xin4")
                p.do("vector", lambda e, g=g, cI=cI: e.tensor_copy(
                    e1sb[:, cI * 16 + g: cI * 16 + g + 1],
                    xin4[:, g * 512 + 511: g * 512 + 512]),
                    r="xin4", w="e1sb")
                p.do("vector", lambda e, g=g, j=j: e.tensor_tensor(
                    rx4[j][:], rx4[j][:], xin4[:, g * 512:(g + 1) * 512], ALU.add),
                    r=("rx4_%d" % j, "xin4"), w="rx4_%d" % j)
                p.dma("scalar", lambda e, g=g, t0c=t0c, j=j: e.dma_start(
                    out=resd[g * 128:(g + 1) * 128, t0c:t0c + 512], in_=rx4[j][:]),
                    r="rx4_%d" % j, w="resd%d.%d" % (g, t0c // 512))
            p.dma("scalar", lambda e, cI=cI: e.dma_start(
                out=xind[:, cI * 512:(cI + 1) * 512]
                .rearrange("(g p) t -> p g t", p=128),
                in_=xin4[:].rearrange("p (g t) -> p g t", g=G)),
                r="xin4", w="xind")

        for cI in range(4):
            p4_prep(cI)

        p.dma("gpsimd", lambda e: e.dma_start(out=e1i.ap(), in_=e1sb[:]),
              r="e1sb", w="e1i")
        p.do("gpsimd", lambda e: e.collective_compute(
            "AllGather", ALU.bypass, replica_groups=[list(range(NCORES))],
            ins=[e1i.ap().opt()], outs=[e1o.ap().opt()]),
            inc="CC", r="e1i", w="e1o")
        e1all = abuf[:, 0:512]
        p.dma("gpsimd", lambda e: e.dma_start(
            out=e1all.rearrange("p (r j) -> p r j", r=NCORES),
            in_=e1o.ap().rearrange("(r p) j -> p r j", p=128)),
            r="e1o", w="abuf")
        for cI in range(4):
            p.dma("sync", lambda e, cI=cI: e.dma_start(
                out=junk4[:], in_=dram["sxsel"][cI]), w="junk4")
            p.do("vector", lambda e, cI=cI: e.tensor_tensor(
                ltmp[:, 0:512], e1all, junk4[:], ALU.mult),
                r=("abuf", "junk4"), w="ltmp")
            p.do("vector", lambda e, cI=cI: e.reduce_sum(
                e1pick[:, cI * 16:(cI + 1) * 16],
                ltmp[:, 0:512].rearrange("p (r j g) -> p g j r", r=8, j=4),
                axis=AX.XY), r="ltmp", w="e1pick")

        def p4_front(cI):
            p.dma("sync", lambda e, cI=cI: e.dma_start(
                out=xin4[:].rearrange("p (g t) -> p g t", g=G),
                in_=xind[:, cI * 512:(cI + 1) * 512]
                .rearrange("(g p) t -> p g t", p=128)),
                r="xind", w="xin4")
            p.do("vector", lambda e, cI=cI: e.scalar_tensor_tensor(
                SM["sxc"], SM["prevcol"], vecs["sxown"][:, cI:cI + 1],
                e1pick[:, cI * 16:(cI + 1) * 16], ALU.mult, ALU.add),
                r=("sm.prevcol", "v_sxown", "e1pick"), w="sm.sxc")
            emit_mix(xmix, "xmix", "mixk", "onemk")

        def emit_mix(dstbuf, dstkey, mixv_, onem, extraw=()):
            for g in range(G):
                wk = (dstkey,) + tuple(e % g for e in extraw)
                p.do("vector", lambda e, g=g, mixv_=mixv_, dstbuf=dstbuf: e.tensor_scalar(
                    dstbuf[:, g * 512 + 1: (g + 1) * 512],
                    xin4[:, g * 512 + 1: (g + 1) * 512],
                    vecs[mixv_][:, g:g + 1], None, ALU.mult),
                    r=("xin4", "v_" + mixv_), w=wk)
                p.do("vector", lambda e, g=g, onem=onem, dstbuf=dstbuf: e.scalar_tensor_tensor(
                    dstbuf[:, g * 512 + 1: (g + 1) * 512],
                    xin4[:, g * 512: (g + 1) * 512 - 1],
                    SM[onem][:, g:g + 1],
                    dstbuf[:, g * 512 + 1: (g + 1) * 512], ALU.mult, ALU.add),
                    r=("xin4", "sm." + onem, dstkey), w=dstkey)
                p.do("vector", lambda e, g=g, mixv_=mixv_, dstbuf=dstbuf: e.tensor_scalar(
                    dstbuf[:, g * 512: g * 512 + 1],
                    xin4[:, g * 512: g * 512 + 1],
                    vecs[mixv_][:, g:g + 1], None, ALU.mult),
                    r=("xin4", "v_" + mixv_), w=dstkey)
                p.do("vector", lambda e, g=g, onem=onem, dstbuf=dstbuf: e.scalar_tensor_tensor(
                    dstbuf[:, g * 512: g * 512 + 1],
                    SM["sxc"][:, g:g + 1], SM[onem][:, g:g + 1],
                    dstbuf[:, g * 512: g * 512 + 1], ALU.mult, ALU.add),
                    r=("sm.sxc", "sm." + onem, dstkey), w=dstkey)

        def emit_mats(W, srcbuf, srckey, evac, wb=None, wn="w4_%d",
                      secs=range(8)):
            if wb is None:
                wb = w4
            for osec in secs:
                jj = osec % 2
                p.dma("gpsimd" if jj == 0 else "sync",
                      lambda e, W=W, osec=osec, jj=jj, wb=wb,
                      wn=wn: e.dma_start(
                    out=wb[jj][:].rearrange("p (a o) -> p a o", a=G),
                    in_=dram[W][:, osec * 256:(osec + 1) * 256]
                    .rearrange("(a p) o -> p a o", p=128)), w=wn % jj)
                for ogl in range(2):
                    og = osec * 2 + ogl
                    pj = og % 2
                    p.group("tensor", [lambda e, ogl=ogl, fg=fg, jj=jj, pj=pj,
                                       srcbuf=srcbuf, wb=wb: e.matmul(
                        ps4[pj][:],
                        wb[jj][:, fg * 256 + ogl * 128: fg * 256 + (ogl + 1) * 128],
                        srcbuf[:, fg * 512:(fg + 1) * 512],
                        start=(fg == 0), stop=(fg == G - 1)) for fg in range(G)],
                        r=(wn % jj, srckey), w="p4%d" % pj)
                    if evac == "k":
                        p.do("scalar", lambda e, og=og, pj=pj: e.activation(
                            abuf[:, og * 512:(og + 1) * 512], ps4[pj][:], AF.Exp),
                            r="p4%d" % pj, w="abuf%d" % og)
                    elif evac == "v":
                        p.do("scalar", lambda e, og=og, pj=pj: e.activation(
                            pbuf[:, og * 512:(og + 1) * 512], ps4[pj][:], AF.Copy),
                            r="p4%d" % pj, w="pbuf%d" % og)
                    else:
                        p.do("scalar", lambda e, og=og, pj=pj: e.activation(
                            rbuf[:, og * 512:(og + 1) * 512], ps4[pj][:],
                            AF.Sigmoid), r="p4%d" % pj,
                            w=("rbuf%d" % og, "rbufmix"))

        def p4_scans(cI):
            for g in range(G):
                sl = slice(g * 512, (g + 1) * 512)
                p.do("vector", lambda e, sl=sl, g=g: e.tensor_tensor(
                    pbuf[:, sl], abuf[:, sl], pbuf[:, sl], ALU.mult),
                    r=("abuf%d" % g, "pbuf%d" % g), w="pbuf%d" % g)
            for g in range(G):
                sl = slice(g * 512, (g + 1) * 512)
                gc = slice(g, g + 1)
                p.do("vector", lambda e, gc=gc, cI=cI: e.tensor_scalar(
                    yN[:, 0:1], SM["carryN"][:, gc], vecs["carrymask"][:, cI:cI + 1],
                    None, ALU.mult), r=("sm.carryN", "v_carrymask"), w="yN")
                p.do("vector", lambda e, gc=gc, cI=cI: e.tensor_scalar(
                    yD[:, 0:1], SM["carryD"][:, gc], vecs["carrymask"][:, cI:cI + 1],
                    None, ALU.mult), r=("sm.carryD", "v_carrymask"), w="yD")
                p.do("vector", lambda e, sl=sl, gc=gc, g=g: e.tensor_tensor_scan(
                    yN[:, 1:513], SM["dcol"][:, gc].broadcast_to([128, 512]),
                    pbuf[:, sl], yN[:, 0:1], ALU.mult, ALU.add),
                    r=("sm.dcol", "pbuf%d" % g, "yN"), w="yN")
                p.do("vector", lambda e, sl=sl, gc=gc, g=g: e.tensor_tensor_scan(
                    yD[:, 1:513], SM["dcol"][:, gc].broadcast_to([128, 512]),
                    abuf[:, sl], yD[:, 0:1], ALU.mult, ALU.add),
                    r=("sm.dcol", "abuf%d" % g, "yD"), w="yD")
                p.do("vector", lambda e, gc=gc: e.tensor_copy(
                    SM["carryN"][:, gc], yN[:, 512:513]), r="yN", w="sm.carryN")
                p.do("vector", lambda e, gc=gc: e.tensor_copy(
                    SM["carryD"][:, gc], yD[:, 512:513]), r="yD", w="sm.carryD")
                p.do("vector", lambda e, sl=sl, gc=gc, g=g: e.scalar_tensor_tensor(
                    pbuf[:, sl], pbuf[:, sl], SM["wcol"][:, gc], yN[:, 0:512],
                    ALU.mult, ALU.add), r=("pbuf%d" % g, "sm.wcol", "yN"),
                    w="pbuf%d" % g)
                p.do("vector", lambda e, sl=sl, gc=gc, g=g: e.scalar_tensor_tensor(
                    abuf[:, sl], abuf[:, sl], SM["wcol"][:, gc], yD[:, 0:512],
                    ALU.mult, ALU.add), r=("abuf%d" % g, "sm.wcol", "yD"),
                    w="abuf%d" % g)

        def p4_exchange(cI):
            p.do("vector", lambda e: e.tensor_copy(e2sb[:, 0:16], SM["carryN"]),
                 r="sm.carryN", w="e2sb")
            p.do("vector", lambda e: e.tensor_copy(e2sb[:, 16:32], SM["carryD"]),
                 r="sm.carryD", w="e2sb")
            margin("vector")
            p.dma("gpsimd", lambda e: e.dma_start(out=e2i.ap(), in_=e2sb[:]),
                  r="e2sb", w="e2i")
            p.do("gpsimd", lambda e: e.collective_compute(
                "AllGather", ALU.bypass, replica_groups=[list(range(NCORES))],
                ins=[e2i.ap().opt()], outs=[e2o.ap().opt()]),
                inc="CC", r="e2i", w="e2o")
            p.dma("gpsimd", lambda e: e.dma_start(
                out=e2all[:].rearrange("p (r j) -> p r j", r=NCORES),
                in_=e2o.ap().rearrange("(r p) j -> p r j", p=128)),
                r="e2o", w="e2all")

        def p4_gamma(cI):
            p.dma("sync", lambda e, cI=cI: e.dma_start(
                out=junk4[:, 0:256], in_=dram["mselS"][cI]), r="junk4", w="junk4")
            p.do("vector", lambda e, cI=cI: e.tensor_tensor(
                ltmp[:, 0:256], e2all[:], junk4[:, 0:256],
                ALU.mult), r=("e2all", "junk4"), w="ltmp")
            p.do("vector", lambda e: e.reduce_sum(
                sce[:], ltmp[:, 0:256].rearrange("p (r j) -> p j r", r=8),
                axis=AX.X), r="ltmp", w="sce")
            p.do("vector", lambda e: e.tensor_copy(GQ["Gprev"], GQ["Gcur"]),
                 r="g.Gcur", w="g.Gprev")
            p.do("vector", lambda e: e.tensor_tensor(
                GQ["Gcur"], GQ["Gprev"], small[:, 10 * 16:12 * 16],
                ALU.mult), r=("g.Gprev", "sm.d1024a", "sm.d1024b"), w="g.Gcur")
            p.do("vector", lambda e: e.tensor_tensor(
                GQ["Gcur"], GQ["Gcur"], sce[:], ALU.add),
                r=("g.Gcur", "sce"), w="g.Gcur")
            p.do("vector", lambda e: e.tensor_tensor(
                GQ["Gd"], GQ["Gprev"], small[:, 12 * 16:14 * 16], ALU.mult),
                r=("g.Gprev", "sm.d512a", "sm.d512b"), w="g.Gd")
            p.do("vector", lambda e, cI=cI: e.tensor_scalar(
                GQ["Gam"], GQ["Gcur"], vecs["gmask_a"][:, cI:cI + 1], None,
                ALU.mult), r=("g.Gcur", "v_gmask_a"), w="g.Gam")
            p.do("vector", lambda e, cI=cI: e.scalar_tensor_tensor(
                GQ["Gam"], GQ["Gd"], vecs["gmask_e"][:, cI:cI + 1], GQ["Gam"],
                ALU.mult, ALU.add), r=("g.Gd", "v_gmask_e", "g.Gam"), w="g.Gam")

        def p4_ratio(cI, gs=range(16)):
            for g in gs:
                sl = slice(g * 512, (g + 1) * 512)
                p.do("vector", lambda e, sl=sl, g=g: e.scalar_tensor_tensor(
                    pbuf[:, sl], powt[:, sl], GQ["Gam"][:, g:g + 1], pbuf[:, sl],
                    ALU.mult, ALU.add), r=("powt", "g.Gam", "pbuf%d" % g),
                    w="pbuf%d" % g)
                p.do("vector", lambda e, sl=sl, g=g: e.scalar_tensor_tensor(
                    abuf[:, sl], powt[:, sl], GQ["Gam"][:, 16 + g:17 + g],
                    abuf[:, sl], ALU.mult, ALU.add),
                    r=("powt", "g.Gam", "abuf%d" % g), w="abuf%d" % g)
                p.do("scalar", lambda e, sl=sl, g=g: e.activation(
                    ltmp[:], abuf[:, sl], AF.Ln), r="abuf%d" % g, w="ltmp")
                p.do("scalar", lambda e, sl=sl, g=g: e.activation(
                    abuf[:, sl], ltmp[:], AF.Exp, scale=-1.0), r="ltmp",
                    w="abuf%d" % g)
                p.do("vector", lambda e, sl=sl, g=g: e.tensor_tensor(
                    pbuf[:, sl], pbuf[:, sl], abuf[:, sl], ALU.mult),
                    r=("pbuf%d" % g, "abuf%d" % g), w="pbuf%d" % g)
                p.do("vector", lambda e, sl=sl, g=g: e.tensor_tensor(
                    rbuf[:, sl], rbuf[:, sl], pbuf[:, sl], ALU.mult),
                    r=("rbuf%d" % g, "pbuf%d" % g), w="rbuf%d" % g)

        def p4_wo(cI):
            for osec in range(8):
                jj = osec % 2
                p.dma("gpsimd" if jj == 0 else "sync",
                      lambda e, osec=osec, jj=jj: e.dma_start(
                    out=w4[jj][:].rearrange("p (a o) -> p a o", a=G),
                    in_=dram["wo_w"][:, osec * 256:(osec + 1) * 256]
                    .rearrange("(a p) o -> p a o", p=128)), w="w4_%d" % jj)
                for ogl in range(2):
                    og = osec * 2 + ogl
                    pj = og % 2
                    blds = [lambda e, ogl=ogl, fg=fg, jj=jj, pj=pj: e.matmul(
                        ps4[pj][:],
                        w4[jj][:, fg * 256 + ogl * 128: fg * 256 + (ogl + 1) * 128],
                        rbuf[:, fg * 512:(fg + 1) * 512],
                        start=(fg == 0), stop=(fg == G - 1)) for fg in range(G)]
                    p.group("tensor", blds,
                            r=tuple("rbuf%d" % g for g in range(G)) +
                              ("w4_%d" % jj,),
                            w="p4%d" % pj)
                    p.dma("sync", lambda e, og=og, cI=cI, pj=pj: e.dma_start(
                        out=rx4[pj][:],
                        in_=resd[og * 128:(og + 1) * 128, cI * 512:(cI + 1) * 512]),
                        r="resd%d.%d" % (og, cI), w="rx4_%d" % pj)
                    p.do("vector", lambda e, og=og, pj=pj: e.tensor_tensor(
                        r3s[:, pj * 512:(pj + 1) * 512], rx4[pj][:], ps4[pj][:],
                        ALU.add), r=("rx4_%d" % pj, "p4%d" % pj), w="r3s%d" % pj)
                    p.dma("scalar", lambda e, og=og, cI=cI, pj=pj: e.dma_start(
                        out=resd[og * 128:(og + 1) * 128, cI * 512:(cI + 1) * 512],
                        in_=r3s[:, pj * 512:(pj + 1) * 512]),
                        r="r3s%d" % pj, w="resd%d.%d" % (og, cI))

        p4_front(0)
        emit_mats("wk_w", xmix, "xmix", "k")
        for cI in range(4):
            emit_mix(rbuf, "rbufmix", "mixv", "onemv", extraw=("rbuf%d",))
            emit_mats("wv_w", rbuf, "rbufmix", "v")
            emit_mix(xmix, "xmix", "mixr", "onemr")
            p.do("vector", lambda e: e.tensor_copy(
                SM["prevcol"],
                xin4[:].rearrange("p (g t) -> p g t", g=G)[:, :, 511]),
                r="xin4", w="sm.prevcol")
            emit_mats("wr_w", xmix, "xmix", "r")
            p4_scans(cI)
            p4_exchange(cI)
            if cI < 3:
                p4_front(cI + 1)
            p4_gamma(cI)
            if cI < 3:
                for osec in range(4):
                    p4_ratio(cI, range(osec * 2, osec * 2 + 2))
                    emit_mats("wk_w", xmix, "xmix", "k", secs=[osec])
                p4_ratio(cI, range(8, 16))
            else:
                p4_ratio(cI)
            p4_wo(cI)
            if cI < 3:
                emit_mats("wk_w", xmix, "xmix", "k", secs=range(4, 8))

        p.barrier()
        if "resid3" in dbg:
            p.dma("sync", lambda e: e.dma_start(out=dbg["resid3"], in_=resd),
                  r="resd0.0", w="dbg_resid3")
            p.barrier()
        p4.close()

    if do_p5:
        emit_ffn_phase(final=True)
        p6 = ExitStack()
        tin = p6.enter_context(nc.sbuf_tensor("p6in", [128, G * 128], F32))
        tout = p6.enter_context(nc.sbuf_tensor("p6out", [128, 2048], F32))
        ps6 = [p6.enter_context(nc.psum_tensor("p6ps%d" % j, [128, 512], F32))
               for j in range(2)]
        for tt in range(16):
            p.dma("sync", lambda e, tt=tt: e.dma_start(
                out=tin[:].rearrange("p (g t) -> p g t", g=G),
                in_=resd[:, tt * 128:(tt + 1) * 128]
                .rearrange("(g p) t -> p g t", p=128)),
                r=tuple("resd%d.%d" % (g, tt // 4) for g in range(G)), w="p6in")
            for gb in range(4):
                for gl in range(4):
                    g = gb * 4 + gl
                    p.do("tensor", lambda e, g=g, gl=gl: e.transpose(
                        ps6[0][:, gl * 128:(gl + 1) * 128],
                        tin[:, g * 128:(g + 1) * 128], identf[:]),
                        r=("p6in", "identf"), w="p6a")
                for dd in range(2):
                    p.do("tensor", lambda e, dd=dd: e.transpose(
                        ps6[1][:, dd * 128:(dd + 1) * 128], identf[:], identf[:]),
                        r="identf", w="p6d")
                p.wait("scalar", "TE")
                for gl in range(4):
                    g = gb * 4 + gl
                    p.do("scalar", lambda e, g=g, gl=gl: e.activation(
                        tout[:, g * 128:(g + 1) * 128],
                        ps6[0][:, gl * 128:(gl + 1) * 128], AF.Copy),
                        r="p6a", w="p6out")
            p.dma("scalar", lambda e, tt=tt: e.dma_start(
                out=outy[tt * 128:(tt + 1) * 128, :], in_=tout[:]),
                r="p6out", w="y")
        p6.close()

    for j, (nm, src) in enumerate([("kdN", kdN), ("kdR", kdR), ("vd", vd), ("qd", qd)]):
        if nm in dbg:
            eng = "sync" if j % 2 == 0 else "gpsimd"
            p.dma(eng, lambda e, nm=nm, src=src: e.dma_start(out=dbg[nm], in_=src),
                  r=nm, w="dbg_" + nm)
    p.barrier()

    p.barrier()

    st.close()

    sem_stack = ExitStack()
    for s in p.cnt:
        p.sem[s] = sem_stack.enter_context(nc.semaphore(s))
    with nc.Block() as blk:
        for eng in ENGS:
            def make(eng):
                def run(e):
                    for g, fn in p.ops:
                        if g == eng:
                            fn(e)
                return run
            getattr(blk, eng)(make(eng))
    sem_stack.close()
    return nc



def kernel(**inputs):
    per_core = _host_prepare(inputs)
    nc = build_program()
    res = run_bass_kernel_spmd(nc, per_core, core_ids=list(range(NCORES)))
    out = np.zeros((B, SEQ, D), np.float32)
    for c in range(NCORES):
        b, h = c // 2, c % 2
        y = res.results[c]["y"]
        for i, g0 in enumerate(OWN[h]):
            out[b, g0 * 512:(g0 + 1) * 512, :] = y[i * 512:(i + 1) * 512, :]
    return out
